# version 05
# speedup vs baseline: 1.3715x; 1.0322x over previous
import contextlib
import numpy as np
import ml_dtypes
import concourse.bass as bass
import concourse.mybir as mybir
from concourse.bass_utils import run_bass_kernel_spmd

F32 = mybir.dt.float32
BF16 = mybir.dt.bfloat16
I32 = mybir.dt.int32
AF = mybir.ActivationFunctionType
OP = mybir.AluOpType
AX = mybir.AxisListType
BFNP = ml_dtypes.bfloat16

NCORE = 8
S = 8192
H = 2048
TOK = S // NCORE
TE = TOK + 3
NFC = H // 128
LAT = 512
NH = 16
I_FFN = 8192
NIC = I_FFN // 128
NHM = 32
DST = 64
HDM = 64
CHUNK = 128
NCH = TOK // CHUNK
GCH = S // CHUNK
CONVD = 6144
EPS = 1e-5
RSQ = float(1.0 / np.sqrt(128.0))
SP2 = [(0, 512), (512, 512)]
SP3 = [(0, 512), (512, 512), (1024, 3)]


def _pack_lhsT(W):
    M, K = W.shape
    nk = K // 128
    return np.ascontiguousarray(
        W.T.reshape(nk, 128, M).transpose(1, 0, 2).reshape(128, nk * M))

def _pack_lhsT_icmajor(W):
    I_, K = W.shape
    nic, nkc = I_ // 128, K // 128
    w4 = W.reshape(nic, 128, nkc, 128)
    return np.ascontiguousarray(w4.transpose(3, 0, 2, 1).reshape(128, nic * nkc * 128))

def _pack_rows(b):
    M = b.shape[0]
    return np.ascontiguousarray(b.reshape(M // 128, 128).T)

def _fm(x):
    T = x.shape[0]
    return np.ascontiguousarray(
        x.T.reshape(NFC, 128, T).transpose(1, 0, 2).reshape(128, NFC * T))

def _unfm(img, T):
    return np.ascontiguousarray(
        img.reshape(128, NFC, T).transpose(1, 0, 2).reshape(H, T).T)


class Ctr:
    def __init__(self, stack, nc, name):
        self.h = stack.enter_context(nc.semaphore(name))
        self.n = 0
    def inc(self, k=1):
        self.n += k
        return self.h
    def dma(self):
        return self.inc(16)


def _barrier(block, ctr, engines):
    val = ctr.n
    for name in engines:
        sec = getattr(block, name)
        @sec
        def _(e, _val=val, _h=ctr.h):
            e.wait_ge(_h, _val)



def emit_ln(nc, block, sems, src_f32, dst_bf, T, spans, tmps, psums, aff=None):
    s_mm, s_ev, s_ac, ph = sems["mm"], sems["ev"], sems["ac"], sems["ph"]
    x0b = src_f32
    sqb = tmps["sqb"]
    rows = tmps["rows"]
    m_row, v_row, r_row = rows[:, 0:T], rows[:, T:2 * T], rows[:, 2 * T:3 * T]
    onesb, onesf = tmps["onesb"], tmps["onesf"]
    p_s, p_q, p_m, p_r = psums

    @block.scalar
    def _(sc):
        sc.activation(sqb[:], x0b[:], AF.Square).then_inc(s_ac.inc(), 1)
    ac_sq = s_ac.n

    @block.tensor
    def _(t):
        t.wait_ge(s_ac.h, ac_sq)
        last = None
        for (t0, tw) in spans:
            for fc in range(NFC):
                last = t.matmul(p_s[:, t0:t0 + tw], onesb[:],
                                x0b[:, fc * T + t0:fc * T + t0 + tw],
                                start=(fc == 0), stop=(fc == NFC - 1))
            for fc in range(NFC):
                last = t.matmul(p_q[:, t0:t0 + tw], onesb[:],
                                sqb[:, fc * T + t0:fc * T + t0 + tw],
                                start=(fc == 0), stop=(fc == NFC - 1))
        last.then_inc(s_mm.inc(), 1)
    mm_stats = s_mm.n

    @block.vector
    def _(v):
        v.wait_ge(s_mm.h, mm_stats)
        v.tensor_scalar(m_row, p_s[0:1, 0:T], 1.0 / H, None, OP.mult)
        v.drain()
        v.tensor_tensor(v_row, m_row, m_row, OP.mult)
        v.drain()
        v.scalar_tensor_tensor(v_row, p_q[0:1, 0:T], 1.0 / H, v_row, OP.mult, OP.subtract)
        v.drain().then_inc(s_ev.inc(), 1)
    ev_var = s_ev.n

    @block.scalar
    def _(sc):
        sc.wait_ge(s_ev.h, ev_var)
        sc.activation(v_row, v_row, AF.Ln, bias=tmps["eps_col"][0:1, :])
        sc.drain()
        sc.activation(r_row, v_row, AF.Exp, scale=-0.5)
        sc.drain().then_inc(s_ac.inc(), 1)
    ac_rstd = s_ac.n

    for (t0, tw) in spans:
        @block.tensor
        def _(t, t0=t0, tw=tw):
            t.wait_ge(s_ac.h, ac_rstd)
            if t0 > 0:
                t.wait_ge(s_ev.h, s_ev.n)
            t.matmul(p_m[:, 0:tw], onesf[0:1, :], m_row[0:1, t0:t0 + tw],
                     start=True, stop=True)
            t.matmul(p_r[:, 0:tw], onesf[0:1, :], r_row[0:1, t0:t0 + tw],
                     start=True, stop=True).then_inc(s_mm.inc(), 1)
        mm_bc = s_mm.n

        @block.vector
        def _(v, t0=t0, tw=tw, mm_bc=mm_bc):
            v.wait_ge(s_mm.h, mm_bc)
            for fc in range(NFC):
                tf = tmps["tf"][fc % 2]
                sl = slice(fc * T + t0, fc * T + t0 + tw)
                v.tensor_tensor(tf[:, 0:tw], x0b[:, sl], p_m[:, 0:tw], OP.subtract)
                v.drain()
                if aff is None:
                    v.tensor_tensor(dst_bf[:, sl], tf[:, 0:tw], p_r[:, 0:tw], OP.mult)
                else:
                    v.scalar_tensor_tensor(dst_bf[:, sl], tf[:, 0:tw],
                                           aff[:, fc:fc + 1], p_r[:, 0:tw],
                                           OP.mult, OP.mult)
                v.drain()
            v.drain().then_inc(s_ev.inc(), 1)

    @block.vector
    def _(v):
        v.drain().then_inc(ph.inc(), 1)


def emit_rms(nc, block, sems, src_bf, dst_bf, T, spans, tmps, psums, wrow):
    s_mm, s_ev, s_ac, ph = sems["mm"], sems["ev"], sems["ac"], sems["ph"]
    sqb = tmps["sqb"]
    rows = tmps["rows"]
    v_row, r_row = rows[:, T:2 * T], rows[:, 2 * T:3 * T]
    onesb, onesf = tmps["onesb"], tmps["onesf"]
    p_s, p_q, p_m, p_r = psums

    @block.scalar
    def _(sc):
        sc.activation(sqb[:], src_bf[:], AF.Square).then_inc(s_ac.inc(), 1)
    ac_sq = s_ac.n

    @block.tensor
    def _(t):
        t.wait_ge(s_ac.h, ac_sq)
        last = None
        for (t0, tw) in spans:
            for fc in range(NFC):
                last = t.matmul(p_q[:, t0:t0 + tw], onesb[:],
                                sqb[:, fc * T + t0:fc * T + t0 + tw],
                                start=(fc == 0), stop=(fc == NFC - 1))
        last.then_inc(s_mm.inc(), 1)
    mm_stats = s_mm.n

    @block.vector
    def _(v):
        v.wait_ge(s_mm.h, mm_stats)
        v.tensor_scalar(v_row, p_q[0:1, 0:T], 1.0 / H, None, OP.mult)
        v.drain().then_inc(s_ev.inc(), 1)
    ev_var = s_ev.n

    @block.scalar
    def _(sc):
        sc.wait_ge(s_ev.h, ev_var)
        sc.activation(v_row, v_row, AF.Ln, bias=tmps["eps_col"][0:1, :])
        sc.drain()
        sc.activation(r_row, v_row, AF.Exp, scale=-0.5)
        sc.drain().then_inc(s_ac.inc(), 1)
    ac_rstd = s_ac.n

    for (t0, tw) in spans:
        @block.tensor
        def _(t, t0=t0, tw=tw):
            t.wait_ge(s_ac.h, ac_rstd)
            if t0 > 0:
                t.wait_ge(s_ev.h, s_ev.n)
            t.matmul(p_r[:, 0:tw], onesf[0:1, :], r_row[0:1, t0:t0 + tw],
                     start=True, stop=True).then_inc(s_mm.inc(), 1)
        mm_bc = s_mm.n

        @block.vector
        def _(v, t0=t0, tw=tw, mm_bc=mm_bc):
            v.wait_ge(s_mm.h, mm_bc)
            for fc in range(NFC):
                sl = slice(fc * T + t0, fc * T + t0 + tw)
                v.scalar_tensor_tensor(dst_bf[:, sl], src_bf[:, sl],
                                       wrow[:, fc:fc + 1], p_r[:, 0:tw],
                                       OP.mult, OP.mult)
                v.drain()
            v.drain().then_inc(s_ev.inc(), 1)

    @block.vector
    def _(v):
        v.drain().then_inc(ph.inc(), 1)

def emit_ffn(nc, block, sems, src_bf, T, spans, dram, quant, scales, dacc, dacc_init, bufs,
             dacc_T=None, dacc_off=0, hoist=False):
    s_wl, s_mm, s_ev, s_ac, s_fr, ph = (sems["wl"], sems["mm"], sems["ev"],
                                        sems["ac"], sems["fr"], sems["ph"])
    wl2 = bufs["wl2"]
    wld = bufs["wld"]
    wgu = bufs["wgu"]
    gsc, usc, dsc = scales if scales else (None, None, None)
    wd = bufs["wd"]
    hbuf, gs, us = bufs["hbuf"], bufs["gs"], bufs["us"]
    p_g, p_u, p_d = bufs["p_g"], bufs["p_u"], bufs["p_d"]
    NSP = len(spans)

    wl0 = s_wl.n
    ev_sub = {}
    ev_h = {}
    ac_evac = {}
    fr_cnt = {}

    NWGU = len(wgu)
    wl_cnt = {}

    def emit_ld(ic):
        bi = ic % NWGU

        @block.gpsimd
        def _(g, ic=ic, bi=bi):
            if ic >= NWGU:
                g.wait_ge(s_mm.h, fr_cnt[ic - NWGU])
            g.dma_start(out=wgu[bi][:], in_=dram["guw"][:, ic * 4096:(ic + 1) * 4096]
                        ).then_inc(wl2[bi].dma(), 16)
        wl_cnt[ic] = wl2[bi].n

        if quant:
            @block.vector
            def _(v, bi=bi, wl_ic=wl_cnt[ic]):
                v.wait_ge(wl2[bi].h, wl_ic)
                v.tensor_scalar(wgu[bi][:], wgu[bi][:], -128.0, None, OP.add
                                ).then_inc(s_ev.inc(), 1)
            ev_sub[ic] = s_ev.n

    if hoist:
        emit_ld(0)
    for grp in range(2):
        for icg in range(32):
            ic = grp * 32 + icg
            bi = ic % NWGU
            if hoist:
                if ic + 1 < 64:
                    emit_ld(ic + 1)
            else:
                emit_ld(ic)
            wl_ic = wl_cnt[ic]

            @block.tensor
            def _(t, ic=ic, bi=bi, wl_ic=wl_ic):
                if quant:
                    t.wait_ge(s_ev.h, ev_sub[ic])
                else:
                    t.wait_ge(wl2[bi].h, wl_ic)
                for sp, (t0, tw) in enumerate(spans):
                    if (ic - 1, sp) in ac_evac:
                        t.wait_ge(s_ac.h, ac_evac[(ic - 1, sp)])
                    for kc in range(NFC):
                        t.matmul(p_g[sp][:, 0:tw], wgu[bi][:, kc * 128:(kc + 1) * 128],
                                 src_bf[:, kc * T + t0:kc * T + t0 + tw],
                                 start=(kc == 0), stop=(kc == NFC - 1))
                    for kc in range(NFC):
                        last = t.matmul(p_u[sp][:, 0:tw], wgu[bi][:, 2048 + kc * 128:2048 + (kc + 1) * 128],
                                        src_bf[:, kc * T + t0:kc * T + t0 + tw],
                                        start=(kc == 0), stop=(kc == NFC - 1))
                    last.then_inc(s_mm.inc(), 1)
            fr_cnt[ic] = s_mm.n
            mm_ic = s_mm.n

            for sp, (t0, tw) in enumerate(spans):
                mm_sp = mm_ic - (NSP - 1 - sp)

                @block.scalar
                def _(sc, ic=ic, sp=sp, t0=t0, tw=tw, mm_sp=mm_sp):
                    sc.wait_ge(s_mm.h, mm_sp)
                    if (ic - 1, sp) in ev_h:
                        sc.wait_ge(s_ev.h, ev_h[(ic - 1, sp)])
                    if gsc is not None:
                        sc.activation(gs[sp][:, 0:tw], p_g[sp][:, 0:tw], AF.Silu,
                                      scale=gsc[:, ic:ic + 1])
                        sc.activation(us[sp][:, 0:tw], p_u[sp][:, 0:tw], AF.Copy,
                                      scale=usc[:, ic:ic + 1]).then_inc(s_ac.inc(), 1)
                    else:
                        sc.activation(gs[sp][:, 0:tw], p_g[sp][:, 0:tw], AF.Silu)
                        sc.activation(us[sp][:, 0:tw], p_u[sp][:, 0:tw], AF.Copy
                                      ).then_inc(s_ac.inc(), 1)
                ac_evac[(ic, sp)] = s_ac.n

                @block.vector
                def _(v, ic=ic, sp=sp, t0=t0, tw=tw):
                    v.wait_ge(s_ac.h, ac_evac[(ic, sp)])
                    if ic == 32 and sp == 0 and "hbuf_free" in bufs:
                        v.wait_ge(s_mm.h, bufs["hbuf_free"])
                    icg_ = ic % 32
                    v.tensor_tensor(hbuf[:, icg_ * T + t0:icg_ * T + t0 + tw],
                                    gs[sp][:, 0:tw], us[sp][:, 0:tw], OP.mult
                                    ).then_inc(s_ev.inc(), 1)
                ev_h[(ic, sp)] = s_ev.n

        ev_group = s_ev.n
        ev_dfree = {}
        if grp == 0:
            fr_d = {}
            bufs["fr_d"] = fr_d
        else:
            fr_d = bufs["fr_d"]
        for oc in range(16):
            bo = oc % 2
            gidx = grp * 16 + oc

            @block.gpsimd
            def _(g, grp=grp, oc=oc, bo=bo, gidx=gidx):
                if gidx >= 2:
                    g.wait_ge(s_mm.h, fr_d[gidx - 2])
                g.dma_start(out=wd[bo][:],
                            in_=dram["dw"][:, (grp * 16 + oc) * 4096:(grp * 16 + oc + 1) * 4096]
                            ).then_inc(wld[bo].dma(), 16)
            wl_oc = wld[bo].n

            if quant:
                @block.vector
                def _(v, bo=bo, wl_oc=wl_oc):
                    v.wait_ge(wld[bo].h, wl_oc)
                    v.tensor_scalar(wd[bo][:], wd[bo][:], -128.0, None, OP.add
                                    ).then_inc(s_ev.inc(), 1)
                ev_dw = s_ev.n
            else:
                ev_dw = None

            @block.tensor
            def _(t, oc=oc, bo=bo, ev_dw=ev_dw, wl_oc=wl_oc):
                if ev_dw is not None:
                    t.wait_ge(s_ev.h, ev_dw)
                else:
                    t.wait_ge(wld[bo].h, wl_oc)
                t.wait_ge(s_ev.h, ev_group)
                for sp, (t0, tw) in enumerate(spans):
                    if (oc - 1, sp) in ev_dfree:
                        t.wait_ge(s_ev.h, ev_dfree[(oc - 1, sp)])
                    for icg in range(32):
                        last = t.matmul(p_d[sp][:, 0:tw], wd[bo][:, icg * 128:(icg + 1) * 128],
                                        hbuf[:, icg * T + t0:icg * T + t0 + tw],
                                        start=(icg == 0), stop=(icg == 31))
                    last.then_inc(s_mm.inc(), 1)
            fr_d[gidx] = s_mm.n
            mm_oc = s_mm.n

            for sp, (t0, tw) in enumerate(spans):
                mm_sp = mm_oc - (NSP - 1 - sp)

                @block.vector
                def _(v, grp=grp, oc=oc, sp=sp, t0=t0, tw=tw, mm_sp=mm_sp):
                    v.wait_ge(s_mm.h, mm_sp)
                    dT = dacc_T if dacc_T is not None else T
                    dsl = dacc[:, oc * dT + t0 - dacc_off:oc * dT + t0 - dacc_off + tw]
                    first = dacc_init and grp == 0
                    if dsc is not None:
                        if first:
                            v.tensor_scalar(dsl, p_d[sp][:, 0:tw], dsc[:, oc:oc + 1],
                                            None, OP.mult).then_inc(s_ev.inc(), 1)
                        else:
                            v.scalar_tensor_tensor(dsl, p_d[sp][:, 0:tw],
                                                   dsc[:, oc:oc + 1], dsl,
                                                   OP.mult, OP.add).then_inc(s_ev.inc(), 1)
                    else:
                        if first:
                            v.tensor_copy(dsl, p_d[sp][:, 0:tw]).then_inc(s_ev.inc(), 1)
                        else:
                            v.tensor_tensor(dsl, p_d[sp][:, 0:tw], dsl, OP.add
                                            ).then_inc(s_ev.inc(), 1)
                ev_dfree[(oc, sp)] = s_ev.n
        if grp == 0:
            bufs["hbuf_free"] = s_mm.n

    @block.vector
    def _(v):
        v.drain().then_inc(ph.inc(), 1)


def _pack_guw(gate_w, up_w):
    g = _pack_lhsT_icmajor(gate_w)
    u = _pack_lhsT_icmajor(up_w)
    out = np.empty((128, NIC * 2 * H), g.dtype)
    for ic in range(NIC):
        out[:, ic * 4096:ic * 4096 + 2048] = g[:, ic * 2048:(ic + 1) * 2048]
        out[:, ic * 4096 + 2048:(ic + 1) * 4096] = u[:, ic * 2048:(ic + 1) * 2048]
    return np.ascontiguousarray(out)


_PROG_CACHE = {}

def build_p1():
    if "p1" in _PROG_CACHE:
        return _PROG_CACHE["p1"]
    nc = bass.Bass("TRN2", target_bir_lowering=False)
    x0 = nc.dram_tensor("x0", [128, NFC * TOK], F32, kind="ExternalInput").ap()
    wqd = nc.dram_tensor("wqd", [128, 16 * LAT], F32, kind="ExternalInput").ap()
    wkvd = nc.dram_tensor("wkvd", [128, 16 * LAT], F32, kind="ExternalInput").ap()
    wqu = nc.dram_tensor("wqu", [128, 4 * H], F32, kind="ExternalInput").ap()
    wkn = nc.dram_tensor("wkn", [128, 4 * 1024], F32, kind="ExternalInput").ap()
    wkr = nc.dram_tensor("wkr", [128, 4 * 1024], F32, kind="ExternalInput").ap()
    wvu = nc.dram_tensor("wvu", [128, 4 * H], F32, kind="ExternalInput").ap()
    bqd = nc.dram_tensor("bqd", [128, 4], F32, kind="ExternalInput").ap()
    bkvd = nc.dram_tensor("bkvd", [128, 4], F32, kind="ExternalInput").ap()
    bqu = nc.dram_tensor("bqu", [128, 16], F32, kind="ExternalInput").ap()
    bkn = nc.dram_tensor("bkn", [128, 8], F32, kind="ExternalInput").ap()
    bkr = nc.dram_tensor("bkr", [128, 8], F32, kind="ExternalInput").ap()
    bvu_bc = nc.dram_tensor("bvu_bc", [128, H], F32, kind="ExternalInput").ap()
    cosb = nc.dram_tensor("cosb", [128, 4 * TOK], BF16, kind="ExternalInput").ap()
    sinb = nc.dram_tensor("sinb", [128, 4 * TOK], BF16, kind="ExternalInput").ap()
    guw = nc.dram_tensor("guw", [128, NIC * 2 * H], I32, kind="ExternalInput").ap()
    dw = nc.dram_tensor("dw", [128, 32 * 4096], I32, kind="ExternalInput").ap()
    gsc = nc.dram_tensor("gsc", [128, NIC], F32, kind="ExternalInput").ap()
    usc = nc.dram_tensor("usc", [128, NIC], F32, kind="ExternalInput").ap()
    dsc = nc.dram_tensor("dsc", [128, 16], F32, kind="ExternalInput").ap()

    k_out = nc.dram_tensor("k_out", [128, NH * TOK], BF16, kind="ExternalOutput").ap()
    v_out = nc.dram_tensor("v_out", [128, 8 * H], BF16, kind="ExternalOutput").ap()
    q_out = nc.dram_tensor("q_out", [128, NH * TOK], BF16, kind="ExternalOutput").ap()
    acc0_out = nc.dram_tensor("acc0_out", [128, NFC * TOK], F32, kind="ExternalOutput").ap()
    norm_out = nc.dram_tensor("norm_out", [1, 32], F32, kind="ExternalOutput").ap()

    with contextlib.ExitStack() as stack:
        E = stack.enter_context
        xn = E(nc.sbuf_tensor("xn", [128, NFC * TOK], BF16))
        onesb = E(nc.sbuf_tensor("onesb", [128, 1], BF16))
        onesf = E(nc.sbuf_tensor("onesf", [1, 128], F32))
        norm_acc = E(nc.sbuf_tensor("norm_acc", [1, 32], F32))
        eps_col = E(nc.sbuf_tensor("eps_col", [128, 1], F32))
        rmax = E(nc.sbuf_tensor("rmax", [1, 2], F32))
        s_ld = Ctr(stack, nc, "s_ld")
        s_wl = Ctr(stack, nc, "s_wl")
        s_mm = Ctr(stack, nc, "s_mm")
        s_ev = Ctr(stack, nc, "s_ev")
        s_ac = Ctr(stack, nc, "s_ac")
        s_st = Ctr(stack, nc, "s_st")
        s_fr = Ctr(stack, nc, "s_fr")
        ph = Ctr(stack, nc, "ph")
        st4 = [Ctr(stack, nc, f"st4_{i}") for i in range(4)]
        wl2 = [Ctr(stack, nc, f"wl2_{i}") for i in range(3)]
        wld = [Ctr(stack, nc, f"wld_{i}") for i in range(2)]
        ld5 = [Ctr(stack, nc, f"ld5_{i}") for i in range(2)]
        st5 = [Ctr(stack, nc, f"st5_{i}") for i in range(2)]
        sems = dict(ld=s_ld, wl=s_wl, mm=s_mm, ev=s_ev, ac=s_ac, st=s_st,
                    fr=s_fr, ph=ph)
        block = E(nc.Block())

        with contextlib.ExitStack() as st1:
            E1 = st1.enter_context
            qlat = E1(nc.sbuf_tensor("qlat", [128, 4 * TOK], BF16))
            kvlat = E1(nc.sbuf_tensor("kvlat", [128, 4 * TOK], BF16))
            rows = E1(nc.sbuf_tensor("rows", [1, 3 * TOK], F32))
            wqds = E1(nc.sbuf_tensor("wqds", [128, 16 * LAT], BF16))
            wkvds = E1(nc.sbuf_tensor("wkvds", [128, 16 * LAT], BF16))
            wqus = E1(nc.sbuf_tensor("wqus", [128, 4 * H], BF16))
            biasq = E1(nc.sbuf_tensor("biasq", [128, 40], F32))
            coss = E1(nc.sbuf_tensor("coss", [128, 4 * TOK], BF16))
            sins = E1(nc.sbuf_tensor("sins", [128, 4 * TOK], BF16))
            tf0 = E1(nc.sbuf_tensor("tf0", [128, 512], F32))
            tf1 = E1(nc.sbuf_tensor("tf1", [128, 512], F32))
            krp = E1(nc.sbuf_tensor("krp", [128, 8 * 512], BF16))
            evb = E1(nc.sbuf_tensor("evb", [128, 4 * 512], BF16))
            sqk = E1(nc.sbuf_tensor("sqk", [128, 512], BF16))
            p_s = E1(nc.psum_tensor("p_s", [1, TOK], F32))
            p_q = E1(nc.psum_tensor("p_q", [1, TOK], F32))
            p_m = E1(nc.psum_tensor("p_m", [128, 512], F32))
            p_r = E1(nc.psum_tensor("p_r", [128, 512], F32))
            p_n = E1(nc.psum_tensor("p_n", [1, 512], F32))

            bq_d = biasq[:, 0:4]
            bkv_d = biasq[:, 4:8]
            bqu_d = biasq[:, 8:24]
            bkn_d = biasq[:, 24:32]
            bkr_d = biasq[:, 32:40]

            ln_stack = contextlib.ExitStack()
            x0b = ln_stack.enter_context(nc.sbuf_tensor("x0b", [128, NFC * TOK], BF16))
            sqb = ln_stack.enter_context(nc.sbuf_tensor("sqb", [128, NFC * TOK], BF16))

            @block.sync
            def _(sy):
                for dst, src in ((biasq[:, 0:4], bqd), (biasq[:, 4:8], bkvd),
                                 (biasq[:, 8:24], bqu), (biasq[:, 24:32], bkn),
                                 (biasq[:, 32:40], bkr)):
                    sy.dma_start(out=dst, in_=src[:, :]).then_inc(s_ld.dma(), 16)
                sy.dma_start(out=coss[:], in_=cosb[:, :]).then_inc(s_ld.dma(), 16)
                sy.dma_start(out=sins[:], in_=sinb[:, :]).then_inc(s_ld.dma(), 16)
            LD0 = s_ld.n

            @block.gpsimd
            def _(g):
                g.dma_start(out=x0b[:], in_=x0[:, :]).then_inc(s_wl.dma(), 16)
                g.dma_start(out=wqds[:], in_=wqd[:, :]).then_inc(s_wl.dma(), 16)
                g.dma_start(out=wkvds[:], in_=wkvd[:, :]).then_inc(s_wl.dma(), 16)
                g.dma_start(out=wqus[:], in_=wqu[:, :]).then_inc(s_wl.dma(), 16)
                g.memset(onesb[:], 1.0)
                g.memset(onesf[:], 1.0)
                g.memset(norm_acc[:], 0.0)
                g.memset(eps_col[:], EPS).then_inc(s_fr.inc(), 1)
            WL0 = s_wl.n

            _barrier(block, s_ld, ["vector", "scalar", "tensor"])
            _barrier(block, s_wl, ["vector", "scalar", "tensor"])
            _barrier(block, s_fr, ["vector", "scalar", "tensor"])

            tmps = dict(sqb=sqb, rows=rows, tf=[tf0, tf1],
                        onesb=onesb, onesf=onesf, eps_col=eps_col)
            emit_ln(nc, block, sems, x0b, xn, TOK, SP2, tmps,
                    (p_s, p_q, p_m, p_r))
            _barrier(block, ph, ["tensor", "vector", "scalar"])
            ln_stack.close()
            stW2 = contextlib.ExitStack()
            EW = stW2.enter_context
            wkns = EW(nc.sbuf_tensor("wkns", [128, 4 * 1024], BF16))
            wkrs = EW(nc.sbuf_tensor("wkrs", [128, 4 * 1024], BF16))
            wvus = EW(nc.sbuf_tensor("wvus", [128, 4 * H], BF16))
            bvus = EW(nc.sbuf_tensor("bvus", [128, H], BF16))

            ph_ln = ph.n

            @block.gpsimd
            def _(g):
                g.wait_ge(ph.h, ph_ln)
                g.dma_start(out=wkns[:], in_=wkn[:, :]).then_inc(s_wl.dma(), 16)
                g.dma_start(out=wkrs[:], in_=wkr[:, :]).then_inc(s_wl.dma(), 16)
                g.dma_start(out=wvus[:], in_=wvu[:, :]).then_inc(s_wl.dma(), 16)
                g.dma_start(out=bvus[:], in_=bvu_bc[:, :]).then_inc(s_wl.dma(), 16)
            _barrier(block, s_wl, ["tensor", "vector", "scalar"])

            lat_units = []
            for dstbuf, wimg, brow in ((qlat, wqds, bq_d), (kvlat, wkvds, bkv_d)):
                for mc in range(4):
                    for sp, (t0, tw) in enumerate(SP2):
                        lat_units.append((dstbuf, wimg, brow, mc, t0, tw))
            ac_ev = {}
            for i, (dstbuf, wimg, brow, mc, t0, tw) in enumerate(lat_units):
                pp = (p_m, p_r)[i % 2]

                @block.tensor
                def _(t, i=i, wimg=wimg, mc=mc, t0=t0, tw=tw, pp=pp):
                    if i >= 2:
                        t.wait_ge(s_ac.h, ac_ev[i - 2])
                    for kc in range(NFC):
                        last = t.matmul(pp[:, 0:tw], wimg[:, kc * LAT + mc * 128:kc * LAT + (mc + 1) * 128],
                                        xn[:, kc * TOK + t0:kc * TOK + t0 + tw],
                                        start=(kc == 0), stop=(kc == NFC - 1))
                    last.then_inc(s_mm.inc(), 1)
                mm_i = s_mm.n

                @block.scalar
                def _(sc, i=i, dstbuf=dstbuf, brow=brow, mc=mc, t0=t0, tw=tw,
                      pp=pp, mm_i=mm_i):
                    sc.wait_ge(s_mm.h, mm_i)
                    sc.activation(dstbuf[:, mc * TOK + t0:mc * TOK + t0 + tw],
                                  pp[:, 0:tw], AF.Identity,
                                  bias=brow[:, mc:mc + 1]).then_inc(s_ac.inc(), 1)
                ac_ev[i] = s_ac.n
            _barrier(block, s_ac, ["tensor", "vector", "scalar"])

            slot_cond = {0: [], 1: [], 2: [], 3: []}
            psum_cond = {0: [], 1: []}
            state = {"norm_ev": 0, "u": 0}

            def norm_update(row, hc, src_tile, tw, gate=None):
                @block.scalar
                def _(sc):
                    if gate is not None:
                        sc.wait_ge(gate[0], gate[1])
                    if state.get("norm_mm"):
                        sc.wait_ge(s_mm.h, state["norm_mm"])
                    sc.activation(sqk[:, 0:tw], src_tile, AF.Square
                                  ).then_inc(s_ac.inc(), 1)
                acn = s_ac.n

                @block.tensor
                def _(t, acn=acn, tw=tw):
                    t.wait_ge(s_ac.h, acn)
                    if state["norm_ev"]:
                        t.wait_ge(s_ev.h, state["norm_ev"])
                    t.matmul(p_n[:, 0:tw], onesb[:], sqk[:, 0:tw],
                             start=True, stop=True).then_inc(s_mm.inc(), 1)
                mmn = s_mm.n
                state["norm_mm"] = mmn

                @block.vector
                def _(v, row=row, hc=hc, mmn=mmn, tw=tw):
                    v.wait_ge(s_mm.h, mmn)
                    v.tensor_reduce(rmax[0:1, 0:1], p_n[0:1, 0:tw], axis=AX.X, op=OP.max)
                    v.drain()
                    col = row * 16 + hc
                    v.tensor_tensor(norm_acc[0:1, col:col + 1],
                                    norm_acc[0:1, col:col + 1],
                                    rmax[0:1, 0:1], OP.max)
                    v.drain().then_inc(s_ev.inc(), 1)
                state["norm_ev"] = s_ev.n
                return s_ac.n

            def proj_unit(wimg, wstride, nkc, src, hc, t0, tw, bias_col,
                          dst_slice, scale_post=None, norm_row=None):
                u = state["u"]; state["u"] = u + 1
                par = u % 2
                pp = (p_m, p_r)[par]
                slot = u % 4
                eb = evb[:, slot * 512:slot * 512 + 512]
                pc = list(psum_cond[par])

                @block.tensor
                def _(t, u=u, wimg=wimg, hc=hc, t0=t0, tw=tw, pp=pp, nkc=nkc,
                      src=src, wstride=wstride, pc=pc):
                    for (hh, vv) in pc:
                        t.wait_ge(hh, vv)
                    for kc in range(nkc):
                        last = t.matmul(pp[:, 0:tw],
                                        wimg[:, kc * wstride + hc * 128:kc * wstride + (hc + 1) * 128],
                                        src[:, kc * TOK + t0:kc * TOK + t0 + tw],
                                        start=(kc == 0), stop=(kc == nkc - 1))
                    last.then_inc(s_mm.inc(), 1)
                mmu = s_mm.n

                scond = list(slot_cond[slot])

                @block.scalar
                def _(sc, tw=tw, pp=pp, eb=eb, mmu=mmu, bias_col=bias_col, scond=scond):
                    sc.wait_ge(s_mm.h, mmu)
                    for (hh, vv) in scond:
                        sc.wait_ge(hh, vv)
                    sc.activation(eb[:, 0:tw], pp[:, 0:tw], AF.Identity, bias=bias_col
                                  ).then_inc(s_ac.inc(), 1)
                psum_cond[par] = [(s_ac.h, s_ac.n)]
                gate = (s_ac.h, s_ac.n)

                if scale_post is not None:
                    @block.vector
                    def _(v, tw=tw, eb=eb, g=gate[1]):
                        v.wait_ge(s_ac.h, g)
                        v.tensor_scalar(eb[:, 0:tw], eb[:, 0:tw], scale_post, None, OP.mult
                                        ).then_inc(s_ev.inc(), 1)
                    gate = (s_ev.h, s_ev.n)

                @block.sync
                def _(sy, eb=eb, tw=tw, dst_slice=dst_slice, g=gate, slot=slot):
                    sy.wait_ge(g[0], g[1])
                    sy.dma_start(out=dst_slice, in_=eb[:, 0:tw]).then_inc(st4[slot].dma(), 16)
                cond = [(st4[slot].h, st4[slot].n)]
                if scale_post is not None:
                    cond.append((s_ev.h, gate[1]))

                if norm_row is not None:
                    ac_after = norm_update(norm_row, hc, eb[:, 0:tw], tw, gate=gate)
                    cond.append((s_ac.h, ac_after))
                slot_cond[slot] = cond

            for hc in range(NH):
                for (t0, tw) in SP2:
                    proj_unit(wqus, H, 4, qlat, hc, t0, tw, bqu_d[:, hc:hc + 1],
                              q_out[:, hc * TOK + t0:hc * TOK + t0 + tw],
                              scale_post=RSQ, norm_row=0)
            for hc in range(8):
                for (t0, tw) in SP2:
                    proj_unit(wkns, 1024, 4, kvlat, hc, t0, tw, bkn_d[:, hc:hc + 1],
                              k_out[:, hc * TOK + t0:hc * TOK + t0 + tw],
                              norm_row=1)

            for spi, (t0, tw) in enumerate(SP2):
                krp_ready = []
                for hc in range(8):
                    u = state["u"]; state["u"] = u + 1
                    par = u % 2
                    pp = (p_m, p_r)[par]
                    pc = list(psum_cond[par])

                    @block.tensor
                    def _(t, hc=hc, t0=t0, tw=tw, pp=pp, pc=pc):
                        for (hh, vv) in pc:
                            t.wait_ge(hh, vv)
                        for kc in range(4):
                            last = t.matmul(pp[:, 0:tw],
                                            wkrs[:, kc * 1024 + hc * 128:kc * 1024 + (hc + 1) * 128],
                                            kvlat[:, kc * TOK + t0:kc * TOK + t0 + tw],
                                            start=(kc == 0), stop=(kc == 3))
                        last.then_inc(s_mm.inc(), 1)
                    mmu = s_mm.n

                    @block.scalar
                    def _(sc, hc=hc, tw=tw, pp=pp, mmu=mmu, spi=spi):
                        sc.wait_ge(s_mm.h, mmu)
                        if spi > 0 and hc == 0:
                            sc.wait_ge(s_ev.h, state["rope_ev"])
                        sc.activation(krp[:, hc * 512:hc * 512 + tw], pp[:, 0:tw],
                                      AF.Identity, bias=bkr_d[:, hc:hc + 1]
                                      ).then_inc(s_ac.inc(), 1)
                    psum_cond[par] = [(s_ac.h, s_ac.n)]
                    krp_ready.append(s_ac.n)

                for o in range(4):
                    u = state["u"]; state["u"] = u + 2
                    slot_lo, slot_hi = u % 4, (u + 1) % 4
                    eb_lo = evb[:, slot_lo * 512:slot_lo * 512 + 512]
                    eb_hi = evb[:, slot_hi * 512:slot_hi * 512 + 512]
                    scond = list(slot_cond[slot_lo]) + list(slot_cond[slot_hi])

                    @block.vector
                    def _(v, o=o, t0=t0, tw=tw, eb_lo=eb_lo, eb_hi=eb_hi,
                          scond=scond, kr=krp_ready[-1]):
                        v.wait_ge(s_ac.h, kr)
                        for (hh, vv) in scond:
                            v.wait_ge(hh, vv)
                        cs = coss[:, o * TOK + t0:o * TOK + t0 + tw]
                        sn = sins[:, o * TOK + t0:o * TOK + t0 + tw]
                        x1s = krp[:, o * 512:o * 512 + tw]
                        x2s = krp[:, (o + 4) * 512:(o + 4) * 512 + tw]
                        v.tensor_tensor(tf0[:, 0:tw], x1s, cs, OP.mult)
                        v.tensor_tensor(tf1[:, 0:tw], x2s, sn, OP.mult)
                        v.drain()
                        v.tensor_tensor(eb_lo[:, 0:tw], tf0[:, 0:tw], tf1[:, 0:tw], OP.subtract)
                        v.drain()
                        v.tensor_tensor(tf0[:, 0:tw], x1s, sn, OP.mult)
                        v.tensor_tensor(tf1[:, 0:tw], x2s, cs, OP.mult)
                        v.drain()
                        v.tensor_tensor(eb_hi[:, 0:tw], tf0[:, 0:tw], tf1[:, 0:tw], OP.add)
                        v.drain().then_inc(s_ev.inc(), 1)
                    ev_rope = s_ev.n

                    @block.sync
                    def _(sy, o=o, t0=t0, tw=tw, eb_lo=eb_lo, eb_hi=eb_hi, g=ev_rope,
                          slot_lo=slot_lo, slot_hi=slot_hi):
                        sy.wait_ge(s_ev.h, g)
                        sy.dma_start(out=k_out[:, (8 + o) * TOK + t0:(8 + o) * TOK + t0 + tw],
                                     in_=eb_lo[:, 0:tw]).then_inc(st4[slot_lo].dma(), 16)
                        sy.dma_start(out=k_out[:, (12 + o) * TOK + t0:(12 + o) * TOK + t0 + tw],
                                     in_=eb_hi[:, 0:tw]).then_inc(st4[slot_hi].dma(), 16)
                    st_lo, st_hi = st4[slot_lo].n, st4[slot_hi].n

                    ac_lo = norm_update(1, 8 + o, eb_lo[:, 0:tw], tw, gate=(s_ev.h, ev_rope))
                    ac_hi = norm_update(1, 12 + o, eb_hi[:, 0:tw], tw, gate=(s_ev.h, ev_rope))
                    slot_cond[slot_lo] = [(st4[slot_lo].h, st_lo), (s_ac.h, ac_lo)]
                    slot_cond[slot_hi] = [(st4[slot_hi].h, st_hi), (s_ac.h, ac_hi)]
                state["rope_ev"] = s_ev.n

            for tt in range(8):
                for ds in range(4):
                    u = state["u"]; state["u"] = u + 1
                    par = u % 2
                    pp = (p_m, p_r)[par]
                    slot = u % 4
                    eb = evb[:, slot * 512:slot * 512 + 512]
                    pc = list(psum_cond[par])

                    @block.tensor
                    def _(t, tt=tt, ds=ds, pp=pp, pc=pc):
                        for (hh, vv) in pc:
                            t.wait_ge(hh, vv)
                        for kc in range(4):
                            last = t.matmul(pp[:, 0:512],
                                            kvlat[:, kc * TOK + tt * 128:kc * TOK + (tt + 1) * 128],
                                            wvus[:, kc * H + ds * 512:kc * H + (ds + 1) * 512],
                                            start=(kc == 0), stop=(kc == 3))
                        last.then_inc(s_mm.inc(), 1)
                    mmu = s_mm.n

                    scond = list(slot_cond[slot])

                    @block.vector
                    def _(v, ds=ds, pp=pp, eb=eb, mmu=mmu, scond=scond):
                        v.wait_ge(s_mm.h, mmu)
                        for (hh, vv) in scond:
                            v.wait_ge(hh, vv)
                        v.tensor_tensor(eb[:, 0:512], pp[:, 0:512],
                                        bvus[:, ds * 512:(ds + 1) * 512], OP.add
                                        ).then_inc(s_ev.inc(), 1)
                    psum_cond[par] = [(s_ev.h, s_ev.n)]
                    evv = s_ev.n

                    @block.sync
                    def _(sy, tt=tt, ds=ds, eb=eb, evv=evv, slot=slot):
                        sy.wait_ge(s_ev.h, evv)
                        sy.dma_start(out=v_out[:, tt * H + ds * 512:tt * H + (ds + 1) * 512],
                                     in_=eb[:, 0:512]).then_inc(st4[slot].dma(), 16)
                    slot_cond[slot] = [(st4[slot].h, st4[slot].n), (s_ev.h, evv)]

            @block.sync
            def _(sy):
                sy.wait_ge(s_ev.h, state["norm_ev"])
                sy.dma_start(out=norm_out[:, :], in_=norm_acc[:]).then_inc(s_st.dma(), 16)

            _barrier(block, s_st, ["tensor", "vector", "scalar", "gpsimd"])
            for c4 in st4:
                _barrier(block, c4, ["tensor", "vector", "scalar", "gpsimd", "sync"])
            stW2.close()

        with contextlib.ExitStack() as st2:
            E2 = st2.enter_context
            dacc = E2(nc.sbuf_tensor("dacc", [128, NFC * TOK], F32))
            wgu0 = E2(nc.sbuf_tensor("wgu0", [128, 2 * H], BF16))
            wgu1 = E2(nc.sbuf_tensor("wgu1", [128, 2 * H], BF16))
            wd0 = E2(nc.sbuf_tensor("wd0", [128, 4096], BF16))
            wd1 = E2(nc.sbuf_tensor("wd1", [128, 4096], BF16))
            hbuf = E2(nc.sbuf_tensor("hbuf", [128, 32 * TOK], BF16))
            gs0 = E2(nc.sbuf_tensor("gs0", [128, 512], BF16))
            gs1 = E2(nc.sbuf_tensor("gs1", [128, 512], BF16))
            us0 = E2(nc.sbuf_tensor("us0", [128, 512], BF16))
            us1 = E2(nc.sbuf_tensor("us1", [128, 512], BF16))
            gscs = E2(nc.sbuf_tensor("gscs", [128, NIC], F32))
            uscs = E2(nc.sbuf_tensor("uscs", [128, NIC], F32))
            dscs = E2(nc.sbuf_tensor("dscs", [128, 16], F32))
            p_g0 = E2(nc.psum_tensor("p_g0", [128, 512], F32))
            p_g1 = E2(nc.psum_tensor("p_g1", [128, 512], F32))
            p_u0 = E2(nc.psum_tensor("p_u0", [128, 512], F32))
            p_u1 = E2(nc.psum_tensor("p_u1", [128, 512], F32))
            p_d0 = E2(nc.psum_tensor("p_d0", [128, 512], F32))
            p_d1 = E2(nc.psum_tensor("p_d1", [128, 512], F32))

            @block.sync
            def _(sy):
                sy.dma_start(out=gscs[:], in_=gsc[:, :]).then_inc(s_ld.dma(), 16)
                sy.dma_start(out=uscs[:], in_=usc[:, :]).then_inc(s_ld.dma(), 16)
                sy.dma_start(out=dscs[:], in_=dsc[:, :]).then_inc(s_ld.dma(), 16)
            _barrier(block, s_ld, ["scalar", "vector"])

            bufs = dict(wg=None, wu=None, wgu=[wgu0, wgu1], wd=[wd0, wd1], hbuf=hbuf,
                        gs=[gs0, gs1], us=[us0, us1],
                        p_g=[p_g0, p_g1], p_u=[p_u0, p_u1], p_d=[p_d0, p_d1],
                        wl2=wl2, wld=wld)
            emit_ffn(nc, block, sems, xn, TOK, SP2,
                     dict(guw=guw, dw=dw), True,
                     (gscs, uscs, dscs), dacc, True, bufs)
            _barrier(block, ph, ["tensor", "vector", "scalar", "sync", "gpsimd"])

            with contextlib.ExitStack() as st3:
                E3 = st3.enter_context
                x0r = E3(nc.sbuf_tensor("x0r", [128, 2 * 512], F32))
                accw = E3(nc.sbuf_tensor("accw", [128, 2 * 512], F32))
                ev5_done = {}
                st5_done = {}
                for fc in range(NFC):
                    for sp, (t0, tw) in enumerate(SP2):
                        idx = fc * 2 + sp
                        j = idx % 2
                        @block.sync
                        def _(sy, fc=fc, t0=t0, tw=tw, j=j, idx=idx):
                            if idx - 2 in ev5_done:
                                sy.wait_ge(s_ev.h, ev5_done[idx - 2])
                            sy.dma_start(out=x0r[:, j * 512:j * 512 + tw],
                                         in_=x0[:, fc * TOK + t0:fc * TOK + t0 + tw]
                                         ).then_inc(ld5[j].dma(), 16)
                        ld5v = ld5[j].n

                        @block.vector
                        def _(v, fc=fc, t0=t0, tw=tw, j=j, ld5v=ld5v, idx=idx):
                            v.wait_ge(ld5[j].h, ld5v)
                            if idx - 2 in st5_done:
                                v.wait_ge(st5[j].h, st5_done[idx - 2])
                            v.tensor_tensor(accw[:, j * 512:j * 512 + tw],
                                            x0r[:, j * 512:j * 512 + tw],
                                            dacc[:, fc * TOK + t0:fc * TOK + t0 + tw],
                                            OP.add).then_inc(s_ev.inc(), 1)
                        ev5_done[idx] = s_ev.n
                        ev5 = s_ev.n

                        @block.sync
                        def _(sy, fc=fc, t0=t0, tw=tw, j=j, ev5=ev5):
                            sy.wait_ge(s_ev.h, ev5)
                            sy.dma_start(out=acc0_out[:, fc * TOK + t0:fc * TOK + t0 + tw],
                                         in_=accw[:, j * 512:j * 512 + tw]
                                         ).then_inc(st5[j].dma(), 16)
                        st5_done[idx] = st5[j].n

                @block.sync
                def _(sy):
                    sy.wait_ge(s_st.h, s_st.n)
                    for c4 in st4 + st5:
                        sy.wait_ge(c4.h, c4.n)

    _PROG_CACHE["p1"] = nc
    return nc


def _fmn(x, nfc):
    T = x.shape[0]
    return np.ascontiguousarray(
        x.T.reshape(nfc, 128, T).transpose(1, 0, 2).reshape(128, nfc * T))

def _rope_tables():
    ROT = 1024
    inv_freq = 1.0 / 10000.0 ** (np.arange(0, ROT, 2, dtype=np.float32) / ROT)
    t = np.arange(S, dtype=np.float32)
    fr = np.outer(t, inv_freq)
    return np.cos(fr).astype(np.float32), np.sin(fr).astype(np.float32)

def p1_in_maps(inp):
    x = np.asarray(inp["L0_attn_hidden_states"])[0]
    cos_t, sin_t = _rope_tables()
    shared = {
        "wqd": _pack_lhsT(np.asarray(inp["L0_attn_Wq_down"])),
        "wkvd": _pack_lhsT(np.asarray(inp["L0_attn_Wkv_down"])),
        "wqu": _pack_lhsT(np.asarray(inp["L0_attn_Wq_up"])),
        "wkn": _pack_lhsT(np.asarray(inp["L0_attn_Wk_nope"])),
        "wkr": _pack_lhsT(np.asarray(inp["L0_attn_Wk_rope"])),
        "wvu": _pack_lhsT(np.asarray(inp["L0_attn_Wv_up"])),
        "bqd": _pack_rows(np.asarray(inp["L0_attn_bq_down"])),
        "bkvd": _pack_rows(np.asarray(inp["L0_attn_bkv_down"])),
        "bqu": _pack_rows(np.asarray(inp["L0_attn_bq_up"])),
        "bkn": _pack_rows(np.asarray(inp["L0_attn_bk_nope"])),
        "bkr": _pack_rows(np.asarray(inp["L0_attn_bk_rope"])),
        "bvu_bc": np.ascontiguousarray(
            np.broadcast_to(np.asarray(inp["L0_attn_bv_up"])[None, :], (128, H))),
        "guw": _pack_guw(np.asarray(inp["L0_ffn_gate_w"]), np.asarray(inp["L0_ffn_up_w"])),
        "dw": np.ascontiguousarray(
            np.asarray(inp["L0_ffn_down_w"]).reshape(16, 128, 2, 32, 128)
            .transpose(4, 2, 0, 3, 1).reshape(128, 131072)),
        "gsc": _pack_rows(np.asarray(inp["L0_ffn_gate_scale"])),
        "usc": _pack_rows(np.asarray(inp["L0_ffn_up_scale"])),
        "dsc": _pack_rows(np.asarray(inp["L0_ffn_down_scale"])),
    }
    maps = []
    for c in range(NCORE):
        sl = slice(c * TOK, (c + 1) * TOK)
        m = dict(shared)
        m["x0"] = _fm(x[sl])
        m["cosb"] = _fmn(cos_t[sl], 4).astype(BFNP)
        m["sinb"] = _fmn(sin_t[sl], 4).astype(BFNP)
        maps.append(m)
    return maps

def run_p1(inp):
    nc = build_p1()
    maps = p1_in_maps(inp)
    res = run_bass_kernel_spmd(nc, maps, core_ids=list(range(NCORE)))
    return res.results


def build_p2():
    if "p2" in _PROG_CACHE:
        return _PROG_CACHE["p2"]
    nc = bass.Bass("TRN2", target_bir_lowering=False)
    qimg = nc.dram_tensor("qimg", [128, NH * TOK], BF16, kind="ExternalInput").ap()
    kimg = nc.dram_tensor("kimg", [128, NH * S], BF16, kind="ExternalInput").ap()
    vimg = nc.dram_tensor("vimg", [128, NH * S], BF16, kind="ExternalInput").ap()
    acc0e = nc.dram_tensor("acc0e", [128, NFC * TOK], F32, kind="ExternalInput").ap()
    normsin = nc.dram_tensor("normsin", [1, 256], F32, kind="ExternalInput").ap()
    wo = nc.dram_tensor("wo", [128, 16 * H], F32, kind="ExternalInput").ap()
    bo = nc.dram_tensor("bo", [128, 16], F32, kind="ExternalInput").ap()
    x1o = nc.dram_tensor("x1o", [128, NFC * TOK], F32, kind="ExternalOutput").ap()
    xn1o = nc.dram_tensor("xn1o", [128, NFC * TOK], BF16, kind="ExternalOutput").ap()

    with contextlib.ExitStack() as stack:
        E = stack.enter_context
        onesb = E(nc.sbuf_tensor("onesb", [128, 1], BF16))
        onesf = E(nc.sbuf_tensor("onesf", [1, 128], F32))
        eps_col = E(nc.sbuf_tensor("eps_col", [128, 1], F32))
        nbbc = E(nc.sbuf_tensor("nbbc", [128, 16], F32))
        o_fm = E(nc.sbuf_tensor("o_fm", [128, NH * TOK], BF16))

        s_ld = Ctr(stack, nc, "s_ld")
        s_mm = Ctr(stack, nc, "s_mm")
        s_ev = Ctr(stack, nc, "s_ev")
        s_ac = Ctr(stack, nc, "s_ac")
        s_st = Ctr(stack, nc, "s_st")
        ph = Ctr(stack, nc, "ph")
        kl2 = [Ctr(stack, nc, f"kl2_{i}") for i in range(2)]
        vl2 = [Ctr(stack, nc, f"vl2_{i}") for i in range(2)]
        ldA = [Ctr(stack, nc, f"ldA_{i}") for i in range(2)]
        wl2 = [Ctr(stack, nc, f"wl2_{i}") for i in range(3)]
        st2 = [Ctr(stack, nc, f"st2_{i}") for i in range(2)]
        sems = dict(ld=s_ld, mm=s_mm, ev=s_ev, ac=s_ac, st=s_st, ph=ph)
        block = E(nc.Block())

        with contextlib.ExitStack() as stA:
            EA = stA.enter_context
            qsb = EA(nc.sbuf_tensor("qsb", [128, NH * TOK], BF16))
            ksb = [EA(nc.sbuf_tensor(f"ksb{i}", [128, S], BF16)) for i in range(2)]
            vsb = [EA(nc.sbuf_tensor(f"vsb{i}", [128, S], BF16)) for i in range(2)]
            pt = [EA(nc.sbuf_tensor(f"pt{i}", [128, 1024], BF16)) for i in range(2)]
            nrm = EA(nc.sbuf_tensor("nrm", [1, 256], F32))
            brow = EA(nc.sbuf_tensor("brow", [1, 64], F32))
            recsb = EA(nc.sbuf_tensor("recsb", [1, 512], F32))
            otmp = EA(nc.sbuf_tensor("otmp", [128, 512], F32))
            psc = [EA(nc.psum_tensor(f"psc{i}", [128, 1024], F32)) for i in range(2)]
            pacc = [EA(nc.psum_tensor(f"pacc{i}", [128, 512], F32)) for i in range(2)]
            pden = [EA(nc.psum_tensor(f"pden{i}", [1, 512], F32)) for i in range(2)]

            @block.sync
            def _(sy):
                sy.dma_start(out=qsb[:], in_=qimg[:, :]).then_inc(s_ld.dma(), 16)
                sy.dma_start(out=nrm[:], in_=normsin[:, :]).then_inc(s_ld.dma(), 16)

            @block.gpsimd
            def _(g):
                g.memset(onesb[:], 1.0)
                g.memset(onesf[:], 1.0)
                g.memset(eps_col[:], EPS).then_inc(s_ev.inc(), 1)
            _barrier(block, s_ld, ["vector", "scalar", "tensor", "gpsimd"])
            _barrier(block, s_ev, ["vector", "scalar", "tensor"])

            @block.vector
            def _(v):
                v.tensor_reduce(brow[0:1, 0:32].rearrange("p (a b) -> p a b", a=32, b=1),
                                nrm[0:1, :].rearrange("p (a b) -> p a b", a=32, b=8),
                                axis=AX.X, op=OP.max)
                v.drain()
                v.tensor_tensor(brow[0:1, 32:48], brow[0:1, 0:16], brow[0:1, 16:32], OP.mult)
                v.drain().then_inc(s_ev.inc(), 1)
            evb0 = s_ev.n

            @block.scalar
            def _(sc):
                sc.wait_ge(s_ev.h, evb0)
                sc.activation(brow[0:1, 32:48], brow[0:1, 32:48], AF.Ln)
                sc.drain()
                sc.activation(brow[0:1, 48:64], brow[0:1, 32:48], AF.Exp, scale=0.5)
                sc.drain().then_inc(s_ac.inc(), 1)
            acb = s_ac.n

            @block.vector
            def _(v):
                v.wait_ge(s_ac.h, acb)
                v.tensor_scalar(brow[0:1, 48:64], brow[0:1, 48:64], -1.0, None, OP.mult)
                v.drain().then_inc(s_ev.inc(), 1)
            evb1 = s_ev.n

            @block.tensor
            def _(t):
                t.wait_ge(s_ev.h, evb1)
                t.matmul(psc[0][:, 0:16], onesf[0:1, :], brow[0:1, 48:64],
                         start=True, stop=True).then_inc(s_mm.inc(), 1)
            mmb = s_mm.n

            @block.vector
            def _(v):
                v.wait_ge(s_mm.h, mmb)
                v.tensor_copy(nbbc[:], psc[0][:, 0:16])
                v.drain().then_inc(s_ev.inc(), 1)
            evb2 = s_ev.n
            _barrier(block, s_ev, ["tensor", "scalar", "gpsimd"])

            kv_free = {}
            acc_free = {}
            for h in range(NH):
                hs = h % 2

                @block.gpsimd
                def _(g, h=h, hs=hs):
                    if h >= 2:
                        g.wait_ge(s_mm.h, kv_free[h - 2])
                    g.dma_start(out=ksb[hs][:], in_=kimg[:, h * S:(h + 1) * S]
                                ).then_inc(kl2[hs].dma(), 16)
                    g.dma_start(out=vsb[hs][:], in_=vimg[:, h * S:(h + 1) * S]
                                ).then_inc(vl2[hs].dma(), 16)
                klv, vlv = kl2[hs].n, vl2[hs].n

                for sp in range(2):
                    t0 = sp * 512
                    ps = pacc[sp % 2]
                    pd = pden[sp % 2]
                    exp_done = {}
                    att_done = {}

                    def emit_scores(kg, h=h, hs=hs, sp=sp, t0=t0, klv=klv):
                        pp = psc[kg % 2]

                        @block.tensor
                        def _(t, kg=kg, pp=pp):
                            if kg == 0:
                                t.wait_ge(kl2[hs].h, klv)
                                t.wait_ge(vl2[hs].h, vlv)
                                if (sp % 2) in acc_free:
                                    hh, vv = acc_free[sp % 2]
                                    t.wait_ge(hh, vv)
                                if "psc0" in acc_free:
                                    hh, vv = acc_free["psc0"]
                                    t.wait_ge(hh, vv)
                            if kg - 2 in exp_done:
                                t.wait_ge(s_ac.h, exp_done[kg - 2])
                            for j in range(2):
                                kk = kg * 2 + j
                                last = t.matmul(pp[:, j * 512:j * 512 + 512],
                                                ksb[hs][:, kk * 128:(kk + 1) * 128],
                                                qsb[:, h * TOK + t0:h * TOK + t0 + 512],
                                                start=True, stop=True)
                            last.then_inc(s_mm.inc(), 1)
                        mm_sc = s_mm.n

                        @block.scalar
                        def _(sc, kg=kg, pp=pp, mm_sc=mm_sc):
                            sc.wait_ge(s_mm.h, mm_sc)
                            if kg - 2 in att_done:
                                sc.wait_ge(s_mm.h, att_done[kg - 2])
                            sc.activation(pt[kg % 2][:], pp[:], AF.Exp,
                                          bias=nbbc[:, h:h + 1]).then_inc(s_ac.inc(), 1)
                        exp_done[kg] = s_ac.n

                    def emit_attnv(kg, h=h, hs=hs, ps=ps, pd=pd):
                        @block.tensor
                        def _(t, kg=kg, ac_exp=exp_done[kg]):
                            t.wait_ge(s_ac.h, ac_exp)
                            for j in range(2):
                                kk = kg * 2 + j
                                t.matmul(ps[:, 0:512],
                                         vsb[hs][:, kk * 128:(kk + 1) * 128],
                                         pt[kg % 2][:, j * 512:(j + 1) * 512],
                                         start=(kk == 0), stop=(kk == 63))
                                last = t.matmul(pd[:, 0:512],
                                                onesb[:],
                                                pt[kg % 2][:, j * 512:(j + 1) * 512],
                                                start=(kk == 0), stop=(kk == 63))
                            last.then_inc(s_mm.inc(), 1)
                        att_done[kg] = s_mm.n

                    for kg in range(33):
                        if kg < 32:
                            emit_scores(kg)
                        if kg >= 1:
                            emit_attnv(kg - 1)
                    mm_att = s_mm.n
                    if h == NH - 1 and sp == 1:
                        pass
                    kv_free[h] = mm_att

                    @block.vector
                    def _(v, pd=pd, mm_att=mm_att):
                        v.wait_ge(s_mm.h, mm_att)
                        v.reciprocal(recsb[0:1, :], pd[0:1, 0:512])
                        v.drain().then_inc(s_ev.inc(), 1)
                    ev_rec = s_ev.n

                    @block.tensor
                    def _(t, ev_rec=ev_rec):
                        t.wait_ge(s_ev.h, ev_rec)
                        t.matmul(psc[0][:, 0:512], onesf[0:1, :], recsb[0:1, :],
                                 start=True, stop=True).then_inc(s_mm.inc(), 1)
                    mm_bc = s_mm.n

                    @block.vector
                    def _(v, h=h, sp=sp, t0=t0, ps=ps, mm_bc=mm_bc):
                        v.wait_ge(s_mm.h, mm_bc)
                        v.tensor_copy(otmp[:], ps[:, 0:512])
                        v.drain()
                        v.tensor_tensor(o_fm[:, h * TOK + t0:h * TOK + t0 + 512],
                                        otmp[:], psc[0][:, 0:512], OP.mult
                                        ).then_inc(s_ev.inc(), 1)
                    acc_free[sp % 2] = (s_ev.h, s_ev.n)
                    acc_free["psc0"] = (s_ev.h, s_ev.n)

            _barrier(block, s_ev, ["tensor", "scalar", "gpsimd", "sync"])
            _barrier(block, s_mm, ["vector", "scalar", "gpsimd", "sync"])

        with contextlib.ExitStack() as stB:
            EB = stB.enter_context
            x1sb = EB(nc.sbuf_tensor("x1sb", [128, NFC * TOK], F32))
            x1b = EB(nc.sbuf_tensor("x1b", [128, NFC * TOK], BF16))
            wosb = [EB(nc.sbuf_tensor(f"wosb{i}", [128, 2048], BF16)) for i in range(2)]
            bosb = EB(nc.sbuf_tensor("bosb", [128, 16], F32))
            a0sb = [EB(nc.sbuf_tensor(f"a0sb{i}", [128, 512], F32)) for i in range(2)]
            pwo = [EB(nc.psum_tensor(f"pwo{i}", [128, 512], F32)) for i in range(2)]

            @block.sync
            def _(sy):
                sy.dma_start(out=bosb[:], in_=bo[:, :]).then_inc(s_ld.dma(), 16)
            _barrier(block, s_ld, ["vector"])

            wo_free = {}
            ev_x1 = {}
            for oc in range(16):
                bi = oc % 2

                @block.gpsimd
                def _(g, oc=oc, bi=bi):
                    if oc >= 2:
                        g.wait_ge(s_mm.h, wo_free[oc - 2])
                    g.dma_start(out=wosb[bi][:], in_=wo[:, oc * 2048:(oc + 1) * 2048]
                                ).then_inc(wl2[bi].dma(), 16)
                wlv = wl2[bi].n

                for sp in range(2):
                    t0 = sp * 512
                    u = oc * 2 + sp
                    j = u % 2

                    @block.sync
                    def _(sy, oc=oc, t0=t0, j=j, u=u):
                        if u - 2 in ev_x1:
                            sy.wait_ge(s_ev.h, ev_x1[u - 2])
                        sy.dma_start(out=a0sb[j][:],
                                     in_=acc0e[:, oc * TOK + t0:oc * TOK + t0 + 512]
                                     ).then_inc(ldA[j].dma(), 16)
                    ldv = ldA[j].n

                    @block.tensor
                    def _(t, oc=oc, bi=bi, sp=sp, t0=t0, j=j, u=u, wlv=wlv):
                        if sp == 0:
                            t.wait_ge(wl2[bi].h, wlv)
                        if u - 2 in ev_x1:
                            t.wait_ge(s_ev.h, ev_x1[u - 2])
                        for kc in range(NFC):
                            last = t.matmul(pwo[j][:, 0:512],
                                            wosb[bi][:, kc * 128:(kc + 1) * 128],
                                            o_fm[:, kc * TOK + t0:kc * TOK + t0 + 512],
                                            start=(kc == 0), stop=(kc == NFC - 1))
                        last.then_inc(s_mm.inc(), 1)
                    mmw = s_mm.n
                    if sp == 1:
                        wo_free[oc] = mmw

                    @block.vector
                    def _(v, oc=oc, t0=t0, j=j, mmw=mmw, ldv=ldv):
                        v.wait_ge(s_mm.h, mmw)
                        v.wait_ge(ldA[j].h, ldv)
                        sl = slice(oc * TOK + t0, oc * TOK + t0 + 512)
                        v.scalar_tensor_tensor(x1sb[:, sl], pwo[j][:, 0:512],
                                               bosb[:, oc:oc + 1], a0sb[j][:],
                                               OP.add, OP.add)
                        v.drain()
                        v.tensor_copy(x1b[:, sl], x1sb[:, sl]).then_inc(s_ev.inc(), 1)
                    ev_x1[u] = s_ev.n
                    evx = s_ev.n

                    @block.sync
                    def _(sy, oc=oc, t0=t0, j=j, evx=evx):
                        sy.wait_ge(s_ev.h, evx)
                        sy.dma_start(out=x1o[:, oc * TOK + t0:oc * TOK + t0 + 512],
                                     in_=x1sb[:, oc * TOK + t0:oc * TOK + t0 + 512]
                                     ).then_inc(st2[j].dma(), 16)

            _barrier(block, s_ev, ["tensor", "scalar", "gpsimd", "sync"])

            with contextlib.ExitStack() as stC:
                EC = stC.enter_context
                sqb2 = o_fm
                xn1sb = EC(nc.sbuf_tensor("xn1sb", [128, NFC * TOK], BF16))
                rows2 = EC(nc.sbuf_tensor("rows2", [1, 3 * TOK], F32))
                tfa = EC(nc.sbuf_tensor("tfa", [128, 512], F32))
                tfb = EC(nc.sbuf_tensor("tfb", [128, 512], F32))
                p_s2 = EC(nc.psum_tensor("p_s2", [1, TOK], F32))
                p_q2 = EC(nc.psum_tensor("p_q2", [1, TOK], F32))
                p_m2 = EC(nc.psum_tensor("p_m2", [128, 512], F32))
                p_r2 = EC(nc.psum_tensor("p_r2", [128, 512], F32))
                tmps = dict(sqb=sqb2, rows=rows2, tf=[tfa, tfb],
                            onesb=onesb, onesf=onesf, eps_col=eps_col)
                emit_ln(nc, block, sems, x1b, xn1sb, TOK, SP2, tmps,
                        (p_s2, p_q2, p_m2, p_r2))
                _barrier(block, ph, ["sync"])

                @block.sync
                def _(sy):
                    sy.dma_start(out=xn1o[:, :], in_=xn1sb[:]).then_inc(s_st.dma(), 16)
                    sy.wait_ge(s_st.h, s_st.n)
                    for cc in st2 + [ldA[0], ldA[1]]:
                        sy.wait_ge(cc.h, cc.n)

    _PROG_CACHE["p2"] = nc
    return nc


def p2_in_maps(inp, p1res):
    K_img = np.zeros((128, NH * S), dtype=BFNP)
    V_img = np.zeros((128, NH * S), dtype=BFNP)
    for c in range(NCORE):
        ko = p1res[c]["k_out"]
        vo = p1res[c]["v_out"]
        for h in range(NH):
            K_img[:, h * S + c * TOK:h * S + (c + 1) * TOK] = ko[:, h * TOK:(h + 1) * TOK]
            for tt in range(8):
                kk = c * 8 + tt
                V_img[:, (h * 64 + kk) * 128:(h * 64 + kk + 1) * 128] = \
                    vo[:, tt * H + h * 128:tt * H + (h + 1) * 128]
    norms = np.zeros((1, 256), np.float32)
    for c in range(NCORE):
        no = p1res[c]["norm_out"][0]
        for k in range(2):
            for h in range(16):
                norms[0, (k * 16 + h) * 8 + c] = no[k * 16 + h]
    shared = {
        "kimg": K_img, "vimg": V_img, "normsin": norms,
        "wo": _pack_lhsT_icmajor(np.asarray(inp["L0_attn_Wo"])),
        "bo": _pack_rows(np.asarray(inp["L0_attn_bo"])),
    }
    maps = []
    for c in range(NCORE):
        m = dict(shared)
        m["qimg"] = p1res[c]["q_out"]
        m["acc0e"] = p1res[c]["acc0_out"]
        maps.append(m)
    return maps

def run_p2(inp, p1res):
    nc = build_p2()
    maps = p2_in_maps(inp, p1res)
    res = run_bass_kernel_spmd(nc, maps, core_ids=list(range(NCORE)))
    return res.results


def build_p3():
    if "p3" in _PROG_CACHE:
        return _PROG_CACHE["p3"]
    nc = bass.Bass("TRN2", target_bir_lowering=False)
    TE_ = TE
    xn1e = nc.dram_tensor("xn1e", [128, NFC * TE_], BF16, kind="ExternalInput").ap()
    n1w_d = nc.dram_tensor("n1w", [128, 16], F32, kind="ExternalInput").ap()
    ipm = nc.dram_tensor("ipm", [128, 64 * H], F32, kind="ExternalInput").ap()
    ipdt = nc.dram_tensor("ipdt", [128, 16 * 32], F32, kind="ExternalInput").ap()
    cvw = nc.dram_tensor("cvw", [128, 48 * 4], F32, kind="ExternalInput").ap()
    cvb = nc.dram_tensor("cvb", [128, 48], F32, kind="ExternalInput").ap()
    lti = nc.dram_tensor("lti", [128, 128], F32, kind="ExternalInput").ap()
    mkt = nc.dram_tensor("mkt", [128, 128], F32, kind="ExternalInput").ap()
    idf = nc.dram_tensor("idf", [128, 128], F32, kind="ExternalInput").ap()
    idb = nc.dram_tensor("idb", [128, 128], BF16, kind="ExternalInput").ap()
    dbc = nc.dram_tensor("dbc", [128, 32], F32, kind="ExternalInput").ap()
    selm = nc.dram_tensor("selm", [32, 4096], F32, kind="ExternalInput").ap()
    guw = nc.dram_tensor("guw", [128, NIC * 2 * H], I32, kind="ExternalInput").ap()
    dw = nc.dram_tensor("dw", [128, 32 * 4096], I32, kind="ExternalInput").ap()
    gsc = nc.dram_tensor("gsc", [128, NIC], F32, kind="ExternalInput").ap()
    usc = nc.dram_tensor("usc", [128, NIC], F32, kind="ExternalInput").ap()
    dsc = nc.dram_tensor("dsc", [128, 16], F32, kind="ExternalInput").ap()

    zo = nc.dram_tensor("zo", [128, NFC * TOK], BF16, kind="ExternalOutput").ap()
    co = nc.dram_tensor("co", [128, NFC * TOK], BF16, kind="ExternalOutput").ap()
    y1o = nc.dram_tensor("y1o", [128, 8 * H], BF16, kind="ExternalOutput").ap()
    ecumo = nc.dram_tensor("ecumo", [128, 8 * 32], F32, kind="ExternalOutput").ap()
    so = nc.dram_tensor("so", [64, 16384], F32, kind="ExternalOutput").ap()
    alo = nc.dram_tensor("alo", [32, 8], F32, kind="ExternalOutput").ap()
    qacc = nc.dram_tensor("qacc", [128, NFC * TOK], F32, kind="ExternalOutput").ap()

    SPE = [(0, 512), (512, 512), (1024, 3)]
    SPO = [(3, 512), (515, 512)]

    with contextlib.ExitStack() as stack:
        E = stack.enter_context
        xn1 = E(nc.sbuf_tensor("xn1", [128, NFC * TE_], BF16))
        onesb = E(nc.sbuf_tensor("onesb", [128, 1], BF16))
        onesf = E(nc.sbuf_tensor("onesf", [1, 128], F32))
        eps_col = E(nc.sbuf_tensor("eps_col", [128, 1], F32))
        stH = contextlib.ExitStack()
        consts = stH.enter_context(nc.sbuf_tensor("consts", [128, 128 * 3 + 32 + 48 * 4 + 48 + 16], F32))
        ltis = consts[:, 0:128]
        mkts = consts[:, 128:256]
        idfs = consts[:, 256:384]
        dbcs = consts[:, 384:416]
        cvws = consts[:, 416:608]
        cvbs = consts[:, 608:656]
        n1ws = consts[:, 656:672]

        s_ld = Ctr(stack, nc, "s_ld")
        s_wl = Ctr(stack, nc, "s_wl")
        s_mm = Ctr(stack, nc, "s_mm")
        s_ev = Ctr(stack, nc, "s_ev")
        s_ac = Ctr(stack, nc, "s_ac")
        s_st = Ctr(stack, nc, "s_st")
        s_fr = Ctr(stack, nc, "s_fr")
        ph = Ctr(stack, nc, "ph")
        ipl = [Ctr(stack, nc, f"ipl{i}") for i in range(3)]
        wl2 = [Ctr(stack, nc, f"wl2_{i}") for i in range(3)]
        wld = [Ctr(stack, nc, f"wld_{i}") for i in range(2)]
        stz = [Ctr(stack, nc, f"stz{i}") for i in range(2)]
        stS = Ctr(stack, nc, "stS")
        stCo = Ctr(stack, nc, "stCo")
        sems = dict(ld=s_ld, wl=s_wl, mm=s_mm, ev=s_ev, ac=s_ac, st=s_st,
                    fr=s_fr, ph=ph)
        block = E(nc.Block())

        @block.sync
        def _(sy):
            sy.dma_start(out=xn1[:], in_=xn1e[:, :]).then_inc(s_ld.dma(), 16)
            sy.dma_start(out=ltis, in_=lti[:, :]).then_inc(s_ld.dma(), 16)
            sy.dma_start(out=mkts, in_=mkt[:, :]).then_inc(s_ld.dma(), 16)
            sy.dma_start(out=idfs, in_=idf[:, :]).then_inc(s_ld.dma(), 16)
            sy.dma_start(out=dbcs, in_=dbc[:, :]).then_inc(s_ld.dma(), 16)
            sy.dma_start(out=cvws, in_=cvw[:, :]).then_inc(s_ld.dma(), 16)
            sy.dma_start(out=cvbs, in_=cvb[:, :]).then_inc(s_ld.dma(), 16)
            sy.dma_start(out=n1ws, in_=n1w_d[:, :]).then_inc(s_ld.dma(), 16)

        @block.gpsimd
        def _(g):
            g.memset(onesb[:], 1.0)
            g.memset(onesf[:], 1.0)
            g.memset(eps_col[:], EPS).then_inc(s_fr.inc(), 1)
        _barrier(block, s_ld, ["vector", "scalar", "tensor", "gpsimd"])
        _barrier(block, s_fr, ["vector", "scalar", "tensor"])

        h1 = stH.enter_context(nc.sbuf_tensor("h1", [128, NFC * TE_], BF16))
        with contextlib.ExitStack() as stB:
            EB = stB.enter_context
            sqb = EB(nc.sbuf_tensor("sqb", [128, NFC * TE_], BF16))
            rows = EB(nc.sbuf_tensor("rows", [1, 3 * TE_], F32))
            tf0 = EB(nc.sbuf_tensor("tf0", [128, 512], F32))
            tf1 = EB(nc.sbuf_tensor("tf1", [128, 512], F32))
            p_s = EB(nc.psum_tensor("p_s", [1, TE_], F32))
            p_q = EB(nc.psum_tensor("p_q", [1, TE_], F32))
            p_m = EB(nc.psum_tensor("p_m", [128, 512], F32))
            p_r = EB(nc.psum_tensor("p_r", [128, 512], F32))

            tmps = dict(sqb=sqb, rows=rows, tf=[tf0, tf1],
                        onesb=onesb, onesf=onesf, eps_col=eps_col)
            emit_rms(nc, block, sems, xn1, h1, TE_, SPE, tmps,
                     (p_s, p_q, p_m, p_r), n1ws)
            _barrier(block, ph, ["tensor", "vector", "scalar", "gpsimd", "sync"])

        with contextlib.ExitStack() as stC:
            EC = stC.enter_context
            selms = EC(nc.sbuf_tensor("selms", [32, 4096], F32))
            identb = EC(nc.sbuf_tensor("identb", [128, 128], BF16))
            ipsb = [EC(nc.sbuf_tensor(f"ipsb{i}", [128, 2048], BF16)) for i in range(3)]
            ipdts = EC(nc.sbuf_tensor("ipdts", [128, 512], BF16))
            dtsb = EC(nc.sbuf_tensor("dtsb", [32, TE_], F32))
            spt = EC(nc.sbuf_tensor("spt", [32, TE_], F32))
            alog = EC(nc.sbuf_tensor("alog", [32, TE_], F32))
            cumtm = EC(nc.sbuf_tensor("cumtm", [128, 8 * 32], F32))
            ecums = EC(nc.sbuf_tensor("ecums", [128, 8 * 32], F32))
            cumhm = EC(nc.sbuf_tensor("cumhm", [32, 8 * 128], F32))
            d2etm = EC(nc.sbuf_tensor("d2etm", [128, 8 * 32], F32))
            alos = EC(nc.sbuf_tensor("alos", [32, 8], F32))
            atm = EC(nc.sbuf_tensor("atm", [128, 32], F32))
            dhm = EC(nc.sbuf_tensor("dhm", [32, 128], F32))
            zfull = [EC(nc.sbuf_tensor(f"zfull{i}", [128, TE_], BF16)) for i in range(2)]
            xbcs = [EC(nc.sbuf_tensor(f"xbcs{i}", [128, TE_], BF16)) for i in range(2)]
            cacc = EC(nc.sbuf_tensor("cacc", [128, TOK], F32))
            cvob = [EC(nc.sbuf_tensor(f"cvob{i}", [128, TOK], BF16)) for i in range(2)]
            xtmc = EC(nc.sbuf_tensor("xtmc", [128, 2048], BF16))
            btmc = EC(nc.sbuf_tensor("btmc", [128, 2048], BF16))
            bfmc = EC(nc.sbuf_tensor("bfmc", [128, TOK], BF16))
            cfmc = EC(nc.sbuf_tensor("cfmc", [128, TOK], BF16))
            y1sb = EC(nc.sbuf_tensor("y1sb", [128, 8 * H], BF16))
            sst = EC(nc.sbuf_tensor("sst", [64, 128], F32))
            segT = [EC(nc.sbuf_tensor(f"segT{i}", [128, 128], BF16)) for i in range(2)]
            dTs = [EC(nc.sbuf_tensor(f"dTs{i}", [128, 128], F32)) for i in range(2)]
            scT = [EC(nc.sbuf_tensor(f"scT{i}", [128, 128], BF16)) for i in range(2)]
            bsb = [EC(nc.sbuf_tensor(f"bsb{i}", [128, 64], BF16)) for i in range(2)]
            p_a = EC(nc.psum_tensor("p_a", [128, 512], F32))
            p_d = [EC(nc.psum_tensor(f"p_dx{i}", [128, 128], F32)) for i in range(2)]
            p_ef = EC(nc.psum_tensor("p_ef", [128, 128], F32))
            p_b = EC(nc.psum_tensor("p_b", [128, 512], F32))
            p_t = EC(nc.psum_tensor("p_t", [128, 1024], BF16))
            stPC = contextlib.ExitStack()
            p_c = stPC.enter_context(nc.psum_tensor("p_c", [128, 128], F32))

            cvobC = EC(nc.sbuf_tensor("cvobC", [128, TOK], BF16))

            @block.gpsimd
            def _(g):
                g.dma_start(out=ipdts[:], in_=ipdt[:, :]).then_inc(s_wl.dma(), 16)
                g.dma_start(out=selms[:], in_=selm[:, :]).then_inc(s_wl.dma(), 16)
                g.dma_start(out=identb[:], in_=idb[:, :]).then_inc(s_wl.dma(), 16)
            _barrier(block, s_wl, ["tensor"])

            for sp, (t0, tw) in enumerate(SPE):
                @block.tensor
                def _(t, sp=sp, t0=t0, tw=tw):
                    if sp > 0:
                        t.wait_ge(s_ac.h, s_ac.n)
                    for kc in range(NFC):
                        last = t.matmul(p_a[0:32, 0:tw], ipdts[:, kc * 32:(kc + 1) * 32],
                                        h1[:, kc * TE_ + t0:kc * TE_ + t0 + tw],
                                        start=(kc == 0), stop=(kc == NFC - 1))
                    last.then_inc(s_mm.inc(), 1)
                mmd = s_mm.n

                @block.scalar
                def _(sc, sp=sp, t0=t0, tw=tw, mmd=mmd):
                    sc.wait_ge(s_mm.h, mmd)
                    sc.activation(dtsb[:, t0:t0 + tw], p_a[0:32, 0:tw], AF.Copy
                                  ).then_inc(s_ac.inc(), 1)
            acdt = s_ac.n

            @block.scalar
            def _(sc):
                sc.wait_ge(s_ac.h, acdt)
                sc.activation(spt[:], dtsb[:], AF.Abs)
                sc.drain()
                sc.activation(spt[:], spt[:], AF.Exp, scale=-1.0)
                sc.drain()
                sc.activation(spt[:], spt[:], AF.Ln, bias=1.0)
                sc.drain()
                sc.activation(alog[:], dtsb[:], AF.Relu)
                sc.drain().then_inc(s_ac.inc(), 1)
            acsp = s_ac.n

            @block.vector
            def _(v):
                v.wait_ge(s_ac.h, acsp)
                v.scalar_tensor_tensor(alog[:], alog[:], -1.0, spt[:], OP.mult, OP.subtract)
                v.drain().then_inc(s_ev.inc(), 1)
            eval_ = s_ev.n

            for cc in range(8):
                c0 = 3 + cc * 128

                @block.tensor
                def _(t, cc=cc, c0=c0):
                    if cc == 0:
                        t.wait_ge(s_ev.h, eval_)
                    else:
                        t.wait_ge(s_ev.h, s_ev.n)
                    t.transpose(p_c[:, 0:32], alog[:, c0:c0 + 128], idfs[0:32, 0:32]
                                ).then_inc(s_mm.inc(), 1)
                mm1 = s_mm.n

                @block.vector
                def _(v, cc=cc, mm1=mm1):
                    v.wait_ge(s_mm.h, mm1)
                    v.tensor_copy(atm[:], p_c[:, 0:32]).then_inc(s_ev.inc(), 1)
                ev1 = s_ev.n

                @block.tensor
                def _(t, cc=cc, ev1=ev1):
                    t.wait_ge(s_ev.h, ev1)
                    t.matmul(p_ef[:, 0:32], ltis, atm[:], start=True, stop=True
                             ).then_inc(s_mm.inc(), 1)
                mm2 = s_mm.n

                @block.vector
                def _(v, cc=cc, mm2=mm2):
                    v.wait_ge(s_mm.h, mm2)
                    v.tensor_copy(cumtm[:, cc * 32:(cc + 1) * 32], p_ef[:, 0:32]
                                  ).then_inc(s_ev.inc(), 1)
                ev2 = s_ev.n

                @block.scalar
                def _(sc, cc=cc, ev2=ev2):
                    sc.wait_ge(s_ev.h, ev2)
                    sc.activation(ecums[:, cc * 32:(cc + 1) * 32],
                                  cumtm[:, cc * 32:(cc + 1) * 32], AF.Exp
                                  ).then_inc(s_ac.inc(), 1)

                @block.tensor
                def _(t, cc=cc, ev2=ev2):
                    t.wait_ge(s_ev.h, ev2)
                    t.transpose(p_c[0:32, 0:128], cumtm[:, cc * 32:(cc + 1) * 32],
                                idfs[:, 0:128]).then_inc(s_mm.inc(), 1)
                mm3 = s_mm.n

                @block.vector
                def _(v, cc=cc, mm3=mm3):
                    v.wait_ge(s_mm.h, mm3)
                    v.tensor_copy(cumhm[:, cc * 128:(cc + 1) * 128], p_c[0:32, 0:128]
                                  ).then_inc(s_ev.inc(), 1)
                ev3 = s_ev.n

                @block.scalar
                def _(sc, cc=cc, ev3=ev3):
                    sc.wait_ge(s_ev.h, ev3)
                    sc.activation(alos[:, cc:cc + 1],
                                  cumhm[:, cc * 128 + 127:cc * 128 + 128], AF.Exp
                                  ).then_inc(s_ac.inc(), 1)

                @block.vector
                def _(v, cc=cc, ev3=ev3):
                    v.wait_ge(s_ev.h, ev3)
                    v.tensor_scalar(dhm[:], cumhm[:, cc * 128:(cc + 1) * 128],
                                    cumhm[:, cc * 128 + 127:cc * 128 + 128],
                                    None, OP.subtract)
                    v.drain().then_inc(s_ev.inc(), 1)
                ev4 = s_ev.n

                @block.scalar
                def _(sc, cc=cc, ev4=ev4):
                    sc.wait_ge(s_ev.h, ev4)
                    sc.activation(dhm[:], dhm[:], AF.Exp, scale=-1.0
                                  ).then_inc(s_ac.inc(), 1)
                ac4 = s_ac.n

                @block.tensor
                def _(t, cc=cc, ac4=ac4):
                    t.wait_ge(s_ac.h, ac4)
                    t.transpose(p_c[:, 64:96], dhm[:], idfs[0:32, 0:32]
                                ).then_inc(s_mm.inc(), 1)
                mm5 = s_mm.n

                @block.vector
                def _(v, cc=cc, mm5=mm5):
                    v.wait_ge(s_mm.h, mm5)
                    v.tensor_copy(d2etm[:, cc * 32:(cc + 1) * 32], p_c[:, 64:96]
                                  ).then_inc(s_ev.inc(), 1)

            _barrier(block, s_ev, ["tensor", "vector", "scalar", "gpsimd", "sync"])
            _barrier(block, s_ac, ["tensor", "vector", "scalar", "sync"])
            stPC.close()
            p_cb = [EC(nc.psum_tensor(f"p_cb{i}", [128, 128], F32)) for i in range(2)]

            @block.sync
            def _(sy):
                sy.dma_start(out=ecumo[:, :], in_=ecums[:]).then_inc(s_st.dma(), 16)
                sy.dma_start(out=alo[:, :], in_=alos[:]).then_inc(s_st.dma(), 16)

            ip_free = {}
            ev_evac = {}
            state3 = {"ipn": 0}

            pcond3 = {}

            def ip_unit(ocw, dst_tile, dst_off, kind):
                n_ = state3["ipn"]; state3["ipn"] = n_ + 1
                bi = n_ % 3

                @block.gpsimd
                def _(g, ocw=ocw, bi=bi, n_=n_):
                    if n_ >= 3:
                        g.wait_ge(s_mm.h, ip_free[n_ - 3])
                    g.dma_start(out=ipsb[bi][:], in_=ipm[:, ocw * 2048:(ocw + 1) * 2048]
                                ).then_inc(ipl[bi].dma(), 16)
                ilv = ipl[bi].n

                for sp, (t0, tw) in enumerate(SPE):
                    par = sp % 2
                    pp = (p_a, p_b)[par]
                    pc = pcond3.get(par)

                    @block.tensor
                    def _(t, bi=bi, sp=sp, t0=t0, tw=tw, pp=pp, ilv=ilv, n_=n_, pc=pc):
                        if sp == 0:
                            t.wait_ge(ipl[bi].h, ilv)
                        if pc is not None:
                            t.wait_ge(s_ac.h, pc)
                        for kc in range(NFC):
                            last = t.matmul(pp[:, 0:tw],
                                            ipsb[bi][:, kc * 128:(kc + 1) * 128],
                                            h1[:, kc * TE_ + t0:kc * TE_ + t0 + tw],
                                            start=(kc == 0), stop=(kc == NFC - 1))
                        last.then_inc(s_mm.inc(), 1)
                    mmu = s_mm.n
                    if sp == len(SPE) - 1:
                        ip_free[n_] = mmu

                    @block.scalar
                    def _(sc, sp=sp, t0=t0, tw=tw, pp=pp, mmu=mmu,
                          dst_tile=dst_tile, dst_off=dst_off):
                        sc.wait_ge(s_mm.h, mmu)
                        sc.activation(dst_tile[:, dst_off + t0:dst_off + t0 + tw],
                                      pp[:, 0:tw], AF.Copy).then_inc(s_ac.inc(), 1)
                    ev_evac[("ip", n_, sp)] = s_ac.n
                    pcond3[par] = s_ac.n
                return s_ac.n

            z_ev = {}
            for oc in range(16):
                zslot = oc % 2

                @block.scalar
                def _(sc, oc=oc, zslot=zslot):
                    if ("z", oc - 2) in z_ev:
                        sc.wait_ge(stz[zslot].h, z_ev[("z", oc - 2)])
                ac_last = ip_unit(oc, zfull[zslot], 0, "z")

                @block.sync
                def _(sy, oc=oc, zslot=zslot, ac_last=ac_last):
                    sy.wait_ge(s_ac.h, ac_last)
                    sy.dma_start(out=zo[:, oc * TOK:(oc + 1) * TOK],
                                 in_=zfull[zslot][:, 3:TE_]).then_inc(stz[zslot].dma(), 16)
                z_ev[("z", oc)] = stz[zslot].n

            conv_seq = {"ev": 0}

            def conv_unit(src_slot, out_dst, ch, gate_ac):
                @block.vector
                def _(v, src_slot=src_slot, out_dst=out_dst, ch=ch, gate_ac=gate_ac):
                    v.wait_ge(s_ac.h, gate_ac)
                    if conv_seq["ev"]:
                        v.wait_ge(s_ev.h, conv_seq["ev"])
                    xb = xbcs[src_slot]
                    v.tensor_scalar(cacc[:], xb[:, 0:TOK], cvws[:, ch * 4:ch * 4 + 1],
                                    None, OP.mult)
                    v.drain()
                    for j in range(1, 4):
                        v.scalar_tensor_tensor(cacc[:], xb[:, j:j + TOK],
                                               cvws[:, ch * 4 + j:ch * 4 + j + 1],
                                               cacc[:], OP.mult, OP.add)
                        v.drain()
                    v.tensor_scalar(out_dst[:], cacc[:], cvbs[:, ch:ch + 1], None, OP.add)
                    v.drain().then_inc(s_ev.inc(), 1)
                conv_seq["ev"] = s_ev.n
                return s_ev.n

            ssd_state = {"tr_ev": 0, "ssd_ev": 0, "ssd_ac": 0, "ssd_mm": 0,
                         "stS": 0, "convfree": {}}
            for fc in range(NFC):
                if ("C", fc - 1) in ssd_state["convfree"]:
                    @block.scalar
                    def _(sc, g=ssd_state["convfree"][("C", fc - 1)]):
                        sc.wait_ge(s_ev.h, g)
                if fc > 0:
                    @block.vector
                    def _(v, g=ssd_state["ssd_ev"]):
                        v.wait_ge(s_ev.h, g)
                    @block.scalar
                    def _(sc, g=ssd_state["ssd_ev"]):
                        sc.wait_ge(s_ev.h, g)
                acx = ip_unit(16 + fc, xbcs[0], 0, "x")
                ev_cx = conv_unit(0, cvob[0], fc, acx)
                ssd_state["convfree"][("x", fc)] = ev_cx
                acb_ = ip_unit(32 + fc, xbcs[1], 0, "B")
                ev_cb = conv_unit(1, cvob[1], 16 + fc, acb_)
                @block.scalar
                def _(sc, g=ev_cx):
                    sc.wait_ge(s_ev.h, g)
                acc_ = ip_unit(48 + fc, xbcs[0], 0, "C")
                if "co_st" in ssd_state:
                    @block.vector
                    def _(v, g=ssd_state["co_st"]):
                        v.wait_ge(stCo.h, g)
                ev_cc = conv_unit(0, cvobC, 32 + fc, acc_)
                ssd_state["convfree"][("C", fc)] = ev_cc

                @block.sync
                def _(sy, fc=fc, g=ev_cc):
                    sy.wait_ge(s_ev.h, g)
                    sy.dma_start(out=co[:, fc * TOK:(fc + 1) * TOK], in_=cvobC[:]
                                 ).then_inc(stCo.dma(), 16)
                ssd_state["co_st"] = stCo.n

                for cc in range(8):
                    for hh in range(2):
                        @block.tensor
                        def _(t, fc=fc, cc=cc, hh=hh, g=ev_cb):
                            t.wait_ge(s_ev.h, g)
                            if ssd_state["tr_ev"]:
                                t.wait_ge(s_ev.h, ssd_state["tr_ev"])
                            idsl = identb[hh * 64:hh * 64 + 64, hh * 64:hh * 64 + 64]
                            t.transpose(p_t[:, 0:64],
                                        cvob[0][hh * 64:hh * 64 + 64, cc * 128:(cc + 1) * 128],
                                        idsl)
                            t.transpose(p_t[:, 64:128],
                                        cvob[1][hh * 64:hh * 64 + 64, cc * 128:(cc + 1) * 128],
                                        idsl).then_inc(s_mm.inc(), 1)
                        mmt = s_mm.n

                        @block.vector
                        def _(v, cc=cc, hh=hh, mmt=mmt):
                            v.wait_ge(s_mm.h, mmt)
                            off = (cc * 2 + hh) * 64
                            v.tensor_copy(xtmc[:, off:off + 64], p_t[:, 0:64])
                            v.tensor_copy(btmc[:, off:off + 64], p_t[:, 64:128]
                                          ).then_inc(s_ev.inc(), 1)
                        ssd_state["tr_ev"] = s_ev.n
                tr_all = s_ev.n

                units = [(cc, hh) for cc in range(8) for hh in range(2)]
                NU = len(units)
                st_ = {"t1": {}, "s2": {}, "s3": {}, "t4": {}, "s5": {},
                       "t6": {}, "s7": {}}
                pv = ssd_state

                def T1(i, fc=fc, tr_all=tr_all):
                    cc, hh = units[i]
                    h_ = 2 * fc + hh
                    sl_ = i % 2

                    @block.tensor
                    def _(t, i=i, cc=cc, h_=h_, sl_=sl_):
                        if i == 0:
                            t.wait_ge(s_ev.h, tr_all)
                            if "prev_s2" in pv:
                                t.wait_ge(s_ev.h, pv["prev_s2"])
                        if i - 2 in st_["s2"]:
                            t.wait_ge(s_ev.h, st_["s2"][i - 2])
                        t.matmul(p_cb[sl_][:, 0:128], selms[:, h_ * 128:(h_ + 1) * 128],
                                 cumhm[0:32, cc * 128:(cc + 1) * 128],
                                 start=True, stop=True).then_inc(s_mm.inc(), 1)
                    st_["t1"][i] = s_mm.n

                def V2(i, fc=fc):
                    cc, hh = units[i]
                    h_ = 2 * fc + hh
                    sl_ = i % 2

                    @block.vector
                    def _(v, i=i, cc=cc, h_=h_, sl_=sl_):
                        v.wait_ge(s_mm.h, st_["t1"][i])
                        if i - 2 in st_["s3"]:
                            v.wait_ge(s_ac.h, st_["s3"][i - 2])
                        elif "prev_s3" in pv and i < 2:
                            v.wait_ge(s_ac.h, pv["prev_s3"])
                        v.tensor_scalar(dTs[sl_][:], p_cb[sl_][:, 0:128],
                                        cumtm[:, cc * 32 + h_:cc * 32 + h_ + 1],
                                        None, OP.subtract)
                        v.drain()
                        v.tensor_tensor(dTs[sl_][:], dTs[sl_][:], mkts, OP.add
                                        ).then_inc(s_ev.inc(), 1)
                    st_["s2"][i] = s_ev.n

                def A3(i):
                    sl_ = i % 2

                    @block.scalar
                    def _(sc, i=i, sl_=sl_):
                        sc.wait_ge(s_ev.h, st_["s2"][i])
                        if i - 2 in st_["s5"]:
                            sc.wait_ge(s_ev.h, st_["s5"][i - 2])
                        elif "prev_s5" in pv and i < 2:
                            sc.wait_ge(s_ev.h, pv["prev_s5"])
                        sc.activation(segT[sl_][:], dTs[sl_][:], AF.Exp
                                      ).then_inc(s_ac.inc(), 1)
                    st_["s3"][i] = s_ac.n

                def T4(i):
                    cc, hh = units[i]
                    sl_ = i % 2

                    @block.tensor
                    def _(t, i=i, cc=cc, hh=hh, sl_=sl_):
                        if i - 2 in st_["s5"]:
                            t.wait_ge(s_ev.h, st_["s5"][i - 2])
                        elif "prev_s5" in pv and i < 2:
                            t.wait_ge(s_ev.h, pv["prev_s5"])
                        t.matmul(p_d[sl_][:, 0:128],
                                 cvob[1][hh * 64:hh * 64 + 64, cc * 128:(cc + 1) * 128],
                                 cvobC[hh * 64:hh * 64 + 64, cc * 128:(cc + 1) * 128],
                                 start=True, stop=True).then_inc(s_mm.inc(), 1)
                    st_["t4"][i] = s_mm.n

                def V5(i, fc=fc):
                    cc, hh = units[i]
                    h_ = 2 * fc + hh
                    sl_ = i % 2
                    off = (cc * 2 + hh) * 64

                    @block.vector
                    def _(v, i=i, cc=cc, h_=h_, sl_=sl_, off=off):
                        v.wait_ge(s_ac.h, st_["s3"][i])
                        v.wait_ge(s_mm.h, st_["t4"][i])
                        if i - 2 in st_["t6"]:
                            v.wait_ge(s_mm.h, st_["t6"][i - 2])
                        elif "prev_t6" in pv and i < 2:
                            v.wait_ge(s_mm.h, pv["prev_t6"])
                        v.tensor_tensor(scT[sl_][:], segT[sl_][:], p_d[sl_][:, 0:128], OP.mult)
                        v.tensor_scalar(bsb[sl_][:], btmc[:, off:off + 64],
                                        d2etm[:, cc * 32 + h_:cc * 32 + h_ + 1],
                                        None, OP.mult).then_inc(s_ev.inc(), 1)
                    st_["s5"][i] = s_ev.n

                def T6(i):
                    cc, hh = units[i]
                    sl_ = i % 2
                    off = (cc * 2 + hh) * 64

                    @block.tensor
                    def _(t, i=i, sl_=sl_, off=off):
                        t.wait_ge(s_ev.h, st_["s5"][i])
                        if i - 1 in st_["s7"]:
                            t.wait_ge(s_ev.h, st_["s7"][i - 1])
                        elif "prev_s7" in pv and i < 1:
                            t.wait_ge(s_ev.h, pv["prev_s7"])
                        t.matmul(p_ef[:, 0:64], scT[sl_][:], xtmc[:, off:off + 64],
                                 start=True, stop=True)
                        t.matmul(p_ef[0:64, 64:128], bsb[sl_][:], xtmc[:, off:off + 64],
                                 start=True, stop=True).then_inc(s_mm.inc(), 1)
                    st_["t6"][i] = s_mm.n

                def V7(i, fc=fc):
                    cc, hh = units[i]
                    h_ = 2 * fc + hh
                    sl_ = i % 2
                    off = (cc * 2 + hh) * 64

                    @block.vector
                    def _(v, i=i, cc=cc, hh=hh, h_=h_, off=off):
                        v.wait_ge(s_mm.h, st_["t6"][i])
                        if hh == 0 and "sst_st" in pv:
                            v.wait_ge(stS.h, pv["sst_st"])
                        v.scalar_tensor_tensor(
                            y1sb[:, cc * H + h_ * 64:cc * H + h_ * 64 + 64],
                            xtmc[:, off:off + 64], dbcs[:, h_:h_ + 1],
                            p_ef[:, 0:64], OP.mult, OP.add)
                        v.tensor_copy(sst[:, hh * 64:hh * 64 + 64], p_ef[0:64, 64:128]
                                      ).then_inc(s_ev.inc(), 1)
                    st_["s7"][i] = s_ev.n

                    if hh == 1:
                        @block.sync
                        def _(sy, fc=fc, cc=cc, g=st_["s7"][i]):
                            sy.wait_ge(s_ev.h, g)
                            base = ((fc * 8 + cc) * 2) * 64
                            sy.dma_start(out=so[:, base:base + 128], in_=sst[:]
                                         ).then_inc(stS.dma(), 16)
                        pv["sst_st"] = stS.n

                for i in range(NU + 2):
                    if i < NU:
                        T1(i); V2(i); A3(i); T4(i)
                    if i >= 2 and i - 2 < NU:
                        V7(i - 2)
                    if i >= 1 and i - 1 < NU:
                        V5(i - 1); T6(i - 1)
                pv["prev_s2"] = st_["s2"][NU - 1]
                pv["prev_s3"] = st_["s3"][NU - 1]
                pv["prev_s5"] = st_["s5"][NU - 1]
                pv["prev_t6"] = st_["t6"][NU - 1]
                pv["prev_s7"] = st_["s7"][NU - 1]
                pv["ssd_ev"] = st_["s7"][NU - 1]
                pv["ssd_mm_all"] = st_["t4"][NU - 1]

            @block.sync
            def _(sy):
                sy.wait_ge(s_ev.h, s_ev.n)
                sy.dma_start(out=y1o[:, :], in_=y1sb[:]).then_inc(s_st.dma(), 16)
            _barrier(block, s_st, ["tensor", "vector", "scalar", "gpsimd", "sync"])
            for cc_ in stz + ipl + [stS, stCo]:
                _barrier(block, cc_, ["tensor", "vector", "scalar", "gpsimd", "sync"])

        stH.close()
        with contextlib.ExitStack() as stD:
            ED = stD.enter_context
            dacc = ED(nc.sbuf_tensor("dacc", [128, NFC * TOK], F32))
            wgu0 = ED(nc.sbuf_tensor("wgu0", [128, 2 * H], BF16))
            wgu1 = ED(nc.sbuf_tensor("wgu1", [128, 2 * H], BF16))
            wgu2 = ED(nc.sbuf_tensor("wgu2", [128, 2 * H], BF16))
            wd0 = ED(nc.sbuf_tensor("wd0", [128, 4096], BF16))
            wd1 = ED(nc.sbuf_tensor("wd1", [128, 4096], BF16))
            hbuf = ED(nc.sbuf_tensor("hbuf", [128, 32 * TE_], BF16))
            gs0 = ED(nc.sbuf_tensor("gs0", [128, 512], BF16))
            gs1 = ED(nc.sbuf_tensor("gs1", [128, 512], BF16))
            us0 = ED(nc.sbuf_tensor("us0", [128, 512], BF16))
            us1 = ED(nc.sbuf_tensor("us1", [128, 512], BF16))
            gscs = ED(nc.sbuf_tensor("gscs", [128, NIC], F32))
            uscs = ED(nc.sbuf_tensor("uscs", [128, NIC], F32))
            dscs = ED(nc.sbuf_tensor("dscs", [128, 16], F32))
            p_g0 = ED(nc.psum_tensor("p_g0", [128, 512], F32))
            p_g1 = ED(nc.psum_tensor("p_g1", [128, 512], F32))
            p_u0 = ED(nc.psum_tensor("p_u0", [128, 512], F32))
            p_u1 = ED(nc.psum_tensor("p_u1", [128, 512], F32))
            p_d0 = ED(nc.psum_tensor("p_d0", [128, 512], F32))
            p_d1 = ED(nc.psum_tensor("p_d1", [128, 512], F32))

            @block.sync
            def _(sy):
                sy.dma_start(out=gscs[:], in_=gsc[:, :]).then_inc(s_ld.dma(), 16)
                sy.dma_start(out=uscs[:], in_=usc[:, :]).then_inc(s_ld.dma(), 16)
                sy.dma_start(out=dscs[:], in_=dsc[:, :]).then_inc(s_ld.dma(), 16)
            _barrier(block, s_ld, ["scalar", "vector", "tensor", "gpsimd"])

            bufs = dict(wgu=[wgu0, wgu1, wgu2], wd=[wd0, wd1], hbuf=hbuf,
                        gs=[gs0, gs1], us=[us0, us1],
                        p_g=[p_g0, p_g1], p_u=[p_u0, p_u1], p_d=[p_d0, p_d1],
                        wl2=wl2, wld=wld)
            emit_ffn(nc, block, sems, xn1, TE_, SPO,
                     dict(guw=guw, dw=dw), True,
                     (gscs, uscs, dscs), dacc, True, bufs,
                     dacc_T=TOK, dacc_off=3, hoist=True)
            _barrier(block, ph, ["tensor", "vector", "scalar", "sync", "gpsimd"])

            @block.sync
            def _(sy):
                sy.dma_start(out=qacc[:, :], in_=dacc[:]).then_inc(s_st.dma(), 16)
                sy.wait_ge(s_st.h, s_st.n)

    _PROG_CACHE["p3"] = nc
    return nc

def p3_in_maps(inp, p2res):
    cos_dummy = None
    ipw = np.asarray(inp["L1_attn_in_proj_w"])
    selm = np.zeros((32, 4096), np.float32)
    for h in range(32):
        selm[h, h * 128:(h + 1) * 128] = 1.0
    lt = np.tril(np.ones((128, 128), np.float32)).T
    mkt = np.where(np.arange(128)[None, :] >= np.arange(128)[:, None],
                   0.0, -1e30).astype(np.float32)
    convw = np.asarray(inp["L1_attn_conv_w"])
    cw_img = np.ascontiguousarray(convw.reshape(48, 128, 4).transpose(1, 0, 2).reshape(128, 192))
    cb_img = _pack_rows(np.asarray(inp["L1_attn_conv_b"]))
    shared = {
        "n1w": _pack_rows(np.asarray(inp["L1_attn_norm1_w"])),
        "ipm": _pack_lhsT_icmajor(ipw[0:8192]),
        "ipdt": _pack_lhsT(ipw[8192:8224]),
        "cvw": cw_img, "cvb": cb_img,
        "lti": lt, "mkt": mkt,
        "idf": np.eye(128, dtype=np.float32),
        "idb": np.eye(128, dtype=np.float32).astype(BFNP),
        "dbc": np.ascontiguousarray(
            np.broadcast_to(np.asarray(inp["L1_attn_D"])[None, :], (128, 32))),
        "selm": selm,
        "guw": _pack_guw(np.asarray(inp["L1_ffn_gate_w"]), np.asarray(inp["L1_ffn_up_w"])),
        "dw": np.ascontiguousarray(
            np.asarray(inp["L1_ffn_down_w"]).reshape(16, 128, 2, 32, 128)
            .transpose(4, 2, 0, 3, 1).reshape(128, 131072)),
        "gsc": _pack_rows(np.asarray(inp["L1_ffn_gate_scale"])),
        "usc": _pack_rows(np.asarray(inp["L1_ffn_up_scale"])),
        "dsc": _pack_rows(np.asarray(inp["L1_ffn_down_scale"])),
    }
    maps = []
    for c in range(NCORE):
        m = dict(shared)
        xe = np.zeros((128, NFC * TE), dtype=BFNP)
        own = p2res[c]["xn1o"]
        for fc in range(NFC):
            xe[:, fc * TE + 3:(fc + 1) * TE] = own[:, fc * TOK:(fc + 1) * TOK]
            if c > 0:
                xe[:, fc * TE:fc * TE + 3] = \
                    p2res[c - 1]["xn1o"][:, fc * TOK + TOK - 3:fc * TOK + TOK]
        m["xn1e"] = xe
        maps.append(m)
    return maps

def run_p3(inp, p2res):
    nc = build_p3()
    maps = p3_in_maps(inp, p2res)
    res = run_bass_kernel_spmd(nc, maps, core_ids=list(range(NCORE)))
    return res.results


def build_p4(core_base_list=None):
    if "p4" in _PROG_CACHE:
        return _PROG_CACHE["p4"]
    nc = bass.Bass("TRN2", target_bir_lowering=False)
    y1i = nc.dram_tensor("y1i", [128, 8 * H], BF16, kind="ExternalInput").ap()
    ci = nc.dram_tensor("ci", [128, NFC * TOK], BF16, kind="ExternalInput").ap()
    zi = nc.dram_tensor("zi", [128, NFC * TOK], BF16, kind="ExternalInput").ap()
    ecumi = nc.dram_tensor("ecumi", [128, 8 * 32], F32, kind="ExternalInput").ap()
    xn1i = nc.dram_tensor("xn1i", [128, NFC * TOK], BF16, kind="ExternalInput").ap()
    x1i = nc.dram_tensor("x1i", [128, NFC * TOK], F32, kind="ExternalInput").ap()
    qacci = nc.dram_tensor("qacci", [128, NFC * TOK], F32, kind="ExternalInput").ap()
    srep = nc.dram_tensor("srep", [8192, 2048], F32, kind="ExternalInput").ap()
    alexp = nc.dram_tensor("alexp", [64, 2048], F32, kind="ExternalInput").ap()
    ownsel = nc.dram_tensor("ownsel", [128, 64], F32, kind="ExternalInput").ap()
    opw = nc.dram_tensor("opw", [128, 16 * H], F32, kind="ExternalInput").ap()
    n2w_d = nc.dram_tensor("n2w", [128, 16], F32, kind="ExternalInput").ap()
    zb_d = nc.dram_tensor("zb", [128, 16], F32, kind="ExternalInput").ap()
    idb = nc.dram_tensor("idb", [128, 128], BF16, kind="ExternalInput").ap()
    guw = nc.dram_tensor("guw", [128, NIC * 2 * H], F32, kind="ExternalInput").ap()
    dw = nc.dram_tensor("dw", [128, 32 * 4096], F32, kind="ExternalInput").ap()
    outo = nc.dram_tensor("outo", [128, NFC * TOK], F32, kind="ExternalOutput").ap()

    with contextlib.ExitStack() as stack:
        E = stack.enter_context
        dacc = E(nc.sbuf_tensor("dacc", [128, NFC * TOK], F32))
        onesb = E(nc.sbuf_tensor("onesb", [128, 1], BF16))
        onesf = E(nc.sbuf_tensor("onesf", [1, 128], F32))
        eps_col = E(nc.sbuf_tensor("eps_col", [128, 1], F32))
        n2ws = E(nc.sbuf_tensor("n2ws", [128, 16], F32))
        zbs = E(nc.sbuf_tensor("zbs", [128, 16], F32))


        s_ld = Ctr(stack, nc, "s_ld")
        s_wl = Ctr(stack, nc, "s_wl")
        s_mm = Ctr(stack, nc, "s_mm")
        s_ev = Ctr(stack, nc, "s_ev")
        s_ac = Ctr(stack, nc, "s_ac")
        s_st = Ctr(stack, nc, "s_st")
        s_fr = Ctr(stack, nc, "s_fr")
        ph = Ctr(stack, nc, "ph")
        sl2 = [Ctr(stack, nc, f"sl2_{i}") for i in range(2)]
        al2 = [Ctr(stack, nc, f"al2_{i}") for i in range(2)]
        zl2 = [Ctr(stack, nc, f"zl2_{i}") for i in range(2)]
        wl2 = [Ctr(stack, nc, f"wl2_{i}") for i in range(3)]
        wld = [Ctr(stack, nc, f"wld_{i}") for i in range(2)]
        xl2 = [Ctr(stack, nc, f"xl2_{i}") for i in range(2)]
        st5 = [Ctr(stack, nc, f"st5_{i}") for i in range(2)]
        sems = dict(ld=s_ld, wl=s_wl, mm=s_mm, ev=s_ev, ac=s_ac, st=s_st,
                    fr=s_fr, ph=ph)
        block = E(nc.Block())

        @block.sync
        def _(sy):
            sy.dma_start(out=n2ws[:], in_=n2w_d[:, :]).then_inc(s_ld.dma(), 16)
            sy.dma_start(out=zbs[:], in_=zb_d[:, :]).then_inc(s_ld.dma(), 16)
            sy.dma_start(out=dacc[:], in_=qacci[:, :]).then_inc(s_ld.dma(), 16)

        @block.gpsimd
        def _(g):
            g.memset(onesb[:], 1.0)
            g.memset(onesf[:], 1.0)
            g.memset(eps_col[:], EPS).then_inc(s_fr.inc(), 1)
        _barrier(block, s_ld, ["vector", "scalar", "tensor", "gpsimd"])
        _barrier(block, s_fr, ["vector", "scalar", "tensor"])

        stG = contextlib.ExitStack()
        gated = stG.enter_context(nc.sbuf_tensor("gated", [128, NFC * TOK], BF16))

        with contextlib.ExitStack() as st1:
            E1 = st1.enter_context
            y1sb = E1(nc.sbuf_tensor("y1sb", [128, 8 * H], BF16))
            identb = E1(nc.sbuf_tensor("identb", [128, 128], BF16))
            owns = E1(nc.sbuf_tensor("owns", [128, 64], F32))
            cisb = gated
            ecums = E1(nc.sbuf_tensor("ecums", [128, 8 * 32], F32))
            Hhm = E1(nc.sbuf_tensor("Hhm", [128, 2048], F32))
            Ssb = [E1(nc.sbuf_tensor(f"Ssb{i}", [128, 2048], F32)) for i in range(2)]
            Hsnap = E1(nc.sbuf_tensor("Hsnap", [128, 8 * 2048], BF16))
            arow_t = E1(nc.sbuf_tensor("arow_t", [1, 2048], F32))
            arow = [arow_t, arow_t]
            ztile_t = E1(nc.sbuf_tensor("ztile_t", [128, TOK], BF16))
            ztile = [ztile_t, ztile_t]
            p_ab = E1(nc.psum_tensor("p_ab", [128, 2048], F32))
            p_yi0 = E1(nc.psum_tensor("p_yi0", [128, 64], F32))
            p_yi1 = E1(nc.psum_tensor("p_yi1", [128, 64], F32))
            p_tr = E1(nc.psum_tensor("p_tr", [128, 128], BF16))

            ld_prev = s_ld.n

            @block.sync
            def _(sy, ld_prev=ld_prev):
                sy.wait_ge(s_ld.h, ld_prev)
                sy.dma_start(out=y1sb[:], in_=y1i[:, :]).then_inc(s_ld.dma(), 16)
                sy.dma_start(out=identb[:], in_=idb[:, :]).then_inc(s_ld.dma(), 16)
                sy.dma_start(out=owns[:], in_=ownsel[:, :]).then_inc(s_ld.dma(), 16)
                sy.dma_start(out=cisb[:], in_=ci[:, :]).then_inc(s_ld.dma(), 16)
                sy.dma_start(out=ecums[:], in_=ecumi[:, :]).then_inc(s_ld.dma(), 16)

            @block.vector
            def _(v):
                v.memset(Hhm[:], 0.0)
                v.drain()
                v.memset(Hsnap[:], 0.0)
                v.drain().then_inc(s_ev.inc(), 1)
            _barrier(block, s_ld, ["vector", "scalar", "tensor", "gpsimd"])

            chain_ev = {"h": s_ev.n}
            for gc in range(GCH):
                bi = gc % 2
                cc = gc % 8

                @block.sync
                def _(sy, gc=gc, bi=bi):
                    if gc >= 2:
                        sy.wait_ge(s_ev.h, chain_ev[("used", gc - 2)])
                    sy.dma_start(out=Ssb[bi][:], in_=srep[gc * 128:(gc + 1) * 128, :]
                                 ).then_inc(sl2[bi].dma(), 16)
                    if gc >= 1:
                        sy.wait_ge(s_mm.h, chain_ev[("ab", gc - 1)])
                    sy.dma_start(out=arow_t[:], in_=alexp[gc:gc + 1, :]
                                 ).then_inc(al2[bi].dma(), 16)
                slv, alv = sl2[bi].n, al2[bi].n

                @block.tensor
                def _(t, bi=bi, alv=alv, gc=gc):
                    t.wait_ge(al2[bi].h, alv)
                    t.wait_ge(s_ev.h, chain_ev["h"])
                    for q4 in range(4):
                        last = t.matmul(p_ab[:, q4 * 512:(q4 + 1) * 512],
                                        onesf[0:1, 0:128],
                                        arow[bi][0:1, q4 * 512:(q4 + 1) * 512],
                                        start=True, stop=True)
                    last.then_inc(s_mm.inc(), 1)
                mm_ab = s_mm.n
                chain_ev[("ab", gc)] = mm_ab

                @block.vector
                def _(v, gc=gc, cc=cc, bi=bi, slv=slv, mm_ab=mm_ab):
                    v.wait_ge(s_mm.h, mm_ab)
                    v.wait_ge(sl2[bi].h, slv)
                    v.scalar_tensor_tensor(Hsnap[:, cc * 2048:(cc + 1) * 2048],
                                           Hhm[:], owns[:, gc:gc + 1],
                                           Hsnap[:, cc * 2048:(cc + 1) * 2048],
                                           OP.mult, OP.add)
                    v.drain()
                    v.tensor_tensor(Hhm[:], Hhm[:], p_ab[:], OP.mult)
                    v.drain()
                    v.tensor_tensor(Hhm[:], Hhm[:], Ssb[bi][:], OP.add)
                    v.drain().then_inc(s_ev.inc(), 1)
                chain_ev["h"] = s_ev.n
                chain_ev[("used", gc)] = s_ev.n


            yiv = {"pc": {}, "hb": 0}
            ev_hb = chain_ev["h"]
            for cc in range(8):
                mm_list = []
                for h_ in range(NHM):
                    fc, hh = h_ // 2, h_ % 2
                    pj = h_ % 2
                    gidx = cc * 32 + h_

                    @block.tensor
                    def _(t, h_=h_, fc=fc, hh=hh, cc=cc, pj=pj, ev_hb=ev_hb, gidx=gidx):
                        if h_ == 0:
                            t.wait_ge(s_ev.h, ev_hb)
                        if (pj, gidx - 2) in yiv["pc"]:
                            t.wait_ge(s_ev.h, yiv["pc"][(pj, gidx - 2)])
                        pyx = (p_yi0, p_yi1)[pj]
                        t.matmul(pyx[:, 0:64],
                                 cisb[hh * 64:hh * 64 + 64, fc * TOK + cc * 128:fc * TOK + (cc + 1) * 128],
                                 Hsnap[hh * 64:hh * 64 + 64, cc * 2048 + h_ * 64:cc * 2048 + (h_ + 1) * 64],
                                 start=True, stop=True).then_inc(s_mm.inc(), 1)
                    mm_yi = s_mm.n
                    mm_list.append(mm_yi)

                    @block.vector
                    def _(v, gc_cc=cc, h_=h_, cc=cc, pj=pj, mm_yi=mm_yi):
                        v.wait_ge(s_mm.h, mm_yi)
                        sl_ = slice(cc * H + h_ * 64, cc * H + (h_ + 1) * 64)
                        pyx = (p_yi0, p_yi1)[pj]
                        v.scalar_tensor_tensor(y1sb[:, sl_], pyx[:, 0:64],
                                               ecums[:, cc * 32 + h_:cc * 32 + h_ + 1],
                                               y1sb[:, sl_], OP.mult, OP.add
                                               ).then_inc(s_ev.inc(), 1)
                    yiv["pc"][(pj, gidx)] = s_ev.n
                pass

            _barrier(block, s_ev, ["tensor", "scalar", "gpsimd", "sync"])

            zld = {}
            for fc in range(NFC):
                bi = fc % 2

                @block.sync
                def _(sy, fc=fc, bi=bi):
                    if fc >= 1:
                        sy.wait_ge(s_ac.h, zld[fc - 1])
                    sy.dma_start(out=ztile[bi][:], in_=zi[:, fc * TOK:(fc + 1) * TOK]
                                 ).then_inc(zl2[bi].dma(), 16)
                zlv = zl2[bi].n

                @block.scalar
                def _(sc, fc=fc, bi=bi, zlv=zlv):
                    sc.wait_ge(zl2[bi].h, zlv)
                    sc.activation(gated[:, fc * TOK:(fc + 1) * TOK], ztile[bi][:],
                                  AF.Silu, bias=zbs[:, fc:fc + 1]).then_inc(s_ac.inc(), 1)
                zld[fc] = s_ac.n
            ac_sil = s_ac.n

            tr_ev = {"n": 0}
            for cc in range(8):
                for h_ in range(NHM):
                    fc, hh = h_ // 2, h_ % 2

                    @block.tensor
                    def _(t, cc=cc, h_=h_, ac_sil=ac_sil):
                        if cc == 0 and h_ == 0:
                            t.wait_ge(s_ac.h, ac_sil)
                        if tr_ev["n"]:
                            t.wait_ge(s_ev.h, tr_ev["n"])
                        t.transpose(p_tr[:, 0:64].rearrange("p f -> p f") if False else p_tr[0:64, 0:128],
                                    y1sb[:, cc * H + h_ * 64:cc * H + (h_ + 1) * 64],
                                    identb[0:128, 0:128]).then_inc(s_mm.inc(), 1)
                    mm_tr = s_mm.n

                    @block.vector
                    def _(v, cc=cc, h_=h_, fc=fc, hh=hh, mm_tr=mm_tr):
                        v.wait_ge(s_mm.h, mm_tr)
                        sl_ = slice(fc * TOK + cc * 128, fc * TOK + (cc + 1) * 128)
                        dst = gated[hh * 64:hh * 64 + 64, sl_]
                        v.tensor_tensor(dst, dst, p_tr[0:64, 0:128], OP.mult
                                        ).then_inc(s_ev.inc(), 1)
                    tr_ev["n"] = s_ev.n

            _barrier(block, s_ev, ["tensor", "scalar", "gpsimd", "sync"])
            _barrier(block, s_ac, ["tensor", "vector", "gpsimd", "sync"])

        with contextlib.ExitStack() as st2:
            E2 = st2.enter_context
            h2b = E2(nc.sbuf_tensor("h2b", [128, NFC * TOK], BF16))
            sqb2 = E2(nc.sbuf_tensor("sqb2", [128, NFC * TOK], BF16))
            opsb = [E2(nc.sbuf_tensor(f"opsb{i}", [128, 2048], BF16)) for i in range(2)]
            xtl = [E2(nc.sbuf_tensor(f"xtl{i}", [128, 512], BF16)) for i in range(2)]
            rows = E2(nc.sbuf_tensor("rows", [1, 3 * TOK], F32))
            tf0 = E2(nc.sbuf_tensor("tf0", [128, 512], F32))
            tf1 = E2(nc.sbuf_tensor("tf1", [128, 512], F32))
            p_a = E2(nc.psum_tensor("p_a", [128, 512], F32))
            p_b = E2(nc.psum_tensor("p_b", [128, 512], F32))
            p_s = E2(nc.psum_tensor("p_s", [1, TOK], F32))
            p_q = E2(nc.psum_tensor("p_q", [1, TOK], F32))

            op_free = {}
            pcond = {}
            for oc in range(16):
                bi = oc % 2

                @block.gpsimd
                def _(g, oc=oc, bi=bi):
                    if oc >= 2:
                        g.wait_ge(s_mm.h, op_free[oc - 2])
                    g.dma_start(out=opsb[bi][:], in_=opw[:, oc * 2048:(oc + 1) * 2048]
                                ).then_inc(wl2[bi].dma(), 16)
                wlv = wl2[bi].n

                for sp, (t0, tw) in enumerate(SP2):
                    u = oc * 2 + sp
                    j = u % 2
                    pp = (p_a, p_b)[j]

                    @block.sync
                    def _(sy, oc=oc, t0=t0, j=j, u=u):
                        if u - 2 in pcond:
                            sy.wait_ge(s_ev.h, pcond[u - 2])
                        sy.dma_start(out=xtl[j][:],
                                     in_=xn1i[:, oc * TOK + t0:oc * TOK + t0 + 512]
                                     ).then_inc(xl2[j].dma(), 16)
                    xlv = xl2[j].n

                    @block.tensor
                    def _(t, oc=oc, bi=bi, sp=sp, t0=t0, pp=pp, u=u, wlv=wlv):
                        if sp == 0:
                            t.wait_ge(wl2[bi].h, wlv)
                        if u - 2 in pcond:
                            t.wait_ge(s_ev.h, pcond[u - 2])
                        for kc in range(NFC):
                            last = t.matmul(pp[:, 0:512],
                                            opsb[bi][:, kc * 128:(kc + 1) * 128],
                                            gated[:, kc * TOK + t0:kc * TOK + t0 + 512],
                                            start=(kc == 0), stop=(kc == NFC - 1))
                        last.then_inc(s_mm.inc(), 1)
                    mmo = s_mm.n
                    if sp == 1:
                        op_free[oc] = mmo

                    @block.vector
                    def _(v, oc=oc, t0=t0, j=j, pp=pp, mmo=mmo, xlv=xlv):
                        v.wait_ge(s_mm.h, mmo)
                        v.wait_ge(xl2[j].h, xlv)
                        sl_ = slice(oc * TOK + t0, oc * TOK + t0 + 512)
                        v.tensor_tensor(h2b[:, sl_], pp[:, 0:512], xtl[j][:], OP.add)
                        v.drain()
                        v.tensor_tensor(dacc[:, sl_], dacc[:, sl_], h2b[:, sl_], OP.add
                                        ).then_inc(s_ev.inc(), 1)
                    pcond[u] = s_ev.n
            _barrier(block, s_ev, ["tensor", "scalar", "gpsimd", "sync"])

            tmps = dict(sqb=sqb2, rows=rows, tf=[tf0, tf1],
                        onesb=onesb, onesf=onesf, eps_col=eps_col)
            emit_rms(nc, block, sems, h2b, gated, TOK, SP2, tmps,
                     (p_s, p_q, p_a, p_b), n2ws)
            _barrier(block, ph, ["tensor", "vector", "scalar", "gpsimd", "sync"])

        with contextlib.ExitStack() as stF:
            EF = stF.enter_context
            wgu0 = EF(nc.sbuf_tensor("wgu0", [128, 2 * H], BF16))
            wgu1 = EF(nc.sbuf_tensor("wgu1", [128, 2 * H], BF16))
            wgu2 = EF(nc.sbuf_tensor("wgu2", [128, 2 * H], BF16))
            wd0 = EF(nc.sbuf_tensor("wd0", [128, 4096], BF16))
            wd1 = EF(nc.sbuf_tensor("wd1", [128, 4096], BF16))
            hbuf = EF(nc.sbuf_tensor("hbuf", [128, 32 * TOK], BF16))
            gs0 = EF(nc.sbuf_tensor("gs0", [128, 512], BF16))
            gs1 = EF(nc.sbuf_tensor("gs1", [128, 512], BF16))
            us0 = EF(nc.sbuf_tensor("us0", [128, 512], BF16))
            us1 = EF(nc.sbuf_tensor("us1", [128, 512], BF16))
            p_g0 = EF(nc.psum_tensor("p_g0", [128, 512], F32))
            p_g1 = EF(nc.psum_tensor("p_g1", [128, 512], F32))
            p_u0 = EF(nc.psum_tensor("p_u0", [128, 512], F32))
            p_u1 = EF(nc.psum_tensor("p_u1", [128, 512], F32))
            p_d0 = EF(nc.psum_tensor("p_d0", [128, 512], F32))
            p_d1 = EF(nc.psum_tensor("p_d1", [128, 512], F32))

            bufs = dict(wgu=[wgu0, wgu1, wgu2], wd=[wd0, wd1], hbuf=hbuf,
                        gs=[gs0, gs1], us=[us0, us1],
                        p_g=[p_g0, p_g1], p_u=[p_u0, p_u1], p_d=[p_d0, p_d1],
                        wl2=wl2, wld=wld)
            emit_ffn(nc, block, sems, gated, TOK, SP2,
                     dict(guw=guw, dw=dw), False,
                     None, dacc, False, bufs, hoist=True)
            _barrier(block, ph, ["tensor", "vector", "scalar", "sync", "gpsimd"])

            with contextlib.ExitStack() as st3:
                E3 = st3.enter_context
                x1r = E3(nc.sbuf_tensor("x1r", [128, 512], F32))
                ev5_done = {}
                st5_done = {}
                for fc in range(NFC):
                    for sp, (t0, tw) in enumerate(SP2):
                        idx = fc * 2 + sp
                        j = 0

                        @block.sync
                        def _(sy, fc=fc, t0=t0, j=j, idx=idx):
                            if idx - 1 in st5_done:
                                sy.wait_ge(st5[0].h, st5_done[idx - 1])
                            sy.dma_start(out=x1r[:, 0:512],
                                         in_=x1i[:, fc * TOK + t0:fc * TOK + t0 + 512]
                                         ).then_inc(xl2[j].dma(), 16)
                        ldv = xl2[j].n

                        @block.vector
                        def _(v, fc=fc, t0=t0, j=j, ldv=ldv, idx=idx):
                            v.wait_ge(xl2[j].h, ldv)
                            v.tensor_tensor(x1r[:, 0:512],
                                            x1r[:, 0:512],
                                            dacc[:, fc * TOK + t0:fc * TOK + t0 + 512],
                                            OP.add).then_inc(s_ev.inc(), 1)
                        ev5_done[idx] = s_ev.n
                        ev5 = s_ev.n

                        @block.sync
                        def _(sy, fc=fc, t0=t0, j=j, ev5=ev5):
                            sy.wait_ge(s_ev.h, ev5)
                            sy.dma_start(out=outo[:, fc * TOK + t0:fc * TOK + t0 + 512],
                                         in_=x1r[:, 0:512]
                                         ).then_inc(st5[j].dma(), 16)
                        st5_done[idx] = st5[j].n

                @block.sync
                def _(sy):
                    for cc_ in st5:
                        sy.wait_ge(cc_.h, cc_.n)
        stG.close()

    _PROG_CACHE["p4"] = nc
    return nc


def p4_in_maps(inp, p2res, p3res):
    srep = np.zeros((8192, 2048), np.float32)
    alexp = np.zeros((64, 2048), np.float32)
    for c in range(NCORE):
        so_ = p3res[c]["so"]
        al_ = p3res[c]["alo"]
        for cc in range(8):
            gc = c * 8 + cc
            for h_ in range(NHM):
                fc, hh = h_ // 2, h_ % 2
                col = ((fc * 8 + cc) * 2 + hh) * 64
                srep[gc * 128:gc * 128 + 64, h_ * 64:(h_ + 1) * 64] = so_[:, col:col + 64]
                srep[gc * 128 + 64:(gc + 1) * 128, h_ * 64:(h_ + 1) * 64] = so_[:, col:col + 64]
            alexp[gc] = np.repeat(al_[:, cc], 64)
    shared = {
        "srep": srep, "alexp": alexp,
        "opw": _pack_lhsT_icmajor(np.asarray(inp["L1_attn_out_proj_w"])),
        "n2w": _pack_rows(np.asarray(inp["L1_attn_norm2_w"])),
        "zb": _pack_rows(np.asarray(inp["L1_attn_z_bias"])),
        "idb": np.eye(128, dtype=np.float32).astype(BFNP),
        "guw": _pack_guw(np.asarray(inp["L1_attn_gate_w"]), np.asarray(inp["L1_attn_up_w"])),
        "dw": np.ascontiguousarray(
            np.asarray(inp["L1_attn_down_w"]).reshape(16, 128, 2, 32, 128)
            .transpose(4, 2, 0, 3, 1).reshape(128, 131072)),
    }
    maps = []
    for c in range(NCORE):
        m = dict(shared)
        own = np.zeros((128, 64), np.float32)
        own[:, c * 8:(c + 1) * 8] = 1.0
        m["ownsel"] = own
        m["y1i"] = p3res[c]["y1o"]
        m["ci"] = p3res[c]["co"]
        m["zi"] = p3res[c]["zo"]
        m["ecumi"] = p3res[c]["ecumo"]
        m["xn1i"] = p2res[c]["xn1o"]
        m["x1i"] = p2res[c]["x1o"]
        m["qacci"] = p3res[c]["qacc"]
        maps.append(m)
    return maps

def run_p4(inp, p2res, p3res):
    nc = build_p4()
    maps = p4_in_maps(inp, p2res, p3res)
    res = run_bass_kernel_spmd(nc, maps, core_ids=list(range(NCORE)))
    return res.results


def kernel(**inputs):
    inp = {k: np.asarray(v) for k, v in inputs.items()}
    p1 = run_p1(inp)
    p2 = run_p2(inp, p1)
    p3 = run_p3(inp, p2)
    p4 = run_p4(inp, p2, p3)
    out = np.zeros((1, S, H), np.float32)
    for c in range(NCORE):
        out[0, c * TOK:(c + 1) * TOK, :] = _unfm(p4[c]["outo"], TOK)
    return out


# version 06
# speedup vs baseline: 1.4542x; 1.0603x over previous
import contextlib
import numpy as np
import ml_dtypes
import concourse.bass as bass
import concourse.mybir as mybir
from concourse.bass_utils import run_bass_kernel_spmd

F32 = mybir.dt.float32
BF16 = mybir.dt.bfloat16
I32 = mybir.dt.int32
AF = mybir.ActivationFunctionType
OP = mybir.AluOpType
AX = mybir.AxisListType
BFNP = ml_dtypes.bfloat16

NCORE = 8
S = 8192
H = 2048
TOK = S // NCORE
TE = TOK + 3
NFC = H // 128
LAT = 512
NH = 16
I_FFN = 8192
NIC = I_FFN // 128
NHM = 32
DST = 64
HDM = 64
CHUNK = 128
NCH = TOK // CHUNK
GCH = S // CHUNK
CONVD = 6144
EPS = 1e-5
RSQ = float(1.0 / np.sqrt(128.0))
SP2 = [(0, 512), (512, 512)]
SP3 = [(0, 512), (512, 512), (1024, 3)]


def _pack_lhsT(W):
    M, K = W.shape
    nk = K // 128
    return np.ascontiguousarray(
        W.T.reshape(nk, 128, M).transpose(1, 0, 2).reshape(128, nk * M))

def _pack_lhsT_icmajor(W):
    I_, K = W.shape
    nic, nkc = I_ // 128, K // 128
    w4 = W.reshape(nic, 128, nkc, 128)
    return np.ascontiguousarray(w4.transpose(3, 0, 2, 1).reshape(128, nic * nkc * 128))

def _pack_rows(b):
    M = b.shape[0]
    return np.ascontiguousarray(b.reshape(M // 128, 128).T)

def _fm(x):
    T = x.shape[0]
    return np.ascontiguousarray(
        x.T.reshape(NFC, 128, T).transpose(1, 0, 2).reshape(128, NFC * T))

def _unfm(img, T):
    return np.ascontiguousarray(
        img.reshape(128, NFC, T).transpose(1, 0, 2).reshape(H, T).T)


class Ctr:
    def __init__(self, stack, nc, name):
        self.h = stack.enter_context(nc.semaphore(name))
        self.n = 0
    def inc(self, k=1):
        self.n += k
        return self.h
    def dma(self):
        return self.inc(16)


def _barrier(block, ctr, engines):
    val = ctr.n
    for name in engines:
        sec = getattr(block, name)
        @sec
        def _(e, _val=val, _h=ctr.h):
            e.wait_ge(_h, _val)



def emit_ln(nc, block, sems, src_f32, dst_bf, T, spans, tmps, psums, aff=None):
    s_mm, s_ev, s_ac, ph = sems["mm"], sems["ev"], sems["ac"], sems["ph"]
    x0b = src_f32
    sqb = tmps["sqb"]
    rows = tmps["rows"]
    m_row, v_row, r_row = rows[:, 0:T], rows[:, T:2 * T], rows[:, 2 * T:3 * T]
    onesb, onesf = tmps["onesb"], tmps["onesf"]
    p_s, p_q, p_m, p_r = psums

    @block.scalar
    def _(sc):
        sc.activation(sqb[:], x0b[:], AF.Square).then_inc(s_ac.inc(), 1)
    ac_sq = s_ac.n

    @block.tensor
    def _(t):
        t.wait_ge(s_ac.h, ac_sq)
        last = None
        for (t0, tw) in spans:
            for fc in range(NFC):
                last = t.matmul(p_s[:, t0:t0 + tw], onesb[:],
                                x0b[:, fc * T + t0:fc * T + t0 + tw],
                                start=(fc == 0), stop=(fc == NFC - 1))
            for fc in range(NFC):
                last = t.matmul(p_q[:, t0:t0 + tw], onesb[:],
                                sqb[:, fc * T + t0:fc * T + t0 + tw],
                                start=(fc == 0), stop=(fc == NFC - 1))
        last.then_inc(s_mm.inc(), 1)
    mm_stats = s_mm.n

    @block.vector
    def _(v):
        v.wait_ge(s_mm.h, mm_stats)
        v.tensor_scalar(m_row, p_s[0:1, 0:T], 1.0 / H, None, OP.mult)
        v.drain()
        v.tensor_tensor(v_row, m_row, m_row, OP.mult)
        v.drain()
        v.scalar_tensor_tensor(v_row, p_q[0:1, 0:T], 1.0 / H, v_row, OP.mult, OP.subtract)
        v.drain().then_inc(s_ev.inc(), 1)
    ev_var = s_ev.n

    @block.scalar
    def _(sc):
        sc.wait_ge(s_ev.h, ev_var)
        sc.activation(v_row, v_row, AF.Ln, bias=tmps["eps_col"][0:1, :])
        sc.drain()
        sc.activation(r_row, v_row, AF.Exp, scale=-0.5)
        sc.drain().then_inc(s_ac.inc(), 1)
    ac_rstd = s_ac.n

    for (t0, tw) in spans:
        @block.tensor
        def _(t, t0=t0, tw=tw):
            t.wait_ge(s_ac.h, ac_rstd)
            if t0 > 0:
                t.wait_ge(s_ev.h, s_ev.n)
            t.matmul(p_m[:, 0:tw], onesf[0:1, :], m_row[0:1, t0:t0 + tw],
                     start=True, stop=True)
            t.matmul(p_r[:, 0:tw], onesf[0:1, :], r_row[0:1, t0:t0 + tw],
                     start=True, stop=True).then_inc(s_mm.inc(), 1)
        mm_bc = s_mm.n

        @block.vector
        def _(v, t0=t0, tw=tw, mm_bc=mm_bc):
            v.wait_ge(s_mm.h, mm_bc)
            for fc in range(NFC):
                tf = tmps["tf"][fc % 2]
                sl = slice(fc * T + t0, fc * T + t0 + tw)
                v.tensor_tensor(tf[:, 0:tw], x0b[:, sl], p_m[:, 0:tw], OP.subtract)
                v.drain()
                if aff is None:
                    v.tensor_tensor(dst_bf[:, sl], tf[:, 0:tw], p_r[:, 0:tw], OP.mult)
                else:
                    v.scalar_tensor_tensor(dst_bf[:, sl], tf[:, 0:tw],
                                           aff[:, fc:fc + 1], p_r[:, 0:tw],
                                           OP.mult, OP.mult)
                v.drain()
            v.drain().then_inc(s_ev.inc(), 1)

    @block.vector
    def _(v):
        v.drain().then_inc(ph.inc(), 1)


def emit_rms(nc, block, sems, src_bf, dst_bf, T, spans, tmps, psums, wrow):
    s_mm, s_ev, s_ac, ph = sems["mm"], sems["ev"], sems["ac"], sems["ph"]
    sqb = tmps["sqb"]
    rows = tmps["rows"]
    v_row, r_row = rows[:, T:2 * T], rows[:, 2 * T:3 * T]
    onesb, onesf = tmps["onesb"], tmps["onesf"]
    p_s, p_q, p_m, p_r = psums

    @block.scalar
    def _(sc):
        sc.activation(sqb[:], src_bf[:], AF.Square).then_inc(s_ac.inc(), 1)
    ac_sq = s_ac.n

    @block.tensor
    def _(t):
        t.wait_ge(s_ac.h, ac_sq)
        last = None
        for (t0, tw) in spans:
            for fc in range(NFC):
                last = t.matmul(p_q[:, t0:t0 + tw], onesb[:],
                                sqb[:, fc * T + t0:fc * T + t0 + tw],
                                start=(fc == 0), stop=(fc == NFC - 1))
        last.then_inc(s_mm.inc(), 1)
    mm_stats = s_mm.n

    @block.vector
    def _(v):
        v.wait_ge(s_mm.h, mm_stats)
        v.tensor_scalar(v_row, p_q[0:1, 0:T], 1.0 / H, None, OP.mult)
        v.drain().then_inc(s_ev.inc(), 1)
    ev_var = s_ev.n

    @block.scalar
    def _(sc):
        sc.wait_ge(s_ev.h, ev_var)
        sc.activation(v_row, v_row, AF.Ln, bias=tmps["eps_col"][0:1, :])
        sc.drain()
        sc.activation(r_row, v_row, AF.Exp, scale=-0.5)
        sc.drain().then_inc(s_ac.inc(), 1)
    ac_rstd = s_ac.n

    for (t0, tw) in spans:
        @block.tensor
        def _(t, t0=t0, tw=tw):
            t.wait_ge(s_ac.h, ac_rstd)
            if t0 > 0:
                t.wait_ge(s_ev.h, s_ev.n)
            t.matmul(p_r[:, 0:tw], onesf[0:1, :], r_row[0:1, t0:t0 + tw],
                     start=True, stop=True).then_inc(s_mm.inc(), 1)
        mm_bc = s_mm.n

        @block.vector
        def _(v, t0=t0, tw=tw, mm_bc=mm_bc):
            v.wait_ge(s_mm.h, mm_bc)
            for fc in range(NFC):
                sl = slice(fc * T + t0, fc * T + t0 + tw)
                v.scalar_tensor_tensor(dst_bf[:, sl], src_bf[:, sl],
                                       wrow[:, fc:fc + 1], p_r[:, 0:tw],
                                       OP.mult, OP.mult)
                v.drain()
            v.drain().then_inc(s_ev.inc(), 1)

    @block.vector
    def _(v):
        v.drain().then_inc(ph.inc(), 1)

def emit_ffn(nc, block, sems, src_bf, T, spans, dram, quant, scales, dacc, dacc_init, bufs,
             dacc_T=None, dacc_off=0, hoist=False):
    s_wl, s_mm, s_ev, s_ac, s_fr, ph = (sems["wl"], sems["mm"], sems["ev"],
                                        sems["ac"], sems["fr"], sems["ph"])
    wl2 = bufs["wl2"]
    wld = bufs["wld"]
    wgu = bufs["wgu"]
    gsc, usc, dsc = scales if scales else (None, None, None)
    wd = bufs["wd"]
    hbuf, gs, us = bufs["hbuf"], bufs["gs"], bufs["us"]
    p_g, p_u, p_d = bufs["p_g"], bufs["p_u"], bufs["p_d"]
    NSP = len(spans)

    wl0 = s_wl.n
    ev_sub = {}
    ev_h = {}
    ac_evac = {}
    fr_cnt = {}

    NWGU = len(wgu)
    wl_cnt = {}

    def emit_ld(ic):
        bi = ic % NWGU

        @block.gpsimd
        def _(g, ic=ic, bi=bi):
            if ic >= NWGU:
                g.wait_ge(s_mm.h, fr_cnt[ic - NWGU])
            g.dma_start(out=wgu[bi][:], in_=dram["guw"][:, ic * 4096:(ic + 1) * 4096]
                        ).then_inc(wl2[bi].dma(), 16)
        wl_cnt[ic] = wl2[bi].n

        if quant:
            @block.vector
            def _(v, bi=bi, wl_ic=wl_cnt[ic]):
                v.wait_ge(wl2[bi].h, wl_ic)
                v.tensor_scalar(wgu[bi][:], wgu[bi][:], -128.0, None, OP.add
                                ).then_inc(s_ev.inc(), 1)
            ev_sub[ic] = s_ev.n

    if hoist:
        emit_ld(0)
    for grp in range(2):
        for icg in range(32):
            ic = grp * 32 + icg
            bi = ic % NWGU
            if hoist:
                if ic + 1 < 64:
                    emit_ld(ic + 1)
            else:
                emit_ld(ic)
            wl_ic = wl_cnt[ic]

            @block.tensor
            def _(t, ic=ic, bi=bi, wl_ic=wl_ic):
                if quant:
                    t.wait_ge(s_ev.h, ev_sub[ic])
                else:
                    t.wait_ge(wl2[bi].h, wl_ic)
                for sp, (t0, tw) in enumerate(spans):
                    if (ic - 1, sp) in ac_evac:
                        t.wait_ge(s_ac.h, ac_evac[(ic - 1, sp)])
                    for kc in range(NFC):
                        t.matmul(p_g[sp][:, 0:tw], wgu[bi][:, kc * 128:(kc + 1) * 128],
                                 src_bf[:, kc * T + t0:kc * T + t0 + tw],
                                 start=(kc == 0), stop=(kc == NFC - 1))
                    for kc in range(NFC):
                        last = t.matmul(p_u[sp][:, 0:tw], wgu[bi][:, 2048 + kc * 128:2048 + (kc + 1) * 128],
                                        src_bf[:, kc * T + t0:kc * T + t0 + tw],
                                        start=(kc == 0), stop=(kc == NFC - 1))
                    last.then_inc(s_mm.inc(), 1)
            fr_cnt[ic] = s_mm.n
            mm_ic = s_mm.n

            for sp, (t0, tw) in enumerate(spans):
                mm_sp = mm_ic - (NSP - 1 - sp)

                @block.scalar
                def _(sc, ic=ic, sp=sp, t0=t0, tw=tw, mm_sp=mm_sp):
                    sc.wait_ge(s_mm.h, mm_sp)
                    if (ic - 1, sp) in ev_h:
                        sc.wait_ge(s_ev.h, ev_h[(ic - 1, sp)])
                    if gsc is not None:
                        sc.activation(gs[sp][:, 0:tw], p_g[sp][:, 0:tw], AF.Silu,
                                      scale=gsc[:, ic:ic + 1])
                        sc.activation(us[sp][:, 0:tw], p_u[sp][:, 0:tw], AF.Copy,
                                      scale=usc[:, ic:ic + 1]).then_inc(s_ac.inc(), 1)
                    else:
                        sc.activation(gs[sp][:, 0:tw], p_g[sp][:, 0:tw], AF.Silu)
                        sc.activation(us[sp][:, 0:tw], p_u[sp][:, 0:tw], AF.Copy
                                      ).then_inc(s_ac.inc(), 1)
                ac_evac[(ic, sp)] = s_ac.n

                @block.vector
                def _(v, ic=ic, sp=sp, t0=t0, tw=tw):
                    v.wait_ge(s_ac.h, ac_evac[(ic, sp)])
                    if ic == 32 and sp == 0 and "hbuf_free" in bufs:
                        v.wait_ge(s_mm.h, bufs["hbuf_free"])
                    icg_ = ic % 32
                    v.tensor_tensor(hbuf[:, icg_ * T + t0:icg_ * T + t0 + tw],
                                    gs[sp][:, 0:tw], us[sp][:, 0:tw], OP.mult
                                    ).then_inc(s_ev.inc(), 1)
                ev_h[(ic, sp)] = s_ev.n

        ev_group = s_ev.n
        ev_dfree = {}
        if grp == 0:
            fr_d = {}
            bufs["fr_d"] = fr_d
        else:
            fr_d = bufs["fr_d"]
        for oc in range(16):
            bo = oc % 2
            gidx = grp * 16 + oc

            @block.gpsimd
            def _(g, grp=grp, oc=oc, bo=bo, gidx=gidx):
                if gidx >= 2:
                    g.wait_ge(s_mm.h, fr_d[gidx - 2])
                g.dma_start(out=wd[bo][:],
                            in_=dram["dw"][:, (grp * 16 + oc) * 4096:(grp * 16 + oc + 1) * 4096]
                            ).then_inc(wld[bo].dma(), 16)
            wl_oc = wld[bo].n

            if quant:
                @block.vector
                def _(v, bo=bo, wl_oc=wl_oc):
                    v.wait_ge(wld[bo].h, wl_oc)
                    v.tensor_scalar(wd[bo][:], wd[bo][:], -128.0, None, OP.add
                                    ).then_inc(s_ev.inc(), 1)
                ev_dw = s_ev.n
            else:
                ev_dw = None

            @block.tensor
            def _(t, oc=oc, bo=bo, ev_dw=ev_dw, wl_oc=wl_oc):
                if ev_dw is not None:
                    t.wait_ge(s_ev.h, ev_dw)
                else:
                    t.wait_ge(wld[bo].h, wl_oc)
                t.wait_ge(s_ev.h, ev_group)
                for sp, (t0, tw) in enumerate(spans):
                    if (oc - 1, sp) in ev_dfree:
                        t.wait_ge(s_ev.h, ev_dfree[(oc - 1, sp)])
                    for icg in range(32):
                        last = t.matmul(p_d[sp][:, 0:tw], wd[bo][:, icg * 128:(icg + 1) * 128],
                                        hbuf[:, icg * T + t0:icg * T + t0 + tw],
                                        start=(icg == 0), stop=(icg == 31))
                    last.then_inc(s_mm.inc(), 1)
            fr_d[gidx] = s_mm.n
            mm_oc = s_mm.n

            for sp, (t0, tw) in enumerate(spans):
                mm_sp = mm_oc - (NSP - 1 - sp)

                @block.vector
                def _(v, grp=grp, oc=oc, sp=sp, t0=t0, tw=tw, mm_sp=mm_sp):
                    v.wait_ge(s_mm.h, mm_sp)
                    dT = dacc_T if dacc_T is not None else T
                    dsl = dacc[:, oc * dT + t0 - dacc_off:oc * dT + t0 - dacc_off + tw]
                    first = dacc_init and grp == 0
                    if dsc is not None:
                        if first:
                            v.tensor_scalar(dsl, p_d[sp][:, 0:tw], dsc[:, oc:oc + 1],
                                            None, OP.mult).then_inc(s_ev.inc(), 1)
                        else:
                            v.scalar_tensor_tensor(dsl, p_d[sp][:, 0:tw],
                                                   dsc[:, oc:oc + 1], dsl,
                                                   OP.mult, OP.add).then_inc(s_ev.inc(), 1)
                    else:
                        if first:
                            v.tensor_copy(dsl, p_d[sp][:, 0:tw]).then_inc(s_ev.inc(), 1)
                        else:
                            v.tensor_tensor(dsl, p_d[sp][:, 0:tw], dsl, OP.add
                                            ).then_inc(s_ev.inc(), 1)
                ev_dfree[(oc, sp)] = s_ev.n
        if grp == 0:
            bufs["hbuf_free"] = s_mm.n

    @block.vector
    def _(v):
        v.drain().then_inc(ph.inc(), 1)


def _pack_guw(gate_w, up_w):
    g = _pack_lhsT_icmajor(gate_w)
    u = _pack_lhsT_icmajor(up_w)
    out = np.empty((128, NIC * 2 * H), g.dtype)
    for ic in range(NIC):
        out[:, ic * 4096:ic * 4096 + 2048] = g[:, ic * 2048:(ic + 1) * 2048]
        out[:, ic * 4096 + 2048:(ic + 1) * 4096] = u[:, ic * 2048:(ic + 1) * 2048]
    return np.ascontiguousarray(out)


_PROG_CACHE = {}

def build_p1():
    if "p1" in _PROG_CACHE:
        return _PROG_CACHE["p1"]
    nc = bass.Bass("TRN2", target_bir_lowering=False)
    x0 = nc.dram_tensor("x0", [128, NFC * TOK], F32, kind="ExternalInput").ap()
    wqd = nc.dram_tensor("wqd", [128, 16 * LAT], F32, kind="ExternalInput").ap()
    wkvd = nc.dram_tensor("wkvd", [128, 16 * LAT], F32, kind="ExternalInput").ap()
    wqu = nc.dram_tensor("wqu", [128, 4 * H], F32, kind="ExternalInput").ap()
    wkn = nc.dram_tensor("wkn", [128, 4 * 1024], F32, kind="ExternalInput").ap()
    wkr = nc.dram_tensor("wkr", [128, 4 * 1024], F32, kind="ExternalInput").ap()
    wvu = nc.dram_tensor("wvu", [128, 4 * H], F32, kind="ExternalInput").ap()
    bqd = nc.dram_tensor("bqd", [128, 4], F32, kind="ExternalInput").ap()
    bkvd = nc.dram_tensor("bkvd", [128, 4], F32, kind="ExternalInput").ap()
    bqu = nc.dram_tensor("bqu", [128, 16], F32, kind="ExternalInput").ap()
    bkn = nc.dram_tensor("bkn", [128, 8], F32, kind="ExternalInput").ap()
    bkr = nc.dram_tensor("bkr", [128, 8], F32, kind="ExternalInput").ap()
    bvu_bc = nc.dram_tensor("bvu_bc", [128, H], F32, kind="ExternalInput").ap()
    cosb = nc.dram_tensor("cosb", [128, 4 * TOK], BF16, kind="ExternalInput").ap()
    sinb = nc.dram_tensor("sinb", [128, 4 * TOK], BF16, kind="ExternalInput").ap()
    guw = nc.dram_tensor("guw", [128, NIC * 2 * H], I32, kind="ExternalInput").ap()
    dw = nc.dram_tensor("dw", [128, 32 * 4096], I32, kind="ExternalInput").ap()
    gsc = nc.dram_tensor("gsc", [128, NIC], F32, kind="ExternalInput").ap()
    usc = nc.dram_tensor("usc", [128, NIC], F32, kind="ExternalInput").ap()
    dsc = nc.dram_tensor("dsc", [128, 16], F32, kind="ExternalInput").ap()

    k_out = nc.dram_tensor("k_out", [128, NH * TOK], BF16, kind="ExternalOutput").ap()
    v_out = nc.dram_tensor("v_out", [128, 8 * H], BF16, kind="ExternalOutput").ap()
    q_out = nc.dram_tensor("q_out", [128, NH * TOK], BF16, kind="ExternalOutput").ap()
    acc0_out = nc.dram_tensor("acc0_out", [128, NFC * TOK], F32, kind="ExternalOutput").ap()
    norm_out = nc.dram_tensor("norm_out", [1, 32], F32, kind="ExternalOutput").ap()

    with contextlib.ExitStack() as stack:
        E = stack.enter_context
        xn = E(nc.sbuf_tensor("xn", [128, NFC * TOK], BF16))
        onesb = E(nc.sbuf_tensor("onesb", [128, 1], BF16))
        onesf = E(nc.sbuf_tensor("onesf", [1, 128], F32))
        norm_acc = E(nc.sbuf_tensor("norm_acc", [1, 32], F32))
        eps_col = E(nc.sbuf_tensor("eps_col", [128, 1], F32))
        rmax = E(nc.sbuf_tensor("rmax", [1, 2], F32))
        s_ld = Ctr(stack, nc, "s_ld")
        s_wl = Ctr(stack, nc, "s_wl")
        s_mm = Ctr(stack, nc, "s_mm")
        s_ev = Ctr(stack, nc, "s_ev")
        s_ac = Ctr(stack, nc, "s_ac")
        s_st = Ctr(stack, nc, "s_st")
        s_fr = Ctr(stack, nc, "s_fr")
        ph = Ctr(stack, nc, "ph")
        st4 = [Ctr(stack, nc, f"st4_{i}") for i in range(4)]
        wl2 = [Ctr(stack, nc, f"wl2_{i}") for i in range(3)]
        wld = [Ctr(stack, nc, f"wld_{i}") for i in range(2)]
        ld5 = [Ctr(stack, nc, f"ld5_{i}") for i in range(2)]
        st5 = [Ctr(stack, nc, f"st5_{i}") for i in range(2)]
        sems = dict(ld=s_ld, wl=s_wl, mm=s_mm, ev=s_ev, ac=s_ac, st=s_st,
                    fr=s_fr, ph=ph)
        block = E(nc.Block())

        with contextlib.ExitStack() as st1:
            E1 = st1.enter_context
            qlat = E1(nc.sbuf_tensor("qlat", [128, 4 * TOK], BF16))
            kvlat = E1(nc.sbuf_tensor("kvlat", [128, 4 * TOK], BF16))
            rows = E1(nc.sbuf_tensor("rows", [1, 3 * TOK], F32))
            wqds = E1(nc.sbuf_tensor("wqds", [128, 16 * LAT], BF16))
            wkvds = E1(nc.sbuf_tensor("wkvds", [128, 16 * LAT], BF16))
            wqus = E1(nc.sbuf_tensor("wqus", [128, 4 * H], BF16))
            biasq = E1(nc.sbuf_tensor("biasq", [128, 40], F32))
            coss = E1(nc.sbuf_tensor("coss", [128, 4 * TOK], BF16))
            sins = E1(nc.sbuf_tensor("sins", [128, 4 * TOK], BF16))
            tf0 = E1(nc.sbuf_tensor("tf0", [128, 512], F32))
            tf1 = E1(nc.sbuf_tensor("tf1", [128, 512], F32))
            krp = E1(nc.sbuf_tensor("krp", [128, 8 * 512], BF16))
            evb = E1(nc.sbuf_tensor("evb", [128, 4 * 512], BF16))
            sqk = E1(nc.sbuf_tensor("sqk", [128, 512], BF16))
            p_s = E1(nc.psum_tensor("p_s", [1, TOK], F32))
            p_q = E1(nc.psum_tensor("p_q", [1, TOK], F32))
            p_m = E1(nc.psum_tensor("p_m", [128, 512], F32))
            p_r = E1(nc.psum_tensor("p_r", [128, 512], F32))
            p_n = E1(nc.psum_tensor("p_n", [1, 512], F32))

            bq_d = biasq[:, 0:4]
            bkv_d = biasq[:, 4:8]
            bqu_d = biasq[:, 8:24]
            bkn_d = biasq[:, 24:32]
            bkr_d = biasq[:, 32:40]

            ln_stack = contextlib.ExitStack()
            x0b = ln_stack.enter_context(nc.sbuf_tensor("x0b", [128, NFC * TOK], BF16))
            sqb = ln_stack.enter_context(nc.sbuf_tensor("sqb", [128, NFC * TOK], BF16))

            @block.sync
            def _(sy):
                for dst, src in ((biasq[:, 0:4], bqd), (biasq[:, 4:8], bkvd),
                                 (biasq[:, 8:24], bqu), (biasq[:, 24:32], bkn),
                                 (biasq[:, 32:40], bkr)):
                    sy.dma_start(out=dst, in_=src[:, :]).then_inc(s_ld.dma(), 16)
                sy.dma_start(out=coss[:], in_=cosb[:, :]).then_inc(s_ld.dma(), 16)
                sy.dma_start(out=sins[:], in_=sinb[:, :]).then_inc(s_ld.dma(), 16)
            LD0 = s_ld.n

            @block.gpsimd
            def _(g):
                g.dma_start(out=x0b[:], in_=x0[:, :]).then_inc(s_wl.dma(), 16)
                g.dma_start(out=wqds[:], in_=wqd[:, :]).then_inc(s_wl.dma(), 16)
                g.dma_start(out=wkvds[:], in_=wkvd[:, :]).then_inc(s_wl.dma(), 16)
                g.dma_start(out=wqus[:], in_=wqu[:, :]).then_inc(s_wl.dma(), 16)
                g.memset(onesb[:], 1.0)
                g.memset(onesf[:], 1.0)
                g.memset(norm_acc[:], 0.0)
                g.memset(eps_col[:], EPS).then_inc(s_fr.inc(), 1)
            WL0 = s_wl.n

            _barrier(block, s_ld, ["vector", "scalar", "tensor"])
            _barrier(block, s_wl, ["vector", "scalar", "tensor"])
            _barrier(block, s_fr, ["vector", "scalar", "tensor"])

            tmps = dict(sqb=sqb, rows=rows, tf=[tf0, tf1],
                        onesb=onesb, onesf=onesf, eps_col=eps_col)
            emit_ln(nc, block, sems, x0b, xn, TOK, SP2, tmps,
                    (p_s, p_q, p_m, p_r))
            _barrier(block, ph, ["tensor", "vector", "scalar"])
            ln_stack.close()
            stW2 = contextlib.ExitStack()
            EW = stW2.enter_context
            wkns = EW(nc.sbuf_tensor("wkns", [128, 4 * 1024], BF16))
            wkrs = EW(nc.sbuf_tensor("wkrs", [128, 4 * 1024], BF16))
            wvus = EW(nc.sbuf_tensor("wvus", [128, 4 * H], BF16))
            bvus = EW(nc.sbuf_tensor("bvus", [128, H], BF16))

            ph_ln = ph.n

            @block.gpsimd
            def _(g):
                g.wait_ge(ph.h, ph_ln)
                g.dma_start(out=wkns[:], in_=wkn[:, :]).then_inc(s_wl.dma(), 16)
                g.dma_start(out=wkrs[:], in_=wkr[:, :]).then_inc(s_wl.dma(), 16)
                g.dma_start(out=wvus[:], in_=wvu[:, :]).then_inc(s_wl.dma(), 16)
                g.dma_start(out=bvus[:], in_=bvu_bc[:, :]).then_inc(s_wl.dma(), 16)
            _barrier(block, s_wl, ["tensor", "vector", "scalar"])

            lat_units = []
            for dstbuf, wimg, brow in ((qlat, wqds, bq_d), (kvlat, wkvds, bkv_d)):
                for mc in range(4):
                    for sp, (t0, tw) in enumerate(SP2):
                        lat_units.append((dstbuf, wimg, brow, mc, t0, tw))
            ac_ev = {}
            for i, (dstbuf, wimg, brow, mc, t0, tw) in enumerate(lat_units):
                pp = (p_m, p_r)[i % 2]

                @block.tensor
                def _(t, i=i, wimg=wimg, mc=mc, t0=t0, tw=tw, pp=pp):
                    if i >= 2:
                        t.wait_ge(s_ac.h, ac_ev[i - 2])
                    for kc in range(NFC):
                        last = t.matmul(pp[:, 0:tw], wimg[:, kc * LAT + mc * 128:kc * LAT + (mc + 1) * 128],
                                        xn[:, kc * TOK + t0:kc * TOK + t0 + tw],
                                        start=(kc == 0), stop=(kc == NFC - 1))
                    last.then_inc(s_mm.inc(), 1)
                mm_i = s_mm.n

                @block.scalar
                def _(sc, i=i, dstbuf=dstbuf, brow=brow, mc=mc, t0=t0, tw=tw,
                      pp=pp, mm_i=mm_i):
                    sc.wait_ge(s_mm.h, mm_i)
                    sc.activation(dstbuf[:, mc * TOK + t0:mc * TOK + t0 + tw],
                                  pp[:, 0:tw], AF.Identity,
                                  bias=brow[:, mc:mc + 1]).then_inc(s_ac.inc(), 1)
                ac_ev[i] = s_ac.n
            _barrier(block, s_ac, ["tensor", "vector", "scalar"])

            slot_cond = {0: [], 1: [], 2: [], 3: []}
            psum_cond = {0: [], 1: []}
            state = {"norm_ev": 0, "u": 0}

            def norm_update(row, hc, src_tile, tw, gate=None):
                @block.scalar
                def _(sc):
                    if gate is not None:
                        sc.wait_ge(gate[0], gate[1])
                    if state.get("norm_mm"):
                        sc.wait_ge(s_mm.h, state["norm_mm"])
                    sc.activation(sqk[:, 0:tw], src_tile, AF.Square
                                  ).then_inc(s_ac.inc(), 1)
                acn = s_ac.n

                @block.tensor
                def _(t, acn=acn, tw=tw):
                    t.wait_ge(s_ac.h, acn)
                    if state["norm_ev"]:
                        t.wait_ge(s_ev.h, state["norm_ev"])
                    t.matmul(p_n[:, 0:tw], onesb[:], sqk[:, 0:tw],
                             start=True, stop=True).then_inc(s_mm.inc(), 1)
                mmn = s_mm.n
                state["norm_mm"] = mmn

                @block.vector
                def _(v, row=row, hc=hc, mmn=mmn, tw=tw):
                    v.wait_ge(s_mm.h, mmn)
                    v.tensor_reduce(rmax[0:1, 0:1], p_n[0:1, 0:tw], axis=AX.X, op=OP.max)
                    v.drain()
                    col = row * 16 + hc
                    v.tensor_tensor(norm_acc[0:1, col:col + 1],
                                    norm_acc[0:1, col:col + 1],
                                    rmax[0:1, 0:1], OP.max)
                    v.drain().then_inc(s_ev.inc(), 1)
                state["norm_ev"] = s_ev.n
                return s_ac.n

            def proj_unit(wimg, wstride, nkc, src, hc, t0, tw, bias_col,
                          dst_slice, scale_post=None, norm_row=None):
                u = state["u"]; state["u"] = u + 1
                par = u % 2
                pp = (p_m, p_r)[par]
                slot = u % 4
                eb = evb[:, slot * 512:slot * 512 + 512]
                pc = list(psum_cond[par])

                @block.tensor
                def _(t, u=u, wimg=wimg, hc=hc, t0=t0, tw=tw, pp=pp, nkc=nkc,
                      src=src, wstride=wstride, pc=pc):
                    for (hh, vv) in pc:
                        t.wait_ge(hh, vv)
                    for kc in range(nkc):
                        last = t.matmul(pp[:, 0:tw],
                                        wimg[:, kc * wstride + hc * 128:kc * wstride + (hc + 1) * 128],
                                        src[:, kc * TOK + t0:kc * TOK + t0 + tw],
                                        start=(kc == 0), stop=(kc == nkc - 1))
                    last.then_inc(s_mm.inc(), 1)
                mmu = s_mm.n

                scond = list(slot_cond[slot])

                @block.scalar
                def _(sc, tw=tw, pp=pp, eb=eb, mmu=mmu, bias_col=bias_col, scond=scond):
                    sc.wait_ge(s_mm.h, mmu)
                    for (hh, vv) in scond:
                        sc.wait_ge(hh, vv)
                    sc.activation(eb[:, 0:tw], pp[:, 0:tw], AF.Identity, bias=bias_col
                                  ).then_inc(s_ac.inc(), 1)
                psum_cond[par] = [(s_ac.h, s_ac.n)]
                gate = (s_ac.h, s_ac.n)

                if scale_post is not None:
                    @block.vector
                    def _(v, tw=tw, eb=eb, g=gate[1]):
                        v.wait_ge(s_ac.h, g)
                        v.tensor_scalar(eb[:, 0:tw], eb[:, 0:tw], scale_post, None, OP.mult
                                        ).then_inc(s_ev.inc(), 1)
                    gate = (s_ev.h, s_ev.n)

                @block.sync
                def _(sy, eb=eb, tw=tw, dst_slice=dst_slice, g=gate, slot=slot):
                    sy.wait_ge(g[0], g[1])
                    sy.dma_start(out=dst_slice, in_=eb[:, 0:tw]).then_inc(st4[slot].dma(), 16)
                cond = [(st4[slot].h, st4[slot].n)]
                if scale_post is not None:
                    cond.append((s_ev.h, gate[1]))

                if norm_row is not None:
                    ac_after = norm_update(norm_row, hc, eb[:, 0:tw], tw, gate=gate)
                    cond.append((s_ac.h, ac_after))
                slot_cond[slot] = cond

            for hc in range(NH):
                for (t0, tw) in SP2:
                    proj_unit(wqus, H, 4, qlat, hc, t0, tw, bqu_d[:, hc:hc + 1],
                              q_out[:, hc * TOK + t0:hc * TOK + t0 + tw],
                              scale_post=RSQ, norm_row=0)
            for hc in range(8):
                for (t0, tw) in SP2:
                    proj_unit(wkns, 1024, 4, kvlat, hc, t0, tw, bkn_d[:, hc:hc + 1],
                              k_out[:, hc * TOK + t0:hc * TOK + t0 + tw],
                              norm_row=1)

            for spi, (t0, tw) in enumerate(SP2):
                krp_ready = []
                for hc in range(8):
                    u = state["u"]; state["u"] = u + 1
                    par = u % 2
                    pp = (p_m, p_r)[par]
                    pc = list(psum_cond[par])

                    @block.tensor
                    def _(t, hc=hc, t0=t0, tw=tw, pp=pp, pc=pc):
                        for (hh, vv) in pc:
                            t.wait_ge(hh, vv)
                        for kc in range(4):
                            last = t.matmul(pp[:, 0:tw],
                                            wkrs[:, kc * 1024 + hc * 128:kc * 1024 + (hc + 1) * 128],
                                            kvlat[:, kc * TOK + t0:kc * TOK + t0 + tw],
                                            start=(kc == 0), stop=(kc == 3))
                        last.then_inc(s_mm.inc(), 1)
                    mmu = s_mm.n

                    @block.scalar
                    def _(sc, hc=hc, tw=tw, pp=pp, mmu=mmu, spi=spi):
                        sc.wait_ge(s_mm.h, mmu)
                        if spi > 0 and hc == 0:
                            sc.wait_ge(s_ev.h, state["rope_ev"])
                        sc.activation(krp[:, hc * 512:hc * 512 + tw], pp[:, 0:tw],
                                      AF.Identity, bias=bkr_d[:, hc:hc + 1]
                                      ).then_inc(s_ac.inc(), 1)
                    psum_cond[par] = [(s_ac.h, s_ac.n)]
                    krp_ready.append(s_ac.n)

                for o in range(4):
                    u = state["u"]; state["u"] = u + 2
                    slot_lo, slot_hi = u % 4, (u + 1) % 4
                    eb_lo = evb[:, slot_lo * 512:slot_lo * 512 + 512]
                    eb_hi = evb[:, slot_hi * 512:slot_hi * 512 + 512]
                    scond = list(slot_cond[slot_lo]) + list(slot_cond[slot_hi])

                    @block.vector
                    def _(v, o=o, t0=t0, tw=tw, eb_lo=eb_lo, eb_hi=eb_hi,
                          scond=scond, kr=krp_ready[-1]):
                        v.wait_ge(s_ac.h, kr)
                        for (hh, vv) in scond:
                            v.wait_ge(hh, vv)
                        cs = coss[:, o * TOK + t0:o * TOK + t0 + tw]
                        sn = sins[:, o * TOK + t0:o * TOK + t0 + tw]
                        x1s = krp[:, o * 512:o * 512 + tw]
                        x2s = krp[:, (o + 4) * 512:(o + 4) * 512 + tw]
                        v.tensor_tensor(tf0[:, 0:tw], x1s, cs, OP.mult)
                        v.tensor_tensor(tf1[:, 0:tw], x2s, sn, OP.mult)
                        v.drain()
                        v.tensor_tensor(eb_lo[:, 0:tw], tf0[:, 0:tw], tf1[:, 0:tw], OP.subtract)
                        v.drain()
                        v.tensor_tensor(tf0[:, 0:tw], x1s, sn, OP.mult)
                        v.tensor_tensor(tf1[:, 0:tw], x2s, cs, OP.mult)
                        v.drain()
                        v.tensor_tensor(eb_hi[:, 0:tw], tf0[:, 0:tw], tf1[:, 0:tw], OP.add)
                        v.drain().then_inc(s_ev.inc(), 1)
                    ev_rope = s_ev.n

                    @block.sync
                    def _(sy, o=o, t0=t0, tw=tw, eb_lo=eb_lo, eb_hi=eb_hi, g=ev_rope,
                          slot_lo=slot_lo, slot_hi=slot_hi):
                        sy.wait_ge(s_ev.h, g)
                        sy.dma_start(out=k_out[:, (8 + o) * TOK + t0:(8 + o) * TOK + t0 + tw],
                                     in_=eb_lo[:, 0:tw]).then_inc(st4[slot_lo].dma(), 16)
                        sy.dma_start(out=k_out[:, (12 + o) * TOK + t0:(12 + o) * TOK + t0 + tw],
                                     in_=eb_hi[:, 0:tw]).then_inc(st4[slot_hi].dma(), 16)
                    st_lo, st_hi = st4[slot_lo].n, st4[slot_hi].n

                    ac_lo = norm_update(1, 8 + o, eb_lo[:, 0:tw], tw, gate=(s_ev.h, ev_rope))
                    ac_hi = norm_update(1, 12 + o, eb_hi[:, 0:tw], tw, gate=(s_ev.h, ev_rope))
                    slot_cond[slot_lo] = [(st4[slot_lo].h, st_lo), (s_ac.h, ac_lo)]
                    slot_cond[slot_hi] = [(st4[slot_hi].h, st_hi), (s_ac.h, ac_hi)]
                state["rope_ev"] = s_ev.n

            for tt in range(8):
                for ds in range(4):
                    u = state["u"]; state["u"] = u + 1
                    par = u % 2
                    pp = (p_m, p_r)[par]
                    slot = u % 4
                    eb = evb[:, slot * 512:slot * 512 + 512]
                    pc = list(psum_cond[par])

                    @block.tensor
                    def _(t, tt=tt, ds=ds, pp=pp, pc=pc):
                        for (hh, vv) in pc:
                            t.wait_ge(hh, vv)
                        for kc in range(4):
                            last = t.matmul(pp[:, 0:512],
                                            kvlat[:, kc * TOK + tt * 128:kc * TOK + (tt + 1) * 128],
                                            wvus[:, kc * H + ds * 512:kc * H + (ds + 1) * 512],
                                            start=(kc == 0), stop=(kc == 3))
                        last.then_inc(s_mm.inc(), 1)
                    mmu = s_mm.n

                    scond = list(slot_cond[slot])

                    @block.vector
                    def _(v, ds=ds, pp=pp, eb=eb, mmu=mmu, scond=scond):
                        v.wait_ge(s_mm.h, mmu)
                        for (hh, vv) in scond:
                            v.wait_ge(hh, vv)
                        v.tensor_tensor(eb[:, 0:512], pp[:, 0:512],
                                        bvus[:, ds * 512:(ds + 1) * 512], OP.add
                                        ).then_inc(s_ev.inc(), 1)
                    psum_cond[par] = [(s_ev.h, s_ev.n)]
                    evv = s_ev.n

                    @block.sync
                    def _(sy, tt=tt, ds=ds, eb=eb, evv=evv, slot=slot):
                        sy.wait_ge(s_ev.h, evv)
                        sy.dma_start(out=v_out[:, tt * H + ds * 512:tt * H + (ds + 1) * 512],
                                     in_=eb[:, 0:512]).then_inc(st4[slot].dma(), 16)
                    slot_cond[slot] = [(st4[slot].h, st4[slot].n), (s_ev.h, evv)]

            @block.sync
            def _(sy):
                sy.wait_ge(s_ev.h, state["norm_ev"])
                sy.dma_start(out=norm_out[:, :], in_=norm_acc[:]).then_inc(s_st.dma(), 16)

            _barrier(block, s_st, ["tensor", "vector", "scalar", "gpsimd"])
            for c4 in st4:
                _barrier(block, c4, ["tensor", "vector", "scalar", "gpsimd", "sync"])
            stW2.close()

        with contextlib.ExitStack() as st2:
            E2 = st2.enter_context
            dacc = E2(nc.sbuf_tensor("dacc", [128, NFC * TOK], F32))
            wgu0 = E2(nc.sbuf_tensor("wgu0", [128, 2 * H], BF16))
            wgu1 = E2(nc.sbuf_tensor("wgu1", [128, 2 * H], BF16))
            wd0 = E2(nc.sbuf_tensor("wd0", [128, 4096], BF16))
            wd1 = E2(nc.sbuf_tensor("wd1", [128, 4096], BF16))
            hbuf = E2(nc.sbuf_tensor("hbuf", [128, 32 * TOK], BF16))
            gs0 = E2(nc.sbuf_tensor("gs0", [128, 512], BF16))
            gs1 = E2(nc.sbuf_tensor("gs1", [128, 512], BF16))
            us0 = E2(nc.sbuf_tensor("us0", [128, 512], BF16))
            us1 = E2(nc.sbuf_tensor("us1", [128, 512], BF16))
            gscs = E2(nc.sbuf_tensor("gscs", [128, NIC], F32))
            uscs = E2(nc.sbuf_tensor("uscs", [128, NIC], F32))
            dscs = E2(nc.sbuf_tensor("dscs", [128, 16], F32))
            p_g0 = E2(nc.psum_tensor("p_g0", [128, 512], F32))
            p_g1 = E2(nc.psum_tensor("p_g1", [128, 512], F32))
            p_u0 = E2(nc.psum_tensor("p_u0", [128, 512], F32))
            p_u1 = E2(nc.psum_tensor("p_u1", [128, 512], F32))
            p_d0 = E2(nc.psum_tensor("p_d0", [128, 512], F32))
            p_d1 = E2(nc.psum_tensor("p_d1", [128, 512], F32))

            @block.sync
            def _(sy):
                sy.dma_start(out=gscs[:], in_=gsc[:, :]).then_inc(s_ld.dma(), 16)
                sy.dma_start(out=uscs[:], in_=usc[:, :]).then_inc(s_ld.dma(), 16)
                sy.dma_start(out=dscs[:], in_=dsc[:, :]).then_inc(s_ld.dma(), 16)
            _barrier(block, s_ld, ["scalar", "vector"])

            bufs = dict(wg=None, wu=None, wgu=[wgu0, wgu1], wd=[wd0, wd1], hbuf=hbuf,
                        gs=[gs0, gs1], us=[us0, us1],
                        p_g=[p_g0, p_g1], p_u=[p_u0, p_u1], p_d=[p_d0, p_d1],
                        wl2=wl2, wld=wld)
            emit_ffn(nc, block, sems, xn, TOK, SP2,
                     dict(guw=guw, dw=dw), True,
                     (gscs, uscs, dscs), dacc, True, bufs)
            _barrier(block, ph, ["tensor", "vector", "scalar", "sync", "gpsimd"])

            with contextlib.ExitStack() as st3:
                E3 = st3.enter_context
                x0r = E3(nc.sbuf_tensor("x0r", [128, 2 * 512], F32))
                accw = E3(nc.sbuf_tensor("accw", [128, 2 * 512], F32))
                ev5_done = {}
                st5_done = {}
                for fc in range(NFC):
                    for sp, (t0, tw) in enumerate(SP2):
                        idx = fc * 2 + sp
                        j = idx % 2
                        @block.sync
                        def _(sy, fc=fc, t0=t0, tw=tw, j=j, idx=idx):
                            if idx - 2 in ev5_done:
                                sy.wait_ge(s_ev.h, ev5_done[idx - 2])
                            sy.dma_start(out=x0r[:, j * 512:j * 512 + tw],
                                         in_=x0[:, fc * TOK + t0:fc * TOK + t0 + tw]
                                         ).then_inc(ld5[j].dma(), 16)
                        ld5v = ld5[j].n

                        @block.vector
                        def _(v, fc=fc, t0=t0, tw=tw, j=j, ld5v=ld5v, idx=idx):
                            v.wait_ge(ld5[j].h, ld5v)
                            if idx - 2 in st5_done:
                                v.wait_ge(st5[j].h, st5_done[idx - 2])
                            v.tensor_tensor(accw[:, j * 512:j * 512 + tw],
                                            x0r[:, j * 512:j * 512 + tw],
                                            dacc[:, fc * TOK + t0:fc * TOK + t0 + tw],
                                            OP.add).then_inc(s_ev.inc(), 1)
                        ev5_done[idx] = s_ev.n
                        ev5 = s_ev.n

                        @block.sync
                        def _(sy, fc=fc, t0=t0, tw=tw, j=j, ev5=ev5):
                            sy.wait_ge(s_ev.h, ev5)
                            sy.dma_start(out=acc0_out[:, fc * TOK + t0:fc * TOK + t0 + tw],
                                         in_=accw[:, j * 512:j * 512 + tw]
                                         ).then_inc(st5[j].dma(), 16)
                        st5_done[idx] = st5[j].n

                @block.sync
                def _(sy):
                    sy.wait_ge(s_st.h, s_st.n)
                    for c4 in st4 + st5:
                        sy.wait_ge(c4.h, c4.n)

    _PROG_CACHE["p1"] = nc
    return nc


def _fmn(x, nfc):
    T = x.shape[0]
    return np.ascontiguousarray(
        x.T.reshape(nfc, 128, T).transpose(1, 0, 2).reshape(128, nfc * T))

def _rope_tables():
    ROT = 1024
    inv_freq = 1.0 / 10000.0 ** (np.arange(0, ROT, 2, dtype=np.float32) / ROT)
    t = np.arange(S, dtype=np.float32)
    fr = np.outer(t, inv_freq)
    return np.cos(fr).astype(np.float32), np.sin(fr).astype(np.float32)

def p1_in_maps(inp):
    x = np.asarray(inp["L0_attn_hidden_states"])[0]
    cos_t, sin_t = _rope_tables()
    shared = {
        "wqd": _pack_lhsT(np.asarray(inp["L0_attn_Wq_down"])),
        "wkvd": _pack_lhsT(np.asarray(inp["L0_attn_Wkv_down"])),
        "wqu": _pack_lhsT(np.asarray(inp["L0_attn_Wq_up"])),
        "wkn": _pack_lhsT(np.asarray(inp["L0_attn_Wk_nope"])),
        "wkr": _pack_lhsT(np.asarray(inp["L0_attn_Wk_rope"])),
        "wvu": _pack_lhsT(np.asarray(inp["L0_attn_Wv_up"])),
        "bqd": _pack_rows(np.asarray(inp["L0_attn_bq_down"])),
        "bkvd": _pack_rows(np.asarray(inp["L0_attn_bkv_down"])),
        "bqu": _pack_rows(np.asarray(inp["L0_attn_bq_up"])),
        "bkn": _pack_rows(np.asarray(inp["L0_attn_bk_nope"])),
        "bkr": _pack_rows(np.asarray(inp["L0_attn_bk_rope"])),
        "bvu_bc": np.ascontiguousarray(
            np.broadcast_to(np.asarray(inp["L0_attn_bv_up"])[None, :], (128, H))),
        "guw": _pack_guw(np.asarray(inp["L0_ffn_gate_w"]), np.asarray(inp["L0_ffn_up_w"])),
        "dw": np.ascontiguousarray(
            np.asarray(inp["L0_ffn_down_w"]).reshape(16, 128, 2, 32, 128)
            .transpose(4, 2, 0, 3, 1).reshape(128, 131072)),
        "gsc": _pack_rows(np.asarray(inp["L0_ffn_gate_scale"])),
        "usc": _pack_rows(np.asarray(inp["L0_ffn_up_scale"])),
        "dsc": _pack_rows(np.asarray(inp["L0_ffn_down_scale"])),
    }
    maps = []
    for c in range(NCORE):
        sl = slice(c * TOK, (c + 1) * TOK)
        m = dict(shared)
        m["x0"] = _fm(x[sl])
        m["cosb"] = _fmn(cos_t[sl], 4).astype(BFNP)
        m["sinb"] = _fmn(sin_t[sl], 4).astype(BFNP)
        maps.append(m)
    return maps

def run_p1(inp):
    nc = build_p1()
    maps = p1_in_maps(inp)
    res = run_bass_kernel_spmd(nc, maps, core_ids=list(range(NCORE)))
    return res.results


def build_p2():
    if "p2" in _PROG_CACHE:
        return _PROG_CACHE["p2"]
    nc = bass.Bass("TRN2", target_bir_lowering=False)
    qimg = nc.dram_tensor("qimg", [128, NH * TOK], BF16, kind="ExternalInput").ap()
    kimg = nc.dram_tensor("kimg", [128, NH * S], BF16, kind="ExternalInput").ap()
    vimg = nc.dram_tensor("vimg", [128, NH * S], BF16, kind="ExternalInput").ap()
    acc0e = nc.dram_tensor("acc0e", [128, NFC * TOK], F32, kind="ExternalInput").ap()
    normsin = nc.dram_tensor("normsin", [1, 256], F32, kind="ExternalInput").ap()
    wo = nc.dram_tensor("wo", [128, 16 * H], F32, kind="ExternalInput").ap()
    bo = nc.dram_tensor("bo", [128, 16], F32, kind="ExternalInput").ap()
    x1o = nc.dram_tensor("x1o", [128, NFC * TOK], F32, kind="ExternalOutput").ap()
    xn1o = nc.dram_tensor("xn1o", [128, NFC * TOK], BF16, kind="ExternalOutput").ap()

    with contextlib.ExitStack() as stack:
        E = stack.enter_context
        onesb = E(nc.sbuf_tensor("onesb", [128, 1], BF16))
        onesf = E(nc.sbuf_tensor("onesf", [1, 128], F32))
        eps_col = E(nc.sbuf_tensor("eps_col", [128, 1], F32))
        nbbc = E(nc.sbuf_tensor("nbbc", [128, 16], F32))
        o_fm = E(nc.sbuf_tensor("o_fm", [128, NH * TOK], BF16))

        s_ld = Ctr(stack, nc, "s_ld")
        s_mm = Ctr(stack, nc, "s_mm")
        s_ev = Ctr(stack, nc, "s_ev")
        s_ac = Ctr(stack, nc, "s_ac")
        s_st = Ctr(stack, nc, "s_st")
        ph = Ctr(stack, nc, "ph")
        kl2 = [Ctr(stack, nc, f"kl2_{i}") for i in range(2)]
        vl2 = [Ctr(stack, nc, f"vl2_{i}") for i in range(2)]
        ldA = [Ctr(stack, nc, f"ldA_{i}") for i in range(2)]
        wl2 = [Ctr(stack, nc, f"wl2_{i}") for i in range(3)]
        st2 = [Ctr(stack, nc, f"st2_{i}") for i in range(2)]
        sems = dict(ld=s_ld, mm=s_mm, ev=s_ev, ac=s_ac, st=s_st, ph=ph)
        block = E(nc.Block())

        with contextlib.ExitStack() as stA:
            EA = stA.enter_context
            qsb = EA(nc.sbuf_tensor("qsb", [128, NH * TOK], BF16))
            ksb = [EA(nc.sbuf_tensor(f"ksb{i}", [128, S], BF16)) for i in range(2)]
            vsb = [EA(nc.sbuf_tensor(f"vsb{i}", [128, S], BF16)) for i in range(2)]
            pt = [EA(nc.sbuf_tensor(f"pt{i}", [128, 1024], BF16)) for i in range(2)]
            nrm = EA(nc.sbuf_tensor("nrm", [1, 256], F32))
            brow = EA(nc.sbuf_tensor("brow", [1, 64], F32))
            recsb = EA(nc.sbuf_tensor("recsb", [1, 512], F32))
            otmp = EA(nc.sbuf_tensor("otmp", [128, 512], F32))
            psc = [EA(nc.psum_tensor(f"psc{i}", [128, 1024], F32)) for i in range(2)]
            pacc = [EA(nc.psum_tensor(f"pacc{i}", [128, 512], F32)) for i in range(2)]
            pden = [EA(nc.psum_tensor(f"pden{i}", [1, 512], F32)) for i in range(2)]

            @block.sync
            def _(sy):
                sy.dma_start(out=qsb[:], in_=qimg[:, :]).then_inc(s_ld.dma(), 16)
                sy.dma_start(out=nrm[:], in_=normsin[:, :]).then_inc(s_ld.dma(), 16)

            @block.gpsimd
            def _(g):
                g.memset(onesb[:], 1.0)
                g.memset(onesf[:], 1.0)
                g.memset(eps_col[:], EPS).then_inc(s_ev.inc(), 1)
            _barrier(block, s_ld, ["vector", "scalar", "tensor", "gpsimd"])
            _barrier(block, s_ev, ["vector", "scalar", "tensor"])

            @block.vector
            def _(v):
                v.tensor_reduce(brow[0:1, 0:32].rearrange("p (a b) -> p a b", a=32, b=1),
                                nrm[0:1, :].rearrange("p (a b) -> p a b", a=32, b=8),
                                axis=AX.X, op=OP.max)
                v.drain()
                v.tensor_tensor(brow[0:1, 32:48], brow[0:1, 0:16], brow[0:1, 16:32], OP.mult)
                v.drain().then_inc(s_ev.inc(), 1)
            evb0 = s_ev.n

            @block.scalar
            def _(sc):
                sc.wait_ge(s_ev.h, evb0)
                sc.activation(brow[0:1, 32:48], brow[0:1, 32:48], AF.Ln)
                sc.drain()
                sc.activation(brow[0:1, 48:64], brow[0:1, 32:48], AF.Exp, scale=0.5)
                sc.drain().then_inc(s_ac.inc(), 1)
            acb = s_ac.n

            @block.vector
            def _(v):
                v.wait_ge(s_ac.h, acb)
                v.tensor_scalar(brow[0:1, 48:64], brow[0:1, 48:64], -1.0, None, OP.mult)
                v.drain().then_inc(s_ev.inc(), 1)
            evb1 = s_ev.n

            @block.tensor
            def _(t):
                t.wait_ge(s_ev.h, evb1)
                t.matmul(psc[0][:, 0:16], onesf[0:1, :], brow[0:1, 48:64],
                         start=True, stop=True).then_inc(s_mm.inc(), 1)
            mmb = s_mm.n

            @block.vector
            def _(v):
                v.wait_ge(s_mm.h, mmb)
                v.tensor_copy(nbbc[:], psc[0][:, 0:16])
                v.drain().then_inc(s_ev.inc(), 1)
            evb2 = s_ev.n
            _barrier(block, s_ev, ["tensor", "scalar", "gpsimd"])

            kv_free = {}
            acc_free = {}
            for h in range(NH):
                hs = h % 2

                @block.gpsimd
                def _(g, h=h, hs=hs):
                    if h >= 2:
                        g.wait_ge(s_mm.h, kv_free[h - 2])
                    g.dma_start(out=ksb[hs][:], in_=kimg[:, h * S:(h + 1) * S]
                                ).then_inc(kl2[hs].dma(), 16)
                    g.dma_start(out=vsb[hs][:], in_=vimg[:, h * S:(h + 1) * S]
                                ).then_inc(vl2[hs].dma(), 16)
                klv, vlv = kl2[hs].n, vl2[hs].n

                for sp in range(2):
                    t0 = sp * 512
                    ps = pacc[sp % 2]
                    pd = pden[sp % 2]
                    ac0 = s_ac.n
                    exp_done = {kg: ac0 + kg + 1 for kg in range(32)}
                    att_done = {}
                    mm_sc = {}

                    def emit_scores(kg, h=h, hs=hs, sp=sp, t0=t0, klv=klv):
                        pp = psc[kg % 2]

                        @block.tensor
                        def _(t, kg=kg, pp=pp):
                            if kg == 0:
                                t.wait_ge(kl2[hs].h, klv)
                                t.wait_ge(vl2[hs].h, vlv)
                                if (sp % 2) in acc_free:
                                    hh, vv = acc_free[sp % 2]
                                    t.wait_ge(hh, vv)
                                if "psc0" in acc_free:
                                    hh, vv = acc_free["psc0"]
                                    t.wait_ge(hh, vv)
                            if kg - 2 in exp_done and kg - 2 >= 0:
                                t.wait_ge(s_ac.h, exp_done[kg - 2])
                            for j in range(2):
                                kk = kg * 2 + j
                                last = t.matmul(pp[:, j * 512:j * 512 + 512],
                                                ksb[hs][:, kk * 128:(kk + 1) * 128],
                                                qsb[:, h * TOK + t0:h * TOK + t0 + 512],
                                                start=True, stop=True)
                            last.then_inc(s_mm.inc(), 1)
                        mm_sc[kg] = s_mm.n

                    def emit_exp(kg, h=h):
                        pp = psc[kg % 2]

                        @block.scalar
                        def _(sc, kg=kg, pp=pp):
                            sc.wait_ge(s_mm.h, mm_sc[kg])
                            if kg - 2 in att_done:
                                sc.wait_ge(s_mm.h, att_done[kg - 2])
                            sc.activation(pt[kg % 2][:], pp[:], AF.Exp,
                                          bias=nbbc[:, h:h + 1]).then_inc(s_ac.inc(), 1)

                    def emit_attnv(kg, h=h, hs=hs, ps=ps, pd=pd):
                        @block.tensor
                        def _(t, kg=kg, ac_exp=exp_done[kg]):
                            t.wait_ge(s_ac.h, ac_exp)
                            for j in range(2):
                                kk = kg * 2 + j
                                t.matmul(ps[:, 0:512],
                                         vsb[hs][:, kk * 128:(kk + 1) * 128],
                                         pt[kg % 2][:, j * 512:(j + 1) * 512],
                                         start=(kk == 0), stop=(kk == 63))
                                last = t.matmul(pd[:, 0:512],
                                                onesb[:],
                                                pt[kg % 2][:, j * 512:(j + 1) * 512],
                                                start=(kk == 0), stop=(kk == 63))
                            last.then_inc(s_mm.inc(), 1)
                        att_done[kg] = s_mm.n

                    for kg in range(33):
                        if kg < 32:
                            emit_scores(kg)
                        if kg >= 1:
                            emit_attnv(kg - 1)
                    for kg in range(32):
                        emit_exp(kg)
                    assert s_ac.n == ac0 + 32
                    mm_att = s_mm.n
                    if h == NH - 1 and sp == 1:
                        pass
                    kv_free[h] = mm_att

                    @block.vector
                    def _(v, pd=pd, mm_att=mm_att):
                        v.wait_ge(s_mm.h, mm_att)
                        v.reciprocal(recsb[0:1, :], pd[0:1, 0:512])
                        v.drain().then_inc(s_ev.inc(), 1)
                    ev_rec = s_ev.n

                    @block.tensor
                    def _(t, ev_rec=ev_rec):
                        t.wait_ge(s_ev.h, ev_rec)
                        t.matmul(psc[0][:, 0:512], onesf[0:1, :], recsb[0:1, :],
                                 start=True, stop=True).then_inc(s_mm.inc(), 1)
                    mm_bc = s_mm.n

                    @block.vector
                    def _(v, h=h, sp=sp, t0=t0, ps=ps, mm_bc=mm_bc):
                        v.wait_ge(s_mm.h, mm_bc)
                        v.tensor_copy(otmp[:], ps[:, 0:512])
                        v.drain()
                        v.tensor_tensor(o_fm[:, h * TOK + t0:h * TOK + t0 + 512],
                                        otmp[:], psc[0][:, 0:512], OP.mult
                                        ).then_inc(s_ev.inc(), 1)
                    acc_free[sp % 2] = (s_ev.h, s_ev.n)
                    acc_free["psc0"] = (s_ev.h, s_ev.n)

            _barrier(block, s_ev, ["tensor", "scalar", "gpsimd", "sync"])
            _barrier(block, s_mm, ["vector", "scalar", "gpsimd", "sync"])

        with contextlib.ExitStack() as stB:
            EB = stB.enter_context
            x1sb = EB(nc.sbuf_tensor("x1sb", [128, NFC * TOK], F32))
            x1b = EB(nc.sbuf_tensor("x1b", [128, NFC * TOK], BF16))
            wosb = [EB(nc.sbuf_tensor(f"wosb{i}", [128, 2048], BF16)) for i in range(2)]
            bosb = EB(nc.sbuf_tensor("bosb", [128, 16], F32))
            a0sb = [EB(nc.sbuf_tensor(f"a0sb{i}", [128, 512], F32)) for i in range(2)]
            pwo = [EB(nc.psum_tensor(f"pwo{i}", [128, 512], F32)) for i in range(2)]

            @block.sync
            def _(sy):
                sy.dma_start(out=bosb[:], in_=bo[:, :]).then_inc(s_ld.dma(), 16)
            _barrier(block, s_ld, ["vector"])

            wo_free = {}
            ev_x1 = {}
            for oc in range(16):
                bi = oc % 2

                @block.gpsimd
                def _(g, oc=oc, bi=bi):
                    if oc >= 2:
                        g.wait_ge(s_mm.h, wo_free[oc - 2])
                    g.dma_start(out=wosb[bi][:], in_=wo[:, oc * 2048:(oc + 1) * 2048]
                                ).then_inc(wl2[bi].dma(), 16)
                wlv = wl2[bi].n

                for sp in range(2):
                    t0 = sp * 512
                    u = oc * 2 + sp
                    j = u % 2

                    @block.sync
                    def _(sy, oc=oc, t0=t0, j=j, u=u):
                        if u - 2 in ev_x1:
                            sy.wait_ge(s_ev.h, ev_x1[u - 2])
                        sy.dma_start(out=a0sb[j][:],
                                     in_=acc0e[:, oc * TOK + t0:oc * TOK + t0 + 512]
                                     ).then_inc(ldA[j].dma(), 16)
                    ldv = ldA[j].n

                    @block.tensor
                    def _(t, oc=oc, bi=bi, sp=sp, t0=t0, j=j, u=u, wlv=wlv):
                        if sp == 0:
                            t.wait_ge(wl2[bi].h, wlv)
                        if u - 2 in ev_x1:
                            t.wait_ge(s_ev.h, ev_x1[u - 2])
                        for kc in range(NFC):
                            last = t.matmul(pwo[j][:, 0:512],
                                            wosb[bi][:, kc * 128:(kc + 1) * 128],
                                            o_fm[:, kc * TOK + t0:kc * TOK + t0 + 512],
                                            start=(kc == 0), stop=(kc == NFC - 1))
                        last.then_inc(s_mm.inc(), 1)
                    mmw = s_mm.n
                    if sp == 1:
                        wo_free[oc] = mmw

                    @block.vector
                    def _(v, oc=oc, t0=t0, j=j, mmw=mmw, ldv=ldv):
                        v.wait_ge(s_mm.h, mmw)
                        v.wait_ge(ldA[j].h, ldv)
                        sl = slice(oc * TOK + t0, oc * TOK + t0 + 512)
                        v.scalar_tensor_tensor(x1sb[:, sl], pwo[j][:, 0:512],
                                               bosb[:, oc:oc + 1], a0sb[j][:],
                                               OP.add, OP.add)
                        v.drain()
                        v.tensor_copy(x1b[:, sl], x1sb[:, sl]).then_inc(s_ev.inc(), 1)
                    ev_x1[u] = s_ev.n
                    evx = s_ev.n

                    @block.sync
                    def _(sy, oc=oc, t0=t0, j=j, evx=evx):
                        sy.wait_ge(s_ev.h, evx)
                        sy.dma_start(out=x1o[:, oc * TOK + t0:oc * TOK + t0 + 512],
                                     in_=x1sb[:, oc * TOK + t0:oc * TOK + t0 + 512]
                                     ).then_inc(st2[j].dma(), 16)

            _barrier(block, s_ev, ["tensor", "scalar", "gpsimd", "sync"])

            with contextlib.ExitStack() as stC:
                EC = stC.enter_context
                sqb2 = o_fm
                xn1sb = EC(nc.sbuf_tensor("xn1sb", [128, NFC * TOK], BF16))
                rows2 = EC(nc.sbuf_tensor("rows2", [1, 3 * TOK], F32))
                tfa = EC(nc.sbuf_tensor("tfa", [128, 512], F32))
                tfb = EC(nc.sbuf_tensor("tfb", [128, 512], F32))
                p_s2 = EC(nc.psum_tensor("p_s2", [1, TOK], F32))
                p_q2 = EC(nc.psum_tensor("p_q2", [1, TOK], F32))
                p_m2 = EC(nc.psum_tensor("p_m2", [128, 512], F32))
                p_r2 = EC(nc.psum_tensor("p_r2", [128, 512], F32))
                tmps = dict(sqb=sqb2, rows=rows2, tf=[tfa, tfb],
                            onesb=onesb, onesf=onesf, eps_col=eps_col)
                emit_ln(nc, block, sems, x1b, xn1sb, TOK, SP2, tmps,
                        (p_s2, p_q2, p_m2, p_r2))
                _barrier(block, ph, ["sync"])

                @block.sync
                def _(sy):
                    sy.dma_start(out=xn1o[:, :], in_=xn1sb[:]).then_inc(s_st.dma(), 16)
                    sy.wait_ge(s_st.h, s_st.n)
                    for cc in st2 + [ldA[0], ldA[1]]:
                        sy.wait_ge(cc.h, cc.n)

    _PROG_CACHE["p2"] = nc
    return nc


def p2_in_maps(inp, p1res):
    K_img = np.zeros((128, NH * S), dtype=BFNP)
    V_img = np.zeros((128, NH * S), dtype=BFNP)
    for c in range(NCORE):
        ko = p1res[c]["k_out"]
        vo = p1res[c]["v_out"]
        for h in range(NH):
            K_img[:, h * S + c * TOK:h * S + (c + 1) * TOK] = ko[:, h * TOK:(h + 1) * TOK]
            for tt in range(8):
                kk = c * 8 + tt
                V_img[:, (h * 64 + kk) * 128:(h * 64 + kk + 1) * 128] = \
                    vo[:, tt * H + h * 128:tt * H + (h + 1) * 128]
    norms = np.zeros((1, 256), np.float32)
    for c in range(NCORE):
        no = p1res[c]["norm_out"][0]
        for k in range(2):
            for h in range(16):
                norms[0, (k * 16 + h) * 8 + c] = no[k * 16 + h]
    shared = {
        "kimg": K_img, "vimg": V_img, "normsin": norms,
        "wo": _pack_lhsT_icmajor(np.asarray(inp["L0_attn_Wo"])),
        "bo": _pack_rows(np.asarray(inp["L0_attn_bo"])),
    }
    maps = []
    for c in range(NCORE):
        m = dict(shared)
        m["qimg"] = p1res[c]["q_out"]
        m["acc0e"] = p1res[c]["acc0_out"]
        maps.append(m)
    return maps

def run_p2(inp, p1res):
    nc = build_p2()
    maps = p2_in_maps(inp, p1res)
    res = run_bass_kernel_spmd(nc, maps, core_ids=list(range(NCORE)))
    return res.results


def build_p3():
    if "p3" in _PROG_CACHE:
        return _PROG_CACHE["p3"]
    nc = bass.Bass("TRN2", target_bir_lowering=False)
    TE_ = TE
    xn1e = nc.dram_tensor("xn1e", [128, NFC * TE_], BF16, kind="ExternalInput").ap()
    n1w_d = nc.dram_tensor("n1w", [128, 16], F32, kind="ExternalInput").ap()
    ipm = nc.dram_tensor("ipm", [128, 64 * H], F32, kind="ExternalInput").ap()
    ipdt = nc.dram_tensor("ipdt", [128, 16 * 32], F32, kind="ExternalInput").ap()
    cvw = nc.dram_tensor("cvw", [128, 48 * 4], F32, kind="ExternalInput").ap()
    cvb = nc.dram_tensor("cvb", [128, 48], F32, kind="ExternalInput").ap()
    lti = nc.dram_tensor("lti", [128, 128], F32, kind="ExternalInput").ap()
    mkt = nc.dram_tensor("mkt", [128, 128], F32, kind="ExternalInput").ap()
    idf = nc.dram_tensor("idf", [128, 128], F32, kind="ExternalInput").ap()
    idb = nc.dram_tensor("idb", [128, 128], BF16, kind="ExternalInput").ap()
    dbc = nc.dram_tensor("dbc", [128, 32], F32, kind="ExternalInput").ap()
    selm = nc.dram_tensor("selm", [32, 4096], F32, kind="ExternalInput").ap()
    guw = nc.dram_tensor("guw", [128, NIC * 2 * H], I32, kind="ExternalInput").ap()
    dw = nc.dram_tensor("dw", [128, 32 * 4096], I32, kind="ExternalInput").ap()
    gsc = nc.dram_tensor("gsc", [128, NIC], F32, kind="ExternalInput").ap()
    usc = nc.dram_tensor("usc", [128, NIC], F32, kind="ExternalInput").ap()
    dsc = nc.dram_tensor("dsc", [128, 16], F32, kind="ExternalInput").ap()

    zo = nc.dram_tensor("zo", [128, NFC * TOK], BF16, kind="ExternalOutput").ap()
    co = nc.dram_tensor("co", [128, NFC * TOK], BF16, kind="ExternalOutput").ap()
    y1o = nc.dram_tensor("y1o", [128, 8 * H], BF16, kind="ExternalOutput").ap()
    ecumo = nc.dram_tensor("ecumo", [128, 8 * 32], F32, kind="ExternalOutput").ap()
    so = nc.dram_tensor("so", [64, 16384], F32, kind="ExternalOutput").ap()
    alo = nc.dram_tensor("alo", [32, 8], F32, kind="ExternalOutput").ap()
    qacc = nc.dram_tensor("qacc", [128, NFC * TOK], F32, kind="ExternalOutput").ap()

    SPE = [(0, 512), (512, 512), (1024, 3)]
    SPO = [(3, 512), (515, 512)]

    with contextlib.ExitStack() as stack:
        E = stack.enter_context
        xn1 = E(nc.sbuf_tensor("xn1", [128, NFC * TE_], BF16))
        onesb = E(nc.sbuf_tensor("onesb", [128, 1], BF16))
        onesf = E(nc.sbuf_tensor("onesf", [1, 128], F32))
        eps_col = E(nc.sbuf_tensor("eps_col", [128, 1], F32))
        stH = contextlib.ExitStack()
        consts = stH.enter_context(nc.sbuf_tensor("consts", [128, 128 * 3 + 32 + 48 * 4 + 48 + 16], F32))
        ltis = consts[:, 0:128]
        mkts = consts[:, 128:256]
        idfs = consts[:, 256:384]
        dbcs = consts[:, 384:416]
        cvws = consts[:, 416:608]
        cvbs = consts[:, 608:656]
        n1ws = consts[:, 656:672]

        s_ld = Ctr(stack, nc, "s_ld")
        s_wl = Ctr(stack, nc, "s_wl")
        s_mm = Ctr(stack, nc, "s_mm")
        s_ev = Ctr(stack, nc, "s_ev")
        s_ac = Ctr(stack, nc, "s_ac")
        s_st = Ctr(stack, nc, "s_st")
        s_fr = Ctr(stack, nc, "s_fr")
        ph = Ctr(stack, nc, "ph")
        ipl = [Ctr(stack, nc, f"ipl{i}") for i in range(3)]
        wl2 = [Ctr(stack, nc, f"wl2_{i}") for i in range(3)]
        wld = [Ctr(stack, nc, f"wld_{i}") for i in range(2)]
        stz = [Ctr(stack, nc, f"stz{i}") for i in range(2)]
        stS = Ctr(stack, nc, "stS")
        stCo = Ctr(stack, nc, "stCo")
        sems = dict(ld=s_ld, wl=s_wl, mm=s_mm, ev=s_ev, ac=s_ac, st=s_st,
                    fr=s_fr, ph=ph)
        block = E(nc.Block())

        @block.sync
        def _(sy):
            sy.dma_start(out=xn1[:], in_=xn1e[:, :]).then_inc(s_ld.dma(), 16)
            sy.dma_start(out=ltis, in_=lti[:, :]).then_inc(s_ld.dma(), 16)
            sy.dma_start(out=mkts, in_=mkt[:, :]).then_inc(s_ld.dma(), 16)
            sy.dma_start(out=idfs, in_=idf[:, :]).then_inc(s_ld.dma(), 16)
            sy.dma_start(out=dbcs, in_=dbc[:, :]).then_inc(s_ld.dma(), 16)
            sy.dma_start(out=cvws, in_=cvw[:, :]).then_inc(s_ld.dma(), 16)
            sy.dma_start(out=cvbs, in_=cvb[:, :]).then_inc(s_ld.dma(), 16)
            sy.dma_start(out=n1ws, in_=n1w_d[:, :]).then_inc(s_ld.dma(), 16)

        @block.gpsimd
        def _(g):
            g.memset(onesb[:], 1.0)
            g.memset(onesf[:], 1.0)
            g.memset(eps_col[:], EPS).then_inc(s_fr.inc(), 1)
        _barrier(block, s_ld, ["vector", "scalar", "tensor", "gpsimd"])
        _barrier(block, s_fr, ["vector", "scalar", "tensor"])

        h1 = stH.enter_context(nc.sbuf_tensor("h1", [128, NFC * TE_], BF16))
        with contextlib.ExitStack() as stB:
            EB = stB.enter_context
            sqb = EB(nc.sbuf_tensor("sqb", [128, NFC * TE_], BF16))
            rows = EB(nc.sbuf_tensor("rows", [1, 3 * TE_], F32))
            tf0 = EB(nc.sbuf_tensor("tf0", [128, 512], F32))
            tf1 = EB(nc.sbuf_tensor("tf1", [128, 512], F32))
            p_s = EB(nc.psum_tensor("p_s", [1, TE_], F32))
            p_q = EB(nc.psum_tensor("p_q", [1, TE_], F32))
            p_m = EB(nc.psum_tensor("p_m", [128, 512], F32))
            p_r = EB(nc.psum_tensor("p_r", [128, 512], F32))

            tmps = dict(sqb=sqb, rows=rows, tf=[tf0, tf1],
                        onesb=onesb, onesf=onesf, eps_col=eps_col)
            emit_rms(nc, block, sems, xn1, h1, TE_, SPE, tmps,
                     (p_s, p_q, p_m, p_r), n1ws)
            _barrier(block, ph, ["tensor", "vector", "scalar", "gpsimd", "sync"])

        with contextlib.ExitStack() as stC:
            EC = stC.enter_context
            selms = EC(nc.sbuf_tensor("selms", [32, 4096], F32))
            identb = EC(nc.sbuf_tensor("identb", [128, 128], BF16))
            ipsb = [EC(nc.sbuf_tensor(f"ipsb{i}", [128, 2048], BF16)) for i in range(3)]
            ipdts = EC(nc.sbuf_tensor("ipdts", [128, 512], BF16))
            dtsb = EC(nc.sbuf_tensor("dtsb", [32, TE_], F32))
            spt = EC(nc.sbuf_tensor("spt", [32, TE_], F32))
            alog = EC(nc.sbuf_tensor("alog", [32, TE_], F32))
            cumtm = EC(nc.sbuf_tensor("cumtm", [128, 8 * 32], F32))
            ecums = EC(nc.sbuf_tensor("ecums", [128, 8 * 32], F32))
            cumhm = EC(nc.sbuf_tensor("cumhm", [32, 8 * 128], F32))
            d2etm = EC(nc.sbuf_tensor("d2etm", [128, 8 * 32], F32))
            alos = EC(nc.sbuf_tensor("alos", [32, 8], F32))
            atm = EC(nc.sbuf_tensor("atm", [128, 32], F32))
            dhm = EC(nc.sbuf_tensor("dhm", [32, 128], F32))
            zfull = [EC(nc.sbuf_tensor(f"zfull{i}", [128, TE_], BF16)) for i in range(2)]
            xbcs = [EC(nc.sbuf_tensor(f"xbcs{i}", [128, TE_], BF16)) for i in range(2)]
            cacc = EC(nc.sbuf_tensor("cacc", [128, TOK], F32))
            cvob = [EC(nc.sbuf_tensor(f"cvob{i}", [128, TOK], BF16)) for i in range(2)]
            xtmc = EC(nc.sbuf_tensor("xtmc", [128, 2048], BF16))
            btmc = EC(nc.sbuf_tensor("btmc", [128, 2048], BF16))
            bfmc = EC(nc.sbuf_tensor("bfmc", [128, TOK], BF16))
            cfmc = EC(nc.sbuf_tensor("cfmc", [128, TOK], BF16))
            y1sb = EC(nc.sbuf_tensor("y1sb", [128, 8 * H], BF16))
            sst = EC(nc.sbuf_tensor("sst", [64, 128], F32))
            segT = [EC(nc.sbuf_tensor(f"segT{i}", [128, 128], BF16)) for i in range(2)]
            dTs = [EC(nc.sbuf_tensor(f"dTs{i}", [128, 128], F32)) for i in range(2)]
            scT = [EC(nc.sbuf_tensor(f"scT{i}", [128, 128], BF16)) for i in range(2)]
            bsb = [EC(nc.sbuf_tensor(f"bsb{i}", [128, 64], BF16)) for i in range(2)]
            p_a = EC(nc.psum_tensor("p_a", [128, 512], F32))
            p_d = [EC(nc.psum_tensor(f"p_dx{i}", [128, 128], F32)) for i in range(2)]
            p_ef = EC(nc.psum_tensor("p_ef", [128, 128], F32))
            p_b = EC(nc.psum_tensor("p_b", [128, 512], F32))
            p_t = EC(nc.psum_tensor("p_t", [128, 1024], BF16))
            stPC = contextlib.ExitStack()
            p_c = stPC.enter_context(nc.psum_tensor("p_c", [128, 128], F32))

            cvobC = EC(nc.sbuf_tensor("cvobC", [128, TOK], BF16))

            @block.gpsimd
            def _(g):
                g.dma_start(out=ipdts[:], in_=ipdt[:, :]).then_inc(s_wl.dma(), 16)
                g.dma_start(out=selms[:], in_=selm[:, :]).then_inc(s_wl.dma(), 16)
                g.dma_start(out=identb[:], in_=idb[:, :]).then_inc(s_wl.dma(), 16)
            _barrier(block, s_wl, ["tensor"])

            for sp, (t0, tw) in enumerate(SPE):
                @block.tensor
                def _(t, sp=sp, t0=t0, tw=tw):
                    if sp > 0:
                        t.wait_ge(s_ac.h, s_ac.n)
                    for kc in range(NFC):
                        last = t.matmul(p_a[0:32, 0:tw], ipdts[:, kc * 32:(kc + 1) * 32],
                                        h1[:, kc * TE_ + t0:kc * TE_ + t0 + tw],
                                        start=(kc == 0), stop=(kc == NFC - 1))
                    last.then_inc(s_mm.inc(), 1)
                mmd = s_mm.n

                @block.scalar
                def _(sc, sp=sp, t0=t0, tw=tw, mmd=mmd):
                    sc.wait_ge(s_mm.h, mmd)
                    sc.activation(dtsb[:, t0:t0 + tw], p_a[0:32, 0:tw], AF.Copy
                                  ).then_inc(s_ac.inc(), 1)
            acdt = s_ac.n

            @block.scalar
            def _(sc):
                sc.wait_ge(s_ac.h, acdt)
                sc.activation(spt[:], dtsb[:], AF.Abs)
                sc.drain()
                sc.activation(spt[:], spt[:], AF.Exp, scale=-1.0)
                sc.drain()
                sc.activation(spt[:], spt[:], AF.Ln, bias=1.0)
                sc.drain()
                sc.activation(alog[:], dtsb[:], AF.Relu)
                sc.drain().then_inc(s_ac.inc(), 1)
            acsp = s_ac.n

            @block.vector
            def _(v):
                v.wait_ge(s_ac.h, acsp)
                v.scalar_tensor_tensor(alog[:], alog[:], -1.0, spt[:], OP.mult, OP.subtract)
                v.drain().then_inc(s_ev.inc(), 1)
            eval_ = s_ev.n

            for cc in range(8):
                c0 = 3 + cc * 128

                @block.tensor
                def _(t, cc=cc, c0=c0):
                    if cc == 0:
                        t.wait_ge(s_ev.h, eval_)
                    else:
                        t.wait_ge(s_ev.h, s_ev.n)
                    t.transpose(p_c[:, 0:32], alog[:, c0:c0 + 128], idfs[0:32, 0:32]
                                ).then_inc(s_mm.inc(), 1)
                mm1 = s_mm.n

                @block.vector
                def _(v, cc=cc, mm1=mm1):
                    v.wait_ge(s_mm.h, mm1)
                    v.tensor_copy(atm[:], p_c[:, 0:32]).then_inc(s_ev.inc(), 1)
                ev1 = s_ev.n

                @block.tensor
                def _(t, cc=cc, ev1=ev1):
                    t.wait_ge(s_ev.h, ev1)
                    t.matmul(p_ef[:, 0:32], ltis, atm[:], start=True, stop=True
                             ).then_inc(s_mm.inc(), 1)
                mm2 = s_mm.n

                @block.vector
                def _(v, cc=cc, mm2=mm2):
                    v.wait_ge(s_mm.h, mm2)
                    v.tensor_copy(cumtm[:, cc * 32:(cc + 1) * 32], p_ef[:, 0:32]
                                  ).then_inc(s_ev.inc(), 1)
                ev2 = s_ev.n

                @block.scalar
                def _(sc, cc=cc, ev2=ev2):
                    sc.wait_ge(s_ev.h, ev2)
                    sc.activation(ecums[:, cc * 32:(cc + 1) * 32],
                                  cumtm[:, cc * 32:(cc + 1) * 32], AF.Exp
                                  ).then_inc(s_ac.inc(), 1)

                @block.tensor
                def _(t, cc=cc, ev2=ev2):
                    t.wait_ge(s_ev.h, ev2)
                    t.transpose(p_c[0:32, 0:128], cumtm[:, cc * 32:(cc + 1) * 32],
                                idfs[:, 0:128]).then_inc(s_mm.inc(), 1)
                mm3 = s_mm.n

                @block.vector
                def _(v, cc=cc, mm3=mm3):
                    v.wait_ge(s_mm.h, mm3)
                    v.tensor_copy(cumhm[:, cc * 128:(cc + 1) * 128], p_c[0:32, 0:128]
                                  ).then_inc(s_ev.inc(), 1)
                ev3 = s_ev.n

                @block.scalar
                def _(sc, cc=cc, ev3=ev3):
                    sc.wait_ge(s_ev.h, ev3)
                    sc.activation(alos[:, cc:cc + 1],
                                  cumhm[:, cc * 128 + 127:cc * 128 + 128], AF.Exp
                                  ).then_inc(s_ac.inc(), 1)

                @block.vector
                def _(v, cc=cc, ev3=ev3):
                    v.wait_ge(s_ev.h, ev3)
                    v.tensor_scalar(dhm[:], cumhm[:, cc * 128:(cc + 1) * 128],
                                    cumhm[:, cc * 128 + 127:cc * 128 + 128],
                                    None, OP.subtract)
                    v.drain().then_inc(s_ev.inc(), 1)
                ev4 = s_ev.n

                @block.scalar
                def _(sc, cc=cc, ev4=ev4):
                    sc.wait_ge(s_ev.h, ev4)
                    sc.activation(dhm[:], dhm[:], AF.Exp, scale=-1.0
                                  ).then_inc(s_ac.inc(), 1)
                ac4 = s_ac.n

                @block.tensor
                def _(t, cc=cc, ac4=ac4):
                    t.wait_ge(s_ac.h, ac4)
                    t.transpose(p_c[:, 64:96], dhm[:], idfs[0:32, 0:32]
                                ).then_inc(s_mm.inc(), 1)
                mm5 = s_mm.n

                @block.vector
                def _(v, cc=cc, mm5=mm5):
                    v.wait_ge(s_mm.h, mm5)
                    v.tensor_copy(d2etm[:, cc * 32:(cc + 1) * 32], p_c[:, 64:96]
                                  ).then_inc(s_ev.inc(), 1)

            _barrier(block, s_ev, ["tensor", "vector", "scalar", "gpsimd", "sync"])
            _barrier(block, s_ac, ["tensor", "vector", "scalar", "sync"])
            stPC.close()
            p_cb = [EC(nc.psum_tensor(f"p_cb{i}", [128, 128], F32)) for i in range(2)]

            @block.sync
            def _(sy):
                sy.dma_start(out=ecumo[:, :], in_=ecums[:]).then_inc(s_st.dma(), 16)
                sy.dma_start(out=alo[:, :], in_=alos[:]).then_inc(s_st.dma(), 16)

            ip_free = {}
            ev_evac = {}
            state3 = {"ipn": 0}

            pcond3 = {}

            def ip_unit(ocw, dst_tile, dst_off, kind):
                n_ = state3["ipn"]; state3["ipn"] = n_ + 1
                bi = n_ % 3

                @block.gpsimd
                def _(g, ocw=ocw, bi=bi, n_=n_):
                    if n_ >= 3:
                        g.wait_ge(s_mm.h, ip_free[n_ - 3])
                    g.dma_start(out=ipsb[bi][:], in_=ipm[:, ocw * 2048:(ocw + 1) * 2048]
                                ).then_inc(ipl[bi].dma(), 16)
                ilv = ipl[bi].n

                for sp, (t0, tw) in enumerate(SPE):
                    par = sp % 2
                    pp = (p_a, p_b)[par]
                    pc = pcond3.get(par)

                    @block.tensor
                    def _(t, bi=bi, sp=sp, t0=t0, tw=tw, pp=pp, ilv=ilv, n_=n_, pc=pc):
                        if sp == 0:
                            t.wait_ge(ipl[bi].h, ilv)
                        if pc is not None:
                            t.wait_ge(s_ac.h, pc)
                        for kc in range(NFC):
                            last = t.matmul(pp[:, 0:tw],
                                            ipsb[bi][:, kc * 128:(kc + 1) * 128],
                                            h1[:, kc * TE_ + t0:kc * TE_ + t0 + tw],
                                            start=(kc == 0), stop=(kc == NFC - 1))
                        last.then_inc(s_mm.inc(), 1)
                    mmu = s_mm.n
                    if sp == len(SPE) - 1:
                        ip_free[n_] = mmu

                    @block.scalar
                    def _(sc, sp=sp, t0=t0, tw=tw, pp=pp, mmu=mmu,
                          dst_tile=dst_tile, dst_off=dst_off):
                        sc.wait_ge(s_mm.h, mmu)
                        sc.activation(dst_tile[:, dst_off + t0:dst_off + t0 + tw],
                                      pp[:, 0:tw], AF.Copy).then_inc(s_ac.inc(), 1)
                    ev_evac[("ip", n_, sp)] = s_ac.n
                    pcond3[par] = s_ac.n
                return s_ac.n

            z_ev = {}
            for oc in range(16):
                zslot = oc % 2

                @block.scalar
                def _(sc, oc=oc, zslot=zslot):
                    if ("z", oc - 2) in z_ev:
                        sc.wait_ge(stz[zslot].h, z_ev[("z", oc - 2)])
                ac_last = ip_unit(oc, zfull[zslot], 0, "z")

                @block.sync
                def _(sy, oc=oc, zslot=zslot, ac_last=ac_last):
                    sy.wait_ge(s_ac.h, ac_last)
                    sy.dma_start(out=zo[:, oc * TOK:(oc + 1) * TOK],
                                 in_=zfull[zslot][:, 3:TE_]).then_inc(stz[zslot].dma(), 16)
                z_ev[("z", oc)] = stz[zslot].n

            conv_seq = {"ev": 0}

            def conv_unit(src_slot, out_dst, ch, gate_ac):
                @block.vector
                def _(v, src_slot=src_slot, out_dst=out_dst, ch=ch, gate_ac=gate_ac):
                    v.wait_ge(s_ac.h, gate_ac)
                    if conv_seq["ev"]:
                        v.wait_ge(s_ev.h, conv_seq["ev"])
                    xb = xbcs[src_slot]
                    v.tensor_scalar(cacc[:], xb[:, 0:TOK], cvws[:, ch * 4:ch * 4 + 1],
                                    None, OP.mult)
                    v.drain()
                    for j in range(1, 4):
                        v.scalar_tensor_tensor(cacc[:], xb[:, j:j + TOK],
                                               cvws[:, ch * 4 + j:ch * 4 + j + 1],
                                               cacc[:], OP.mult, OP.add)
                        v.drain()
                    v.tensor_scalar(out_dst[:], cacc[:], cvbs[:, ch:ch + 1], None, OP.add)
                    v.drain().then_inc(s_ev.inc(), 1)
                conv_seq["ev"] = s_ev.n
                return s_ev.n

            ssd_state = {"tr_ev": 0, "ssd_ev": 0, "ssd_ac": 0, "ssd_mm": 0,
                         "stS": 0, "convfree": {}}
            for fc in range(NFC):
                if ("C", fc - 1) in ssd_state["convfree"]:
                    @block.scalar
                    def _(sc, g=ssd_state["convfree"][("C", fc - 1)]):
                        sc.wait_ge(s_ev.h, g)
                if fc > 0:
                    @block.vector
                    def _(v, g=ssd_state["ssd_ev"]):
                        v.wait_ge(s_ev.h, g)
                    @block.scalar
                    def _(sc, g=ssd_state["ssd_ev"]):
                        sc.wait_ge(s_ev.h, g)
                acx = ip_unit(16 + fc, xbcs[0], 0, "x")
                ev_cx = conv_unit(0, cvob[0], fc, acx)
                ssd_state["convfree"][("x", fc)] = ev_cx
                acb_ = ip_unit(32 + fc, xbcs[1], 0, "B")
                ev_cb = conv_unit(1, cvob[1], 16 + fc, acb_)
                @block.scalar
                def _(sc, g=ev_cx):
                    sc.wait_ge(s_ev.h, g)
                acc_ = ip_unit(48 + fc, xbcs[0], 0, "C")
                if "co_st" in ssd_state:
                    @block.vector
                    def _(v, g=ssd_state["co_st"]):
                        v.wait_ge(stCo.h, g)
                ev_cc = conv_unit(0, cvobC, 32 + fc, acc_)
                ssd_state["convfree"][("C", fc)] = ev_cc

                @block.sync
                def _(sy, fc=fc, g=ev_cc):
                    sy.wait_ge(s_ev.h, g)
                    sy.dma_start(out=co[:, fc * TOK:(fc + 1) * TOK], in_=cvobC[:]
                                 ).then_inc(stCo.dma(), 16)
                ssd_state["co_st"] = stCo.n

                for cc in range(8):
                    for hh in range(2):
                        @block.tensor
                        def _(t, fc=fc, cc=cc, hh=hh, g=ev_cb):
                            t.wait_ge(s_ev.h, g)
                            if ssd_state["tr_ev"]:
                                t.wait_ge(s_ev.h, ssd_state["tr_ev"])
                            idsl = identb[hh * 64:hh * 64 + 64, hh * 64:hh * 64 + 64]
                            t.transpose(p_t[:, 0:64],
                                        cvob[0][hh * 64:hh * 64 + 64, cc * 128:(cc + 1) * 128],
                                        idsl)
                            t.transpose(p_t[:, 64:128],
                                        cvob[1][hh * 64:hh * 64 + 64, cc * 128:(cc + 1) * 128],
                                        idsl).then_inc(s_mm.inc(), 1)
                        mmt = s_mm.n

                        @block.vector
                        def _(v, cc=cc, hh=hh, mmt=mmt):
                            v.wait_ge(s_mm.h, mmt)
                            off = (cc * 2 + hh) * 64
                            v.tensor_copy(xtmc[:, off:off + 64], p_t[:, 0:64])
                            v.tensor_copy(btmc[:, off:off + 64], p_t[:, 64:128]
                                          ).then_inc(s_ev.inc(), 1)
                        ssd_state["tr_ev"] = s_ev.n
                tr_all = s_ev.n

                units = [(cc, hh) for cc in range(8) for hh in range(2)]
                NU = len(units)
                st_ = {"t1": {}, "s2": {}, "s3": {}, "t4": {}, "s5": {},
                       "t6": {}, "s7": {}}
                pv = ssd_state

                def T1(i, fc=fc, tr_all=tr_all):
                    cc, hh = units[i]
                    h_ = 2 * fc + hh
                    sl_ = i % 2

                    @block.tensor
                    def _(t, i=i, cc=cc, h_=h_, sl_=sl_):
                        if i == 0:
                            t.wait_ge(s_ev.h, tr_all)
                            if "prev_s2" in pv:
                                t.wait_ge(s_ev.h, pv["prev_s2"])
                        if i - 2 in st_["s2"]:
                            t.wait_ge(s_ev.h, st_["s2"][i - 2])
                        t.matmul(p_cb[sl_][:, 0:128], selms[:, h_ * 128:(h_ + 1) * 128],
                                 cumhm[0:32, cc * 128:(cc + 1) * 128],
                                 start=True, stop=True).then_inc(s_mm.inc(), 1)
                    st_["t1"][i] = s_mm.n

                def V2(i, fc=fc):
                    cc, hh = units[i]
                    h_ = 2 * fc + hh
                    sl_ = i % 2

                    @block.vector
                    def _(v, i=i, cc=cc, h_=h_, sl_=sl_):
                        v.wait_ge(s_mm.h, st_["t1"][i])
                        if i - 2 in st_["s3"]:
                            v.wait_ge(s_ac.h, st_["s3"][i - 2])
                        elif "prev_s3" in pv and i < 2:
                            v.wait_ge(s_ac.h, pv["prev_s3"])
                        v.tensor_scalar(dTs[sl_][:], p_cb[sl_][:, 0:128],
                                        cumtm[:, cc * 32 + h_:cc * 32 + h_ + 1],
                                        None, OP.subtract)
                        v.drain()
                        v.tensor_tensor(dTs[sl_][:], dTs[sl_][:], mkts, OP.add
                                        ).then_inc(s_ev.inc(), 1)
                    st_["s2"][i] = s_ev.n

                def A3(i):
                    sl_ = i % 2

                    @block.scalar
                    def _(sc, i=i, sl_=sl_):
                        sc.wait_ge(s_ev.h, st_["s2"][i])
                        if i - 2 in st_["s5"]:
                            sc.wait_ge(s_ev.h, st_["s5"][i - 2])
                        elif "prev_s5" in pv and i < 2:
                            sc.wait_ge(s_ev.h, pv["prev_s5"])
                        sc.activation(segT[sl_][:], dTs[sl_][:], AF.Exp
                                      ).then_inc(s_ac.inc(), 1)
                    st_["s3"][i] = s_ac.n

                def T4(i):
                    cc, hh = units[i]
                    sl_ = i % 2

                    @block.tensor
                    def _(t, i=i, cc=cc, hh=hh, sl_=sl_):
                        if i - 2 in st_["s5"]:
                            t.wait_ge(s_ev.h, st_["s5"][i - 2])
                        elif "prev_s5" in pv and i < 2:
                            t.wait_ge(s_ev.h, pv["prev_s5"])
                        t.matmul(p_d[sl_][:, 0:128],
                                 cvob[1][hh * 64:hh * 64 + 64, cc * 128:(cc + 1) * 128],
                                 cvobC[hh * 64:hh * 64 + 64, cc * 128:(cc + 1) * 128],
                                 start=True, stop=True).then_inc(s_mm.inc(), 1)
                    st_["t4"][i] = s_mm.n

                def V5(i, fc=fc):
                    cc, hh = units[i]
                    h_ = 2 * fc + hh
                    sl_ = i % 2
                    off = (cc * 2 + hh) * 64

                    @block.vector
                    def _(v, i=i, cc=cc, h_=h_, sl_=sl_, off=off):
                        v.wait_ge(s_ac.h, st_["s3"][i])
                        v.wait_ge(s_mm.h, st_["t4"][i])
                        if i - 2 in st_["t6"]:
                            v.wait_ge(s_mm.h, st_["t6"][i - 2])
                        elif "prev_t6" in pv and i < 2:
                            v.wait_ge(s_mm.h, pv["prev_t6"])
                        v.tensor_tensor(scT[sl_][:], segT[sl_][:], p_d[sl_][:, 0:128], OP.mult)
                        v.tensor_scalar(bsb[sl_][:], btmc[:, off:off + 64],
                                        d2etm[:, cc * 32 + h_:cc * 32 + h_ + 1],
                                        None, OP.mult).then_inc(s_ev.inc(), 1)
                    st_["s5"][i] = s_ev.n

                def T6(i):
                    cc, hh = units[i]
                    sl_ = i % 2
                    off = (cc * 2 + hh) * 64

                    @block.tensor
                    def _(t, i=i, sl_=sl_, off=off):
                        t.wait_ge(s_ev.h, st_["s5"][i])
                        if i - 1 in st_["s7"]:
                            t.wait_ge(s_ev.h, st_["s7"][i - 1])
                        elif "prev_s7" in pv and i < 1:
                            t.wait_ge(s_ev.h, pv["prev_s7"])
                        t.matmul(p_ef[:, 0:64], scT[sl_][:], xtmc[:, off:off + 64],
                                 start=True, stop=True)
                        t.matmul(p_ef[0:64, 64:128], bsb[sl_][:], xtmc[:, off:off + 64],
                                 start=True, stop=True).then_inc(s_mm.inc(), 1)
                    st_["t6"][i] = s_mm.n

                def V7(i, fc=fc):
                    cc, hh = units[i]
                    h_ = 2 * fc + hh
                    sl_ = i % 2
                    off = (cc * 2 + hh) * 64

                    @block.vector
                    def _(v, i=i, cc=cc, hh=hh, h_=h_, off=off):
                        v.wait_ge(s_mm.h, st_["t6"][i])
                        if hh == 0 and "sst_st" in pv:
                            v.wait_ge(stS.h, pv["sst_st"])
                        v.scalar_tensor_tensor(
                            y1sb[:, cc * H + h_ * 64:cc * H + h_ * 64 + 64],
                            xtmc[:, off:off + 64], dbcs[:, h_:h_ + 1],
                            p_ef[:, 0:64], OP.mult, OP.add)
                        v.tensor_copy(sst[:, hh * 64:hh * 64 + 64], p_ef[0:64, 64:128]
                                      ).then_inc(s_ev.inc(), 1)
                    st_["s7"][i] = s_ev.n

                    if hh == 1:
                        @block.sync
                        def _(sy, fc=fc, cc=cc, g=st_["s7"][i]):
                            sy.wait_ge(s_ev.h, g)
                            base = ((fc * 8 + cc) * 2) * 64
                            sy.dma_start(out=so[:, base:base + 128], in_=sst[:]
                                         ).then_inc(stS.dma(), 16)
                        pv["sst_st"] = stS.n

                for i in range(NU + 2):
                    if i < NU:
                        T1(i); V2(i); A3(i); T4(i)
                    if i >= 2 and i - 2 < NU:
                        V7(i - 2)
                    if i >= 1 and i - 1 < NU:
                        V5(i - 1); T6(i - 1)
                pv["prev_s2"] = st_["s2"][NU - 1]
                pv["prev_s3"] = st_["s3"][NU - 1]
                pv["prev_s5"] = st_["s5"][NU - 1]
                pv["prev_t6"] = st_["t6"][NU - 1]
                pv["prev_s7"] = st_["s7"][NU - 1]
                pv["ssd_ev"] = st_["s7"][NU - 1]
                pv["ssd_mm_all"] = st_["t4"][NU - 1]

            @block.sync
            def _(sy):
                sy.wait_ge(s_ev.h, s_ev.n)
                sy.dma_start(out=y1o[:, :], in_=y1sb[:]).then_inc(s_st.dma(), 16)
            _barrier(block, s_st, ["tensor", "vector", "scalar", "gpsimd", "sync"])
            for cc_ in stz + ipl + [stS, stCo]:
                _barrier(block, cc_, ["tensor", "vector", "scalar", "gpsimd", "sync"])

        stH.close()
        with contextlib.ExitStack() as stD:
            ED = stD.enter_context
            dacc = ED(nc.sbuf_tensor("dacc", [128, NFC * TOK], F32))
            wgu0 = ED(nc.sbuf_tensor("wgu0", [128, 2 * H], BF16))
            wgu1 = ED(nc.sbuf_tensor("wgu1", [128, 2 * H], BF16))
            wgu2 = ED(nc.sbuf_tensor("wgu2", [128, 2 * H], BF16))
            wd0 = ED(nc.sbuf_tensor("wd0", [128, 4096], BF16))
            wd1 = ED(nc.sbuf_tensor("wd1", [128, 4096], BF16))
            hbuf = ED(nc.sbuf_tensor("hbuf", [128, 32 * TE_], BF16))
            gs0 = ED(nc.sbuf_tensor("gs0", [128, 512], BF16))
            gs1 = ED(nc.sbuf_tensor("gs1", [128, 512], BF16))
            us0 = ED(nc.sbuf_tensor("us0", [128, 512], BF16))
            us1 = ED(nc.sbuf_tensor("us1", [128, 512], BF16))
            gscs = ED(nc.sbuf_tensor("gscs", [128, NIC], F32))
            uscs = ED(nc.sbuf_tensor("uscs", [128, NIC], F32))
            dscs = ED(nc.sbuf_tensor("dscs", [128, 16], F32))
            p_g0 = ED(nc.psum_tensor("p_g0", [128, 512], F32))
            p_g1 = ED(nc.psum_tensor("p_g1", [128, 512], F32))
            p_u0 = ED(nc.psum_tensor("p_u0", [128, 512], F32))
            p_u1 = ED(nc.psum_tensor("p_u1", [128, 512], F32))
            p_d0 = ED(nc.psum_tensor("p_d0", [128, 512], F32))
            p_d1 = ED(nc.psum_tensor("p_d1", [128, 512], F32))

            @block.sync
            def _(sy):
                sy.dma_start(out=gscs[:], in_=gsc[:, :]).then_inc(s_ld.dma(), 16)
                sy.dma_start(out=uscs[:], in_=usc[:, :]).then_inc(s_ld.dma(), 16)
                sy.dma_start(out=dscs[:], in_=dsc[:, :]).then_inc(s_ld.dma(), 16)
            _barrier(block, s_ld, ["scalar", "vector", "tensor", "gpsimd"])

            bufs = dict(wgu=[wgu0, wgu1, wgu2], wd=[wd0, wd1], hbuf=hbuf,
                        gs=[gs0, gs1], us=[us0, us1],
                        p_g=[p_g0, p_g1], p_u=[p_u0, p_u1], p_d=[p_d0, p_d1],
                        wl2=wl2, wld=wld)
            emit_ffn(nc, block, sems, xn1, TE_, SPO,
                     dict(guw=guw, dw=dw), True,
                     (gscs, uscs, dscs), dacc, True, bufs,
                     dacc_T=TOK, dacc_off=3, hoist=True)
            _barrier(block, ph, ["tensor", "vector", "scalar", "sync", "gpsimd"])

            @block.sync
            def _(sy):
                sy.dma_start(out=qacc[:, :], in_=dacc[:]).then_inc(s_st.dma(), 16)
                sy.wait_ge(s_st.h, s_st.n)

    _PROG_CACHE["p3"] = nc
    return nc

def p3_in_maps(inp, p2res):
    cos_dummy = None
    ipw = np.asarray(inp["L1_attn_in_proj_w"])
    selm = np.zeros((32, 4096), np.float32)
    for h in range(32):
        selm[h, h * 128:(h + 1) * 128] = 1.0
    lt = np.tril(np.ones((128, 128), np.float32)).T
    mkt = np.where(np.arange(128)[None, :] >= np.arange(128)[:, None],
                   0.0, -1e30).astype(np.float32)
    convw = np.asarray(inp["L1_attn_conv_w"])
    cw_img = np.ascontiguousarray(convw.reshape(48, 128, 4).transpose(1, 0, 2).reshape(128, 192))
    cb_img = _pack_rows(np.asarray(inp["L1_attn_conv_b"]))
    shared = {
        "n1w": _pack_rows(np.asarray(inp["L1_attn_norm1_w"])),
        "ipm": _pack_lhsT_icmajor(ipw[0:8192]),
        "ipdt": _pack_lhsT(ipw[8192:8224]),
        "cvw": cw_img, "cvb": cb_img,
        "lti": lt, "mkt": mkt,
        "idf": np.eye(128, dtype=np.float32),
        "idb": np.eye(128, dtype=np.float32).astype(BFNP),
        "dbc": np.ascontiguousarray(
            np.broadcast_to(np.asarray(inp["L1_attn_D"])[None, :], (128, 32))),
        "selm": selm,
        "guw": _pack_guw(np.asarray(inp["L1_ffn_gate_w"]), np.asarray(inp["L1_ffn_up_w"])),
        "dw": np.ascontiguousarray(
            np.asarray(inp["L1_ffn_down_w"]).reshape(16, 128, 2, 32, 128)
            .transpose(4, 2, 0, 3, 1).reshape(128, 131072)),
        "gsc": _pack_rows(np.asarray(inp["L1_ffn_gate_scale"])),
        "usc": _pack_rows(np.asarray(inp["L1_ffn_up_scale"])),
        "dsc": _pack_rows(np.asarray(inp["L1_ffn_down_scale"])),
    }
    maps = []
    for c in range(NCORE):
        m = dict(shared)
        xe = np.zeros((128, NFC * TE), dtype=BFNP)
        own = p2res[c]["xn1o"]
        for fc in range(NFC):
            xe[:, fc * TE + 3:(fc + 1) * TE] = own[:, fc * TOK:(fc + 1) * TOK]
            if c > 0:
                xe[:, fc * TE:fc * TE + 3] = \
                    p2res[c - 1]["xn1o"][:, fc * TOK + TOK - 3:fc * TOK + TOK]
        m["xn1e"] = xe
        maps.append(m)
    return maps

def run_p3(inp, p2res):
    nc = build_p3()
    maps = p3_in_maps(inp, p2res)
    res = run_bass_kernel_spmd(nc, maps, core_ids=list(range(NCORE)))
    return res.results


def build_p4(core_base_list=None):
    if "p4" in _PROG_CACHE:
        return _PROG_CACHE["p4"]
    nc = bass.Bass("TRN2", target_bir_lowering=False)
    y1i = nc.dram_tensor("y1i", [128, 8 * H], BF16, kind="ExternalInput").ap()
    ci = nc.dram_tensor("ci", [128, NFC * TOK], BF16, kind="ExternalInput").ap()
    zi = nc.dram_tensor("zi", [128, NFC * TOK], BF16, kind="ExternalInput").ap()
    ecumi = nc.dram_tensor("ecumi", [128, 8 * 32], F32, kind="ExternalInput").ap()
    xn1i = nc.dram_tensor("xn1i", [128, NFC * TOK], BF16, kind="ExternalInput").ap()
    x1i = nc.dram_tensor("x1i", [128, NFC * TOK], F32, kind="ExternalInput").ap()
    qacci = nc.dram_tensor("qacci", [128, NFC * TOK], F32, kind="ExternalInput").ap()
    srep = nc.dram_tensor("srep", [8192, 2048], F32, kind="ExternalInput").ap()
    alexp = nc.dram_tensor("alexp", [64, 2048], F32, kind="ExternalInput").ap()
    ownsel = nc.dram_tensor("ownsel", [128, 64], F32, kind="ExternalInput").ap()
    opw = nc.dram_tensor("opw", [128, 16 * H], F32, kind="ExternalInput").ap()
    n2w_d = nc.dram_tensor("n2w", [128, 16], F32, kind="ExternalInput").ap()
    zb_d = nc.dram_tensor("zb", [128, 16], F32, kind="ExternalInput").ap()
    idb = nc.dram_tensor("idb", [128, 128], BF16, kind="ExternalInput").ap()
    guw = nc.dram_tensor("guw", [128, NIC * 2 * H], F32, kind="ExternalInput").ap()
    dw = nc.dram_tensor("dw", [128, 32 * 4096], F32, kind="ExternalInput").ap()
    outo = nc.dram_tensor("outo", [128, NFC * TOK], F32, kind="ExternalOutput").ap()

    with contextlib.ExitStack() as stack:
        E = stack.enter_context
        dacc = E(nc.sbuf_tensor("dacc", [128, NFC * TOK], F32))
        onesb = E(nc.sbuf_tensor("onesb", [128, 1], BF16))
        onesf = E(nc.sbuf_tensor("onesf", [1, 128], F32))
        eps_col = E(nc.sbuf_tensor("eps_col", [128, 1], F32))
        n2ws = E(nc.sbuf_tensor("n2ws", [128, 16], F32))
        zbs = E(nc.sbuf_tensor("zbs", [128, 16], F32))


        s_ld = Ctr(stack, nc, "s_ld")
        s_wl = Ctr(stack, nc, "s_wl")
        s_mm = Ctr(stack, nc, "s_mm")
        s_ev = Ctr(stack, nc, "s_ev")
        s_ac = Ctr(stack, nc, "s_ac")
        s_st = Ctr(stack, nc, "s_st")
        s_fr = Ctr(stack, nc, "s_fr")
        ph = Ctr(stack, nc, "ph")
        sl2 = [Ctr(stack, nc, f"sl2_{i}") for i in range(2)]
        al2 = [Ctr(stack, nc, f"al2_{i}") for i in range(2)]
        zl2 = [Ctr(stack, nc, f"zl2_{i}") for i in range(2)]
        wl2 = [Ctr(stack, nc, f"wl2_{i}") for i in range(3)]
        wld = [Ctr(stack, nc, f"wld_{i}") for i in range(2)]
        xl2 = [Ctr(stack, nc, f"xl2_{i}") for i in range(2)]
        st5 = [Ctr(stack, nc, f"st5_{i}") for i in range(2)]
        sems = dict(ld=s_ld, wl=s_wl, mm=s_mm, ev=s_ev, ac=s_ac, st=s_st,
                    fr=s_fr, ph=ph)
        block = E(nc.Block())

        @block.sync
        def _(sy):
            sy.dma_start(out=n2ws[:], in_=n2w_d[:, :]).then_inc(s_ld.dma(), 16)
            sy.dma_start(out=zbs[:], in_=zb_d[:, :]).then_inc(s_ld.dma(), 16)
            sy.dma_start(out=dacc[:], in_=qacci[:, :]).then_inc(s_ld.dma(), 16)

        @block.gpsimd
        def _(g):
            g.memset(onesb[:], 1.0)
            g.memset(onesf[:], 1.0)
            g.memset(eps_col[:], EPS).then_inc(s_fr.inc(), 1)
        _barrier(block, s_ld, ["vector", "scalar", "tensor", "gpsimd"])
        _barrier(block, s_fr, ["vector", "scalar", "tensor"])

        stG = contextlib.ExitStack()
        gated = stG.enter_context(nc.sbuf_tensor("gated", [128, NFC * TOK], BF16))

        with contextlib.ExitStack() as st1:
            E1 = st1.enter_context
            y1sb = E1(nc.sbuf_tensor("y1sb", [128, 8 * H], BF16))
            identb = E1(nc.sbuf_tensor("identb", [128, 128], BF16))
            owns = E1(nc.sbuf_tensor("owns", [128, 64], F32))
            cisb = gated
            ecums = E1(nc.sbuf_tensor("ecums", [128, 8 * 32], F32))
            Hhm = E1(nc.sbuf_tensor("Hhm", [128, 2048], F32))
            Ssb = [E1(nc.sbuf_tensor(f"Ssb{i}", [128, 2048], F32)) for i in range(2)]
            Hsnap = E1(nc.sbuf_tensor("Hsnap", [128, 8 * 2048], BF16))
            arow_t = E1(nc.sbuf_tensor("arow_t", [1, 2048], F32))
            arow = [arow_t, arow_t]
            ztile_t = E1(nc.sbuf_tensor("ztile_t", [128, TOK], BF16))
            ztile = [ztile_t, ztile_t]
            p_ab = E1(nc.psum_tensor("p_ab", [128, 2048], F32))
            p_yi0 = E1(nc.psum_tensor("p_yi0", [128, 64], F32))
            p_yi1 = E1(nc.psum_tensor("p_yi1", [128, 64], F32))
            p_tr = E1(nc.psum_tensor("p_tr", [128, 128], BF16))

            ld_prev = s_ld.n

            @block.sync
            def _(sy, ld_prev=ld_prev):
                sy.wait_ge(s_ld.h, ld_prev)
                sy.dma_start(out=y1sb[:], in_=y1i[:, :]).then_inc(s_ld.dma(), 16)
                sy.dma_start(out=identb[:], in_=idb[:, :]).then_inc(s_ld.dma(), 16)
                sy.dma_start(out=owns[:], in_=ownsel[:, :]).then_inc(s_ld.dma(), 16)
                sy.dma_start(out=cisb[:], in_=ci[:, :]).then_inc(s_ld.dma(), 16)
                sy.dma_start(out=ecums[:], in_=ecumi[:, :]).then_inc(s_ld.dma(), 16)

            @block.vector
            def _(v):
                v.memset(Hhm[:], 0.0)
                v.drain()
                v.memset(Hsnap[:], 0.0)
                v.drain().then_inc(s_ev.inc(), 1)
            _barrier(block, s_ld, ["vector", "scalar", "tensor", "gpsimd"])

            chain_ev = {"h": s_ev.n}
            for gc in range(GCH):
                bi = gc % 2
                cc = gc % 8

                @block.sync
                def _(sy, gc=gc, bi=bi):
                    if gc >= 2:
                        sy.wait_ge(s_ev.h, chain_ev[("used", gc - 2)])
                    sy.dma_start(out=Ssb[bi][:], in_=srep[gc * 128:(gc + 1) * 128, :]
                                 ).then_inc(sl2[bi].dma(), 16)
                    if gc >= 1:
                        sy.wait_ge(s_mm.h, chain_ev[("ab", gc - 1)])
                    sy.dma_start(out=arow_t[:], in_=alexp[gc:gc + 1, :]
                                 ).then_inc(al2[bi].dma(), 16)
                slv, alv = sl2[bi].n, al2[bi].n

                @block.tensor
                def _(t, bi=bi, alv=alv, gc=gc):
                    t.wait_ge(al2[bi].h, alv)
                    t.wait_ge(s_ev.h, chain_ev["h"])
                    for q4 in range(4):
                        last = t.matmul(p_ab[:, q4 * 512:(q4 + 1) * 512],
                                        onesf[0:1, 0:128],
                                        arow[bi][0:1, q4 * 512:(q4 + 1) * 512],
                                        start=True, stop=True)
                    last.then_inc(s_mm.inc(), 1)
                mm_ab = s_mm.n
                chain_ev[("ab", gc)] = mm_ab

                @block.vector
                def _(v, gc=gc, cc=cc, bi=bi, slv=slv, mm_ab=mm_ab):
                    v.wait_ge(s_mm.h, mm_ab)
                    v.wait_ge(sl2[bi].h, slv)
                    v.scalar_tensor_tensor(Hsnap[:, cc * 2048:(cc + 1) * 2048],
                                           Hhm[:], owns[:, gc:gc + 1],
                                           Hsnap[:, cc * 2048:(cc + 1) * 2048],
                                           OP.mult, OP.add)
                    v.drain()
                    v.tensor_tensor(Hhm[:], Hhm[:], p_ab[:], OP.mult)
                    v.drain()
                    v.tensor_tensor(Hhm[:], Hhm[:], Ssb[bi][:], OP.add)
                    v.drain().then_inc(s_ev.inc(), 1)
                chain_ev["h"] = s_ev.n
                chain_ev[("used", gc)] = s_ev.n


            yiv = {"pc": {}, "hb": 0}
            ev_hb = chain_ev["h"]
            for cc in range(8):
                mm_list = []
                for h_ in range(NHM):
                    fc, hh = h_ // 2, h_ % 2
                    pj = h_ % 2
                    gidx = cc * 32 + h_

                    @block.tensor
                    def _(t, h_=h_, fc=fc, hh=hh, cc=cc, pj=pj, ev_hb=ev_hb, gidx=gidx):
                        if h_ == 0:
                            t.wait_ge(s_ev.h, ev_hb)
                        if (pj, gidx - 2) in yiv["pc"]:
                            t.wait_ge(s_ev.h, yiv["pc"][(pj, gidx - 2)])
                        pyx = (p_yi0, p_yi1)[pj]
                        t.matmul(pyx[:, 0:64],
                                 cisb[hh * 64:hh * 64 + 64, fc * TOK + cc * 128:fc * TOK + (cc + 1) * 128],
                                 Hsnap[hh * 64:hh * 64 + 64, cc * 2048 + h_ * 64:cc * 2048 + (h_ + 1) * 64],
                                 start=True, stop=True).then_inc(s_mm.inc(), 1)
                    mm_yi = s_mm.n
                    mm_list.append(mm_yi)

                    @block.vector
                    def _(v, gc_cc=cc, h_=h_, cc=cc, pj=pj, mm_yi=mm_yi):
                        v.wait_ge(s_mm.h, mm_yi)
                        sl_ = slice(cc * H + h_ * 64, cc * H + (h_ + 1) * 64)
                        pyx = (p_yi0, p_yi1)[pj]
                        v.scalar_tensor_tensor(y1sb[:, sl_], pyx[:, 0:64],
                                               ecums[:, cc * 32 + h_:cc * 32 + h_ + 1],
                                               y1sb[:, sl_], OP.mult, OP.add
                                               ).then_inc(s_ev.inc(), 1)
                    yiv["pc"][(pj, gidx)] = s_ev.n
                pass

            _barrier(block, s_ev, ["tensor", "scalar", "gpsimd", "sync"])

            zld = {}
            for fc in range(NFC):
                bi = fc % 2

                @block.sync
                def _(sy, fc=fc, bi=bi):
                    if fc >= 1:
                        sy.wait_ge(s_ac.h, zld[fc - 1])
                    sy.dma_start(out=ztile[bi][:], in_=zi[:, fc * TOK:(fc + 1) * TOK]
                                 ).then_inc(zl2[bi].dma(), 16)
                zlv = zl2[bi].n

                @block.scalar
                def _(sc, fc=fc, bi=bi, zlv=zlv):
                    sc.wait_ge(zl2[bi].h, zlv)
                    sc.activation(gated[:, fc * TOK:(fc + 1) * TOK], ztile[bi][:],
                                  AF.Silu, bias=zbs[:, fc:fc + 1]).then_inc(s_ac.inc(), 1)
                zld[fc] = s_ac.n
            ac_sil = s_ac.n

            tr_ev = {"n": 0}
            for cc in range(8):
                for h_ in range(NHM):
                    fc, hh = h_ // 2, h_ % 2

                    @block.tensor
                    def _(t, cc=cc, h_=h_, ac_sil=ac_sil):
                        if cc == 0 and h_ == 0:
                            t.wait_ge(s_ac.h, ac_sil)
                        if tr_ev["n"]:
                            t.wait_ge(s_ev.h, tr_ev["n"])
                        t.transpose(p_tr[:, 0:64].rearrange("p f -> p f") if False else p_tr[0:64, 0:128],
                                    y1sb[:, cc * H + h_ * 64:cc * H + (h_ + 1) * 64],
                                    identb[0:128, 0:128]).then_inc(s_mm.inc(), 1)
                    mm_tr = s_mm.n

                    @block.vector
                    def _(v, cc=cc, h_=h_, fc=fc, hh=hh, mm_tr=mm_tr):
                        v.wait_ge(s_mm.h, mm_tr)
                        sl_ = slice(fc * TOK + cc * 128, fc * TOK + (cc + 1) * 128)
                        dst = gated[hh * 64:hh * 64 + 64, sl_]
                        v.tensor_tensor(dst, dst, p_tr[0:64, 0:128], OP.mult
                                        ).then_inc(s_ev.inc(), 1)
                    tr_ev["n"] = s_ev.n

            _barrier(block, s_ev, ["tensor", "scalar", "gpsimd", "sync"])
            _barrier(block, s_ac, ["tensor", "vector", "gpsimd", "sync"])

        with contextlib.ExitStack() as st2:
            E2 = st2.enter_context
            h2b = E2(nc.sbuf_tensor("h2b", [128, NFC * TOK], BF16))
            sqb2 = E2(nc.sbuf_tensor("sqb2", [128, NFC * TOK], BF16))
            opsb = [E2(nc.sbuf_tensor(f"opsb{i}", [128, 2048], BF16)) for i in range(2)]
            xtl = [E2(nc.sbuf_tensor(f"xtl{i}", [128, 512], BF16)) for i in range(2)]
            rows = E2(nc.sbuf_tensor("rows", [1, 3 * TOK], F32))
            tf0 = E2(nc.sbuf_tensor("tf0", [128, 512], F32))
            tf1 = E2(nc.sbuf_tensor("tf1", [128, 512], F32))
            p_a = E2(nc.psum_tensor("p_a", [128, 512], F32))
            p_b = E2(nc.psum_tensor("p_b", [128, 512], F32))
            p_s = E2(nc.psum_tensor("p_s", [1, TOK], F32))
            p_q = E2(nc.psum_tensor("p_q", [1, TOK], F32))

            op_free = {}
            pcond = {}
            for oc in range(16):
                bi = oc % 2

                @block.gpsimd
                def _(g, oc=oc, bi=bi):
                    if oc >= 2:
                        g.wait_ge(s_mm.h, op_free[oc - 2])
                    g.dma_start(out=opsb[bi][:], in_=opw[:, oc * 2048:(oc + 1) * 2048]
                                ).then_inc(wl2[bi].dma(), 16)
                wlv = wl2[bi].n

                for sp, (t0, tw) in enumerate(SP2):
                    u = oc * 2 + sp
                    j = u % 2
                    pp = (p_a, p_b)[j]

                    @block.sync
                    def _(sy, oc=oc, t0=t0, j=j, u=u):
                        if u - 2 in pcond:
                            sy.wait_ge(s_ev.h, pcond[u - 2])
                        sy.dma_start(out=xtl[j][:],
                                     in_=xn1i[:, oc * TOK + t0:oc * TOK + t0 + 512]
                                     ).then_inc(xl2[j].dma(), 16)
                    xlv = xl2[j].n

                    @block.tensor
                    def _(t, oc=oc, bi=bi, sp=sp, t0=t0, pp=pp, u=u, wlv=wlv):
                        if sp == 0:
                            t.wait_ge(wl2[bi].h, wlv)
                        if u - 2 in pcond:
                            t.wait_ge(s_ev.h, pcond[u - 2])
                        for kc in range(NFC):
                            last = t.matmul(pp[:, 0:512],
                                            opsb[bi][:, kc * 128:(kc + 1) * 128],
                                            gated[:, kc * TOK + t0:kc * TOK + t0 + 512],
                                            start=(kc == 0), stop=(kc == NFC - 1))
                        last.then_inc(s_mm.inc(), 1)
                    mmo = s_mm.n
                    if sp == 1:
                        op_free[oc] = mmo

                    @block.vector
                    def _(v, oc=oc, t0=t0, j=j, pp=pp, mmo=mmo, xlv=xlv):
                        v.wait_ge(s_mm.h, mmo)
                        v.wait_ge(xl2[j].h, xlv)
                        sl_ = slice(oc * TOK + t0, oc * TOK + t0 + 512)
                        v.tensor_tensor(h2b[:, sl_], pp[:, 0:512], xtl[j][:], OP.add)
                        v.drain()
                        v.tensor_tensor(dacc[:, sl_], dacc[:, sl_], h2b[:, sl_], OP.add
                                        ).then_inc(s_ev.inc(), 1)
                    pcond[u] = s_ev.n
            _barrier(block, s_ev, ["tensor", "scalar", "gpsimd", "sync"])

            tmps = dict(sqb=sqb2, rows=rows, tf=[tf0, tf1],
                        onesb=onesb, onesf=onesf, eps_col=eps_col)
            emit_rms(nc, block, sems, h2b, gated, TOK, SP2, tmps,
                     (p_s, p_q, p_a, p_b), n2ws)
            _barrier(block, ph, ["tensor", "vector", "scalar", "gpsimd", "sync"])

        with contextlib.ExitStack() as stF:
            EF = stF.enter_context
            wgu0 = EF(nc.sbuf_tensor("wgu0", [128, 2 * H], BF16))
            wgu1 = EF(nc.sbuf_tensor("wgu1", [128, 2 * H], BF16))
            wgu2 = EF(nc.sbuf_tensor("wgu2", [128, 2 * H], BF16))
            wd0 = EF(nc.sbuf_tensor("wd0", [128, 4096], BF16))
            wd1 = EF(nc.sbuf_tensor("wd1", [128, 4096], BF16))
            hbuf = EF(nc.sbuf_tensor("hbuf", [128, 32 * TOK], BF16))
            gs0 = EF(nc.sbuf_tensor("gs0", [128, 512], BF16))
            gs1 = EF(nc.sbuf_tensor("gs1", [128, 512], BF16))
            us0 = EF(nc.sbuf_tensor("us0", [128, 512], BF16))
            us1 = EF(nc.sbuf_tensor("us1", [128, 512], BF16))
            p_g0 = EF(nc.psum_tensor("p_g0", [128, 512], F32))
            p_g1 = EF(nc.psum_tensor("p_g1", [128, 512], F32))
            p_u0 = EF(nc.psum_tensor("p_u0", [128, 512], F32))
            p_u1 = EF(nc.psum_tensor("p_u1", [128, 512], F32))
            p_d0 = EF(nc.psum_tensor("p_d0", [128, 512], F32))
            p_d1 = EF(nc.psum_tensor("p_d1", [128, 512], F32))

            bufs = dict(wgu=[wgu0, wgu1, wgu2], wd=[wd0, wd1], hbuf=hbuf,
                        gs=[gs0, gs1], us=[us0, us1],
                        p_g=[p_g0, p_g1], p_u=[p_u0, p_u1], p_d=[p_d0, p_d1],
                        wl2=wl2, wld=wld)
            emit_ffn(nc, block, sems, gated, TOK, SP2,
                     dict(guw=guw, dw=dw), False,
                     None, dacc, False, bufs, hoist=True)
            _barrier(block, ph, ["tensor", "vector", "scalar", "sync", "gpsimd"])

            with contextlib.ExitStack() as st3:
                E3 = st3.enter_context
                x1r = E3(nc.sbuf_tensor("x1r", [128, 512], F32))
                ev5_done = {}
                st5_done = {}
                for fc in range(NFC):
                    for sp, (t0, tw) in enumerate(SP2):
                        idx = fc * 2 + sp
                        j = 0

                        @block.sync
                        def _(sy, fc=fc, t0=t0, j=j, idx=idx):
                            if idx - 1 in st5_done:
                                sy.wait_ge(st5[0].h, st5_done[idx - 1])
                            sy.dma_start(out=x1r[:, 0:512],
                                         in_=x1i[:, fc * TOK + t0:fc * TOK + t0 + 512]
                                         ).then_inc(xl2[j].dma(), 16)
                        ldv = xl2[j].n

                        @block.vector
                        def _(v, fc=fc, t0=t0, j=j, ldv=ldv, idx=idx):
                            v.wait_ge(xl2[j].h, ldv)
                            v.tensor_tensor(x1r[:, 0:512],
                                            x1r[:, 0:512],
                                            dacc[:, fc * TOK + t0:fc * TOK + t0 + 512],
                                            OP.add).then_inc(s_ev.inc(), 1)
                        ev5_done[idx] = s_ev.n
                        ev5 = s_ev.n

                        @block.sync
                        def _(sy, fc=fc, t0=t0, j=j, ev5=ev5):
                            sy.wait_ge(s_ev.h, ev5)
                            sy.dma_start(out=outo[:, fc * TOK + t0:fc * TOK + t0 + 512],
                                         in_=x1r[:, 0:512]
                                         ).then_inc(st5[j].dma(), 16)
                        st5_done[idx] = st5[j].n

                @block.sync
                def _(sy):
                    for cc_ in st5:
                        sy.wait_ge(cc_.h, cc_.n)
        stG.close()

    _PROG_CACHE["p4"] = nc
    return nc


def p4_in_maps(inp, p2res, p3res):
    srep = np.zeros((8192, 2048), np.float32)
    alexp = np.zeros((64, 2048), np.float32)
    for c in range(NCORE):
        so_ = p3res[c]["so"]
        al_ = p3res[c]["alo"]
        for cc in range(8):
            gc = c * 8 + cc
            for h_ in range(NHM):
                fc, hh = h_ // 2, h_ % 2
                col = ((fc * 8 + cc) * 2 + hh) * 64
                srep[gc * 128:gc * 128 + 64, h_ * 64:(h_ + 1) * 64] = so_[:, col:col + 64]
                srep[gc * 128 + 64:(gc + 1) * 128, h_ * 64:(h_ + 1) * 64] = so_[:, col:col + 64]
            alexp[gc] = np.repeat(al_[:, cc], 64)
    shared = {
        "srep": srep, "alexp": alexp,
        "opw": _pack_lhsT_icmajor(np.asarray(inp["L1_attn_out_proj_w"])),
        "n2w": _pack_rows(np.asarray(inp["L1_attn_norm2_w"])),
        "zb": _pack_rows(np.asarray(inp["L1_attn_z_bias"])),
        "idb": np.eye(128, dtype=np.float32).astype(BFNP),
        "guw": _pack_guw(np.asarray(inp["L1_attn_gate_w"]), np.asarray(inp["L1_attn_up_w"])),
        "dw": np.ascontiguousarray(
            np.asarray(inp["L1_attn_down_w"]).reshape(16, 128, 2, 32, 128)
            .transpose(4, 2, 0, 3, 1).reshape(128, 131072)),
    }
    maps = []
    for c in range(NCORE):
        m = dict(shared)
        own = np.zeros((128, 64), np.float32)
        own[:, c * 8:(c + 1) * 8] = 1.0
        m["ownsel"] = own
        m["y1i"] = p3res[c]["y1o"]
        m["ci"] = p3res[c]["co"]
        m["zi"] = p3res[c]["zo"]
        m["ecumi"] = p3res[c]["ecumo"]
        m["xn1i"] = p2res[c]["xn1o"]
        m["x1i"] = p2res[c]["x1o"]
        m["qacci"] = p3res[c]["qacc"]
        maps.append(m)
    return maps

def run_p4(inp, p2res, p3res):
    nc = build_p4()
    maps = p4_in_maps(inp, p2res, p3res)
    res = run_bass_kernel_spmd(nc, maps, core_ids=list(range(NCORE)))
    return res.results


def kernel(**inputs):
    inp = {k: np.asarray(v) for k, v in inputs.items()}
    p1 = run_p1(inp)
    p2 = run_p2(inp, p1)
    p3 = run_p3(inp, p2)
    p4 = run_p4(inp, p2, p3)
    out = np.zeros((1, S, H), np.float32)
    for c in range(NCORE):
        out[0, c * TOK:(c + 1) * TOK, :] = _unfm(p4[c]["outo"], TOK)
    return out
